# version 13
# baseline (speedup 1.0000x reference)
import numpy as np
from contextlib import ExitStack
import concourse.bass as bass
import concourse.mybir as mybir
from concourse.bass_utils import run_bass_kernel_spmd

F32 = mybir.dt.float32
BF16 = mybir.dt.bfloat16
I32 = mybir.dt.int32
AF = mybir.ActivationFunctionType
ALU = mybir.AluOpType
AX = mybir.AxisListType

T = 2048
D = 1024
H = 16
INNER = 4096
NTT = T // 128
NDC = D // 128
NIC = INNER // 128
MAGIC = 12582912.0
EPS_LN = 1e-5
EPS_QN = 1e-6
NELEM_W = D * INNER

SIGN_TABLE = {
    0: [(0, 1.0), (1, -1.0), (2, -1.0), (3, -1.0)],
    1: [(1, 1.0), (0, 1.0), (3, -1.0), (2, 1.0)],
    2: [(2, 1.0), (3, 1.0), (0, 1.0), (1, -1.0)],
    3: [(3, 1.0), (2, -1.0), (1, 1.0), (0, 1.0)],
}


class Ctr:
    def __init__(self, stack, nc, name, ctx=None):
        self.h = stack.enter_context(nc.semaphore(name))
        self.n = 0
        self.ctx = ctx

    def inc(self, inst, k=1):
        inst.then_inc(self.h, k)
        self.n += k
        if self.ctx is not None:
            self.ctx._note_inc(inst, self)
        return self.n


_SERIAL_OPS = frozenset((
    "tensor_copy", "tensor_tensor", "tensor_add", "tensor_sub", "tensor_mul",
    "tensor_max", "tensor_scalar", "tensor_single_scalar", "tensor_scalar_add",
    "tensor_scalar_sub", "tensor_scalar_mul", "tensor_scalar_max",
    "tensor_scalar_min", "tensor_relu", "tensor_reduce", "reduce_max",
    "reduce_sum", "scalar_tensor_tensor", "tensor_tensor_scan", "reciprocal",
    "memset", "memzero", "activation", "sign", "square", "sqrt", "mul",
    "copy", "add", "select", "copy_predicated", "transpose", "iota",
    "partition_broadcast", "partition_all_reduce", "engine_nop", "pool",
    "pool_avg", "pool_max", "bn_stats", "bn_aggr", "max", "stream_shuffle",
))


class SerialEng:
    def __init__(self, ctr):
        self._eng = None
        self._ctr = ctr
        self._pend = None
        self._sync = None

    def _rebind(self, eng):
        self._eng = eng
        return self

    def _user_inc(self, inst, ctr):
        if self._pend is inst:
            self._pend = None
            self._sync = (ctr.h, ctr.n)

    def _fence(self):
        if self._pend is not None:
            v = self._ctr.inc(self._pend)
            self._sync = (self._ctr.h, v)
            self._pend = None
        if self._sync is not None:
            self._eng.wait_ge(*self._sync)
            self._sync = None

    def __getattr__(self, name):
        f = getattr(self._eng, name)
        if name not in _SERIAL_OPS:
            return f

        def g(*a, **k):
            self._fence()
            inst = f(*a, **k)
            self._pend = inst
            return inst
        return g


class Ctx:
    def __init__(self, nc, block, stack):
        self.nc = nc
        self.block = block
        self.stack = stack
        for c in ("pe", "dv", "sc", "gp", "ld", "lw", "st",
                  "hdr", "fb", "qw", "lx0", "lx1",
                  "rs", "rl0", "rl1", "la0", "la1",
                  "sx0", "sx1", "sy",
                  "wa", "wv", "wp", "lg", "sg", "w0", "w1", "wr",
                  "dvs", "scs", "gps"):
            setattr(self, c, Ctr(stack, nc, "s_" + c, ctx=self))
        self._proxies = {"vector": SerialEng(self.dvs),
                         "scalar": SerialEng(self.scs),
                         "gpsimd": SerialEng(self.gps)}

    def _note_inc(self, inst, ctr):
        for p in self._proxies.values():
            p._user_inc(inst, ctr)

    def _note_sync(self, inst, h, v):
        for p in self._proxies.values():
            if p._pend is inst:
                p._pend = None
                p._sync = (h, v)

    def sec(self, eng, fn):
        if eng in self._proxies:
            getattr(self.block, eng)(lambda e: fn(self._proxies[eng]._rebind(e)))
        else:
            getattr(self.block, eng)(fn)

    def marks(self):
        return {c: getattr(self, c).n
                for c in ("pe", "dv", "sc", "gp", "ld", "lw", "st", "hdr", "fb",
                          "qw", "lx0", "lx1", "rs", "rl0", "rl1", "la0", "la1",
                          "sx0", "sx1", "sy", "wa", "wv", "wp", "lg", "sg",
                          "w0", "w1", "wr")}

    def wait_marks(self, eng, m, which=("pe", "dv", "sc", "gp")):
        for w in which:
            if m[w] > 0:
                eng.wait_ge(getattr(self, w).h, m[w])

    def barrier(self, engines=("sync", "vector", "scalar", "tensor", "gpsimd"),
                on=("pe", "dv", "sc", "gp", "ld", "lw", "hdr", "fb", "qw",
                    "lx0", "lx1", "rs", "wa", "wv", "wp")):
        m = self.marks()
        for e in engines:
            self.sec(e, lambda eng, m=m: self.wait_marks(eng, m, on))


def sb(ctx, name, shape, dt, stack=None):
    return (stack or ctx.stack).enter_context(ctx.nc.sbuf_tensor(name, shape, dt))


def psum(ctx, name, shape, dt=F32, stack=None):
    return (stack or ctx.stack).enter_context(ctx.nc.psum_tensor(name, shape, dt))


def _tap(ctx, dst, src_ap):
    def f(sy):
        m = ctx.marks()
        ctx.wait_marks(sy, m, ("pe", "dv", "sc", "gp", "ld", "lw", "hdr",
                               "fb", "qw", "lx0", "lx1", "rs", "rl0", "rl1",
                               "la0", "la1", "sx0", "sx1", "sy",
                               "wa", "wv", "wp"))
        ctx.st.inc(sy.dma_start(out=dst, in_=src_ap), 16)
    ctx.sec("sync", f)


def build(n_heads=H, taps=(), phase_stop=None):
    nc = bass.Bass("TRN2", target_bir_lowering=False, detect_race_conditions=False)
    taps = set(taps)
    E = {}

    for nm, shape in (("x", [T, D]), ("qW", [4, 256, 256]), ("kW", [4, 256, 256]),
                      ("vW", [4, 256, 256]), ("oW", [4, 256, 256]),
                      ("qb", [4, 256]), ("kb", [4, 256]), ("vb", [4, 256]),
                      ("ob", [4, 256]), ("J", [D]), ("wgT", [D, INNER]),
                      ("wuT", [D, INNER]), ("wdT", [INNER, D])):
        E[nm + "_d"] = nc.dram_tensor(nm, shape, F32, kind="ExternalInput").ap()
    E["idb_d"] = nc.dram_tensor("idb", [128, 128], BF16, kind="ExternalInput").ap()
    E["idf_d"] = nc.dram_tensor("idf", [128, 128], F32, kind="ExternalInput").ap()
    E["y_d"] = nc.dram_tensor("y", [T, D], F32, kind="ExternalOutput").ap()

    E["fbs_d"] = nc.dram_tensor("fbs", [4, D], F32, kind="Internal").ap()
    E["tg_d"] = nc.dram_tensor("tern_g", [D, INNER], BF16, kind="Internal").ap()
    E["tu_d"] = nc.dram_tensor("tern_u", [D, INNER], BF16, kind="Internal").ap()
    E["td_d"] = nc.dram_tensor("tern_d", [INNER, D], BF16, kind="Internal").ap()
    E["h_d"] = nc.dram_tensor("h_sc", [T, INNER], F32, kind="Internal").ap()
    E["rscr_d"] = nc.dram_tensor("rscr", [64, 512], F32, kind="Internal").ap()
    E["x1_d"] = nc.dram_tensor("x1_sc", [T, D], F32, kind="Internal").ap()
    E["wsum_d"] = nc.dram_tensor("wsum", [3, 128], F32, kind="Internal").ap()
    E["sc_d"] = nc.dram_tensor("wsc_sc", [3, 2], F32, kind="Internal").ap()

    tap_shapes = {
        "xT": ([128, NDC * T], BF16), "qT": ([128, NDC * T], BF16),
        "kT": ([128, NDC * T], BF16), "Vaug": ([128, NTT * 1040], BF16),
        "attnoutT": ([128, NDC * T], BF16), "x1": ([T, D], F32),
        "PT0": ([128, 8 * T], BF16), "cb": ([128, 1], F32),
        "fbT": ([128, 32], F32), "WTo": ([128, NDC * D], BF16),
        "terng": ([D, INNER], BF16), "ternd": ([INNER, D], BF16),
        "wsc": ([128, 3], F32), "hsc": ([T, INNER], F32),
        "wacc": ([128, 2], F32), "wrow": ([1, 128], F32), "ws11": ([1, 2], F32),
    }
    E["tap_d"] = {name: nc.dram_tensor("tap_" + name, *tap_shapes[name],
                                       kind="ExternalOutput").ap() for name in taps}
    E["taps"] = taps

    with ExitStack() as st0:
        block = st0.enter_context(nc.Block())
        ctx = Ctx(nc, block, st0)

        for nm, shape, dt in (
                ("ones128", [128, 128], F32), ("identb", [128, 128], BF16),
                ("identf", [128, 128], F32), ("cb128", [128, 1], F32),
                ("epsqn", [128, 1], F32), ("epsln", [128, 1], F32),
                ("fbT", [128, 32], F32), ("fbrows", [1, 2 * D], F32),
                ("c11", [1, 1], F32), ("wsc128", [128, 3], F32),
                ("swq128", [128, 3], F32),
                ("WT_o", [128, NDC * D], BF16),
                ("xT", [128, NDC * T], BF16)):
            E[nm] = sb(ctx, nm, shape, dt)
        E["attnoutT"] = E["xT"]

        stA = st0.enter_context(ExitStack())
        E["qT"] = sb(ctx, "qT", [128, NDC * T], BF16, stA)
        E["kT"] = sb(ctx, "kT", [128, NDC * T], BF16, stA)
        E["Vaug"] = sb(ctx, "Vaug", [128, NTT * 1040], BF16, stA)

        stWQ = st0.enter_context(ExitStack())
        CW = 512
        E["wq_bufs"] = dict(
            CW=CW,
            wbuf=sb(ctx, "wq_wbuf", [128, 2 * CW], F32, stWQ),
            t1=sb(ctx, "wq_t1", [128, 2 * CW], F32, stWQ),
            tbuf=sb(ctx, "wq_tbuf", [128, 2 * CW], BF16, stWQ),
            accs=sb(ctx, "wq_acc", [128, 2], F32, stWQ),
            rowx=sb(ctx, "wq_row", [1, 128], F32, stWQ),
            s11=sb(ctx, "wq_s11", [1, 2], F32, stWQ),
        )
        build_p0(ctx, E, st0)
        if phase_stop == "P0":
            finish(ctx); return nc
        stPW = st0.enter_context(ExitStack())
        build_proj(ctx, E, stPW)
        if phase_stop == "PROJ":
            finish(ctx); return nc
        wq_thunks = build_wq(ctx, E, stWQ, stPW)
        if phase_stop == "WQ":
            for e, f in wq_thunks:
                ctx.sec(e, f)
            finish(ctx); return nc
        m = ctx.marks()
        for e in ("vector", "scalar", "tensor", "gpsimd"):
            ctx.sec(e, lambda eng, m=m: ctx.wait_marks(
                eng, m, ("pe", "dv", "sc", "gp")))
        stPW.close()

        stT = st0.enter_context(ExitStack())
        build_attn(ctx, E, stT, n_heads, wq_thunks)
        ctx.barrier(on=("pe", "dv", "sc", "gp", "rs", "rl0", "rl1",
                        "wv", "wa", "wp", "lw", "wr"))
        if phase_stop == "ATTN":
            finish(ctx); return nc
        stT.close()
        stWQ.close()
        stA.close()

        stF = st0.enter_context(ExitStack())
        build_ffn(ctx, E, stF)
        finish(ctx)
    return nc


def finish(ctx):
    def f(sy):
        m = ctx.marks()
        ctx.wait_marks(sy, m, ("pe", "dv", "sc", "gp", "ld", "lw", "hdr", "fb",
                               "qw", "lx0", "lx1", "rs", "rl0", "rl1",
                               "la0", "la1", "sx0", "sx1", "sy",
                               "wa", "wv", "wp", "lg", "sg", "w0", "w1", "wr"))
        sy.wait_ge(ctx.st.h, ctx.st.n)
    ctx.sec("sync", f)


def build_p0(ctx, E, stX):
    taps, tap_d = E["taps"], E["tap_d"]
    ones128, identb, identf = E["ones128"], E["identb"], E["identf"]
    cb128, fbT, fbrows, c11 = E["cb128"], E["fbT"], E["fbrows"], E["c11"]
    Vaug, xT = E["Vaug"], E["xT"]

    with ExitStack() as s0:
        xld = sb(ctx, "xld", [128, 2 * D], F32, s0)
        xbf = sb(ctx, "xbf", [128, 2 * D], BF16, s0)
        fb32 = sb(ctx, "fb32", [32, 128], F32, s0)
        brow = sb(ctx, "brow", [1, D], F32, s0)
        jrow = sb(ctx, "jrow", [1, D], F32, s0)
        braw = sb(ctx, "braw", [1, 4 * D], F32, s0)
        pxt = psum(ctx, "pxt", [128, 2 * D], BF16, s0)
        pfb = psum(ctx, "pfb", [128, 32], F32, s0)
        pbc = psum(ctx, "pbc", [128, 1], F32, s0)

        cast_mark, evacT_mark = {}, {}
        n = ctx.dv.n + 2
        for tt in range(NTT):
            n += 1
            cast_mark[tt] = n
            if tt >= 1:
                n += 1
                evacT_mark[tt - 1] = n
        n += 1
        evacT_mark[NTT - 1] = n
        cb_mark, fbT_mark = n + 1, n + 2
        pe0 = ctx.pe.n
        xpose_mark = {tt: pe0 + tt + 1 for tt in range(NTT)}
        cbc_mark, pfb_mark = pe0 + NTT + 1, pe0 + NTT + 2
        HDRN = 7 * 16

        def xwait(eng, tt):
            lx = ctx.lx0 if tt % 2 == 0 else ctx.lx1
            eng.wait_ge(lx.h, 16 * (tt // 2 + 1))

        def sy1(sy):
            ctx.hdr.inc(sy.dma_start(out=jrow[:], in_=E["J_d"].rearrange("(p n) -> p n", p=1)), 16)
            for P, nm in enumerate(("qb", "kb", "vb", "ob")):
                ctx.hdr.inc(sy.dma_start(out=braw[:, P * D:(P + 1) * D],
                                         in_=E[nm + "_d"].rearrange("(o a) b -> o (a b)", o=1)), 16)
            ctx.hdr.inc(sy.dma_start(out=identb[:], in_=E["idb_d"]), 16)
            ctx.hdr.inc(sy.dma_start(out=identf[:], in_=E["idf_d"]), 16)
            for tt in range(NTT):
                if tt >= 2:
                    sy.wait_ge(ctx.dv.h, cast_mark[tt - 2])
                lx = ctx.lx0 if tt % 2 == 0 else ctx.lx1
                lx.inc(sy.dma_start(out=xld[:, (tt % 2) * D:(tt % 2 + 1) * D],
                                    in_=E["x_d"][tt * 128:(tt + 1) * 128, :]), 16)
        ctx.sec("sync", sy1)

        def sc1(s):
            s.wait_ge(ctx.hdr.h, HDRN)
            ctx.sc.inc(s.activation(out=jrow[:], in_=jrow[:], func=AF.Sin))
        ctx.sec("scalar", sc1)
        sc_sin = ctx.sc.n

        def ve1(v):
            v.memset(ones128[:], 1.0)
            v.memset(E["epsqn"][:], EPS_QN)
            v.memset(E["epsln"][:], EPS_LN)
            v.memset(Vaug[:], 1.0)
            v.wait_ge(ctx.sc.h, sc_sin)
            v.reduce_sum(out=c11[:], in_=jrow[:], axis=AX.X)
            got = ctx.dv.inc(v.tensor_scalar(out=c11[:], in0=c11[:], scalar1=0.1 / D,
                                             scalar2=-8.0, op0=ALU.mult, op1=ALU.add))
            assert got == 1
            v.wait_ge(ctx.hdr.h, HDRN)
            for P in range(4):
                base = P * D
                b = lambda w: braw[:, base + w * 256: base + (w + 1) * 256]
                t01, t01m = brow[:, 0:256], brow[:, 256:512]
                t23, t23m = brow[:, 512:768], brow[:, 768:1024]
                v.tensor_add(out=t01, in0=b(0), in1=b(1))
                v.tensor_sub(out=t01m, in0=b(0), in1=b(1))
                v.tensor_add(out=t23, in0=b(2), in1=b(3))
                v.tensor_sub(out=t23m, in0=b(2), in1=b(3))
                v.tensor_sub(out=b(0), in0=t01m, in1=t23)
                v.tensor_add(out=b(1), in0=t01, in1=t23m)
                v.tensor_add(out=b(2), in0=t01m, in1=t23)
                last = v.tensor_sub(out=b(3), in0=t01, in1=t23m)
            ctx.dv.inc(last)
            v.tensor_copy(out=fbrows[:, 0:D], in_=braw[:, 2 * D:3 * D])
            v.tensor_copy(out=fbrows[:, D:2 * D], in_=braw[:, 3 * D:4 * D])
            xTr = xT.ap().rearrange("p (dc t) -> p dc t", dc=NDC)
            pxtr = pxt.ap().rearrange("p (b dc t) -> p b dc t", b=2, dc=NDC)

            def evac(tt):
                v.wait_ge(ctx.pe.h, xpose_mark[tt])
                got = ctx.dv.inc(v.tensor_copy(out=xTr[:, :, tt * 128:(tt + 1) * 128],
                                               in_=pxtr[:, tt % 2, :, :]))
                assert got == evacT_mark[tt], (got, evacT_mark[tt])
            for tt in range(NTT):
                xwait(v, tt)
                got = ctx.dv.inc(v.tensor_copy(out=xbf[:, (tt % 2) * D:(tt % 2 + 1) * D],
                                               in_=xld[:, (tt % 2) * D:(tt % 2 + 1) * D]))
                assert got == cast_mark[tt], (got, cast_mark[tt])
                if tt >= 1:
                    evac(tt - 1)
            evac(NTT - 1)
            v.wait_ge(ctx.pe.h, cbc_mark)
            got = ctx.dv.inc(v.tensor_copy(out=cb128[:], in_=pbc[:]))
            assert got == cb_mark
            v.wait_ge(ctx.pe.h, pfb_mark)
            got = ctx.dv.inc(v.tensor_copy(out=fbT[:], in_=pfb[:]))
            assert got == fbT_mark
        ctx.sec("vector", ve1)

        def te1(t):
            for tt in range(NTT):
                t.wait_ge(ctx.dv.h, cast_mark[tt])
                if tt >= 2:
                    t.wait_ge(ctx.dv.h, evacT_mark[tt - 2])
                for dc in range(NDC):
                    mm = t.transpose(
                        out=pxt[:, (tt % 2) * D + dc * 128:(tt % 2) * D + (dc + 1) * 128],
                        in_=xbf[:, (tt % 2) * D + dc * 128:(tt % 2) * D + (dc + 1) * 128],
                        identity=identb[:])
                got = ctx.pe.inc(mm)
                assert got == xpose_mark[tt]
            t.wait_ge(ctx.dv.h, 1)
            got = ctx.pe.inc(t.matmul(out=pbc[:], lhsT=ones128[0:1, :], rhs=c11[:],
                                      start=True, stop=True))
            assert got == cbc_mark
            t.wait_ge(ctx.fb.h, 16)
            got = ctx.pe.inc(t.transpose(out=pfb[:], in_=fb32[:], identity=identf[0:32, 0:32]))
            assert got == pfb_mark
        ctx.sec("tensor", te1)

        def sy2(sy):
            sy.wait_ge(ctx.dv.h, 2)
            base = ctx.st.n
            for P in range(4):
                ctx.st.inc(sy.dma_start(out=E["fbs_d"][P:P + 1, :],
                                        in_=braw[:, P * D:(P + 1) * D]), 16)
            sy.wait_ge(ctx.st.h, base + 64)
            ctx.fb.inc(sy.dma_start(out=fb32[:],
                                    in_=E["fbs_d"].rearrange("a (c p) -> (a c) p", p=128)), 16)
        ctx.sec("sync", sy2)

        if "xT" in taps:
            _tap(ctx, tap_d["xT"], xT[:])
        if "cb" in taps:
            _tap(ctx, tap_d["cb"], cb128[:])
        if "fbT" in taps:
            _tap(ctx, tap_d["fbT"], fbT[:])
        ctx.barrier()


def build_wq(ctx, E, s0, sP):
    wsc128, swq128 = E["wsc128"], E["swq128"]

    B = E["wq_bufs"]
    CW = B["CW"]
    wbuf, t1, tbuf = B["wbuf"], B["t1"], B["tbuf"]
    accs, rowx, s11 = B["accs"], B["rowx"], B["s11"]
    wsum_d, sc_d = E["wsum_d"], E["sc_d"]

    mats = []
    for wk, tk_ in (("wgT_d", "tg_d"), ("wuT_d", "tu_d"), ("wdT_d", "td_d")):
        w_d, t_d = E[wk], E[tk_]
        rows, cols = w_d.shape
        chunks = [(r, c) for r in range(rows // 128) for c in range(cols // CW)]
        mats.append((w_d, t_d, chunks))

    thunks = []
    sec = lambda e, f: thunks.append((e, f))

    sc_done = {}
    prev_p1 = {0: 0, 1: 0}
    prev_acc_store = 0
    for mi, (w_d, t_d, chunks) in enumerate(mats):
        nch = len(chunks)
        p1_mark = {ch: ctx.wv.n + ch + 1 for ch in range(nch)}
        ctx.wv.n += nch
        ld_mark = {}
        for ch in range(nch):
            lp = (ctx.w0, ctx.w1)[ch % 2]
            ld_mark[ch] = lp.n + 16
            lp.n += 16

        for ch, (r, c) in enumerate(chunks):
            def f_l(sy, ch=ch, r=r, c=c, w_d=w_d, p1_mark=p1_mark,
                    pp=dict(prev_p1)):
                if ch >= 2:
                    sy.wait_ge(ctx.wv.h, p1_mark[ch - 2])
                elif pp[ch % 2]:
                    sy.wait_ge(ctx.wv.h, pp[ch % 2])
                sy.dma_start(out=wbuf[:, (ch % 2) * CW:(ch % 2 + 1) * CW],
                             in_=w_d[r * 128:(r + 1) * 128, c * CW:(c + 1) * CW]
                             ).then_inc((ctx.w0, ctx.w1)[ch % 2].h, 16)
            sec("gpsimd", f_l)

            def f_s(v, ch=ch, ld_mark=ld_mark, p1_mark=p1_mark,
                    pas=prev_acc_store):
                v.wait_ge((ctx.w0, ctx.w1)[ch % 2].h, ld_mark[ch])
                if ch == 0 and pas:
                    v.wait_ge(ctx.wr.h, pas)
                src = wbuf[:, (ch % 2) * CW:(ch % 2 + 1) * CW]
                v.scalar_tensor_tensor(out=t1[:, 0:CW], in0=src, scalar=-1.0,
                                       in1=src, op0=ALU.mult, op1=ALU.max,
                                       accum_out=accs[:, 1:2])
                if ch == 0:
                    inst = v.tensor_copy(out=accs[:, 0:1], in_=accs[:, 1:2])
                else:
                    inst = v.tensor_add(out=accs[:, 0:1], in0=accs[:, 0:1],
                                        in1=accs[:, 1:2])
                inst.then_inc(ctx.wv.h, 1)
                ctx._note_sync(inst, ctx.wv.h, p1_mark[ch])
            sec("vector", f_s)
        prev_p1 = {0: p1_mark[nch - 2], 1: p1_mark[nch - 1]}

        acc_done = p1_mark[nch - 1]
        wr0 = ctx.wr.n
        m_strow, m_ldrow = wr0 + 16, wr0 + 32
        s11_mark = ctx.wv.n + 1
        ctx.wv.n += 1
        m_sts11, m_bc = wr0 + 48, wr0 + 80
        ctx.wr.n = m_bc
        prev_acc_store = m_strow

        def f_rt1(sy, mi=mi, acc_done=acc_done, m_strow=m_strow, wr0=wr0):
            sy.wait_ge(ctx.wv.h, acc_done)
            if wr0:
                sy.wait_ge(ctx.wr.h, wr0)
            sy.dma_start(
                out=wsum_d[mi:mi + 1, :].rearrange("a b -> b a"),
                in_=accs[:, 0:1]).then_inc(ctx.wr.h, 16)
            sy.wait_ge(ctx.wr.h, m_strow)
            sy.dma_start(out=rowx[:], in_=wsum_d[mi:mi + 1, :]
                         ).then_inc(ctx.wr.h, 16)
        sec("gpsimd", f_rt1)

        def f_sum(v, m_ldrow=m_ldrow, s11_mark=s11_mark):
            v.wait_ge(ctx.wr.h, m_ldrow)
            v.reduce_sum(out=s11[:, 0:1], in_=rowx[:], axis=AX.X)
            v.tensor_scalar(out=s11[:, 0:1], in0=s11[:, 0:1],
                            scalar1=1.0 / NELEM_W, scalar2=1e-5,
                            op0=ALU.mult, op1=ALU.max)
            inst = v.reciprocal(out=s11[:, 1:2], in_=s11[:, 0:1])
            inst.then_inc(ctx.wv.h, 1)
            ctx._note_sync(inst, ctx.wv.h, s11_mark)
        sec("vector", f_sum)

        def f_rt2(sy, mi=mi, s11_mark=s11_mark, m_sts11=m_sts11):
            sy.wait_ge(ctx.wv.h, s11_mark)
            sy.dma_start(out=sc_d[mi:mi + 1, :], in_=s11[:]
                         ).then_inc(ctx.wr.h, 16)
            sy.wait_ge(ctx.wr.h, m_sts11)
            _, bap0 = bass.broadcast_tensor_aps(wsc128[:, mi:mi + 1],
                                                sc_d[mi:mi + 1, 0:1])
            sy.dma_start(out=wsc128[:, mi:mi + 1], in_=bap0
                         ).then_inc(ctx.wr.h, 16)
            _, bap1 = bass.broadcast_tensor_aps(swq128[:, mi:mi + 1],
                                                sc_d[mi:mi + 1, 1:2])
            sy.dma_start(out=swq128[:, mi:mi + 1], in_=bap1
                         ).then_inc(ctx.wr.h, 16)
        sec("gpsimd", f_rt2)
        sc_done[mi] = m_bc

    p1_final = dict(prev_p1)
    E["wq_scales_wr"] = ctx.wr.n

    prev_q = 0
    for mi, (w_d, t_d, chunks) in enumerate(mats):
        nch = len(chunks)
        st_base = ctx.sg.n
        q_mark = {ch: ctx.wv.n + ch + 1 for ch in range(nch)}
        ctx.wv.n += nch
        ctx.sg.n += 16 * nch
        ld2_mark = {}
        for ch in range(nch):
            lp = (ctx.w0, ctx.w1)[ch % 2]
            ld2_mark[ch] = lp.n + 16
            lp.n += 16

        for ch, (r, c) in enumerate(chunks):
            def f_ls(sy, ch=ch, r=r, c=c, chunks=chunks, q_mark=q_mark,
                     nch=nch, t_d=t_d, w_d=w_d, prev_q=prev_q):
                if ch >= 2:
                    sy.wait_ge(ctx.wv.h, q_mark[ch - 2])
                elif prev_q:
                    sy.wait_ge(ctx.wv.h, prev_q)
                else:
                    sy.wait_ge(ctx.wv.h, p1_final[ch % 2])
                sy.dma_start(out=wbuf[:, (ch % 2) * CW:(ch % 2 + 1) * CW],
                             in_=w_d[r * 128:(r + 1) * 128, c * CW:(c + 1) * CW]
                             ).then_inc((ctx.w0, ctx.w1)[ch % 2].h, 16)
                if ch >= 1:
                    r0, c0 = chunks[ch - 1]
                    sy.wait_ge(ctx.wv.h, q_mark[ch - 1])
                    sy.dma_start(out=t_d[r0 * 128:(r0 + 1) * 128, c0 * CW:(c0 + 1) * CW],
                                 in_=tbuf[:, ((ch - 1) % 2) * CW:((ch - 1) % 2 + 1) * CW]
                                 ).then_inc(ctx.sg.h, 16)
                if ch == nch - 1:
                    sy.wait_ge(ctx.wv.h, q_mark[ch])
                    sy.dma_start(out=t_d[r * 128:(r + 1) * 128, c * CW:(c + 1) * CW],
                                 in_=tbuf[:, (ch % 2) * CW:(ch % 2 + 1) * CW]
                                 ).then_inc(ctx.sg.h, 16)
            sec("gpsimd", f_ls)

            def f_q(v, ch=ch, mi=mi, ld2_mark=ld2_mark, sc_done=sc_done,
                    nch=nch, st_base=st_base, q_mark=q_mark):
                v.wait_ge((ctx.w0, ctx.w1)[ch % 2].h, ld2_mark[ch])
                if ch == 0:
                    v.wait_ge(ctx.wr.h, sc_done[mi])
                if ch >= 2:
                    v.wait_ge(ctx.sg.h, st_base + 16 * ch)
                else:
                    v.wait_ge(ctx.sg.h, st_base)
                src = wbuf[:, (ch % 2) * CW:(ch % 2 + 1) * CW]
                tc = t1[:, (ch % 2) * CW:(ch % 2 + 1) * CW]
                v.tensor_scalar(out=tc, in0=src, scalar1=swq128[:, mi:mi + 1],
                                scalar2=MAGIC, op0=ALU.mult, op1=ALU.add)
                v.tensor_scalar(out=tc, in0=tc, scalar1=MAGIC, scalar2=1.0,
                                op0=ALU.subtract, op1=ALU.min)
                v.tensor_single_scalar(out=tc, in_=tc, scalar=-1.0, op=ALU.max)
                v.tensor_copy(out=src.bitcast(I32), in_=tc.bitcast(I32))
                inst = v.tensor_copy(out=tbuf[:, (ch % 2) * CW:(ch % 2 + 1) * CW],
                                     in_=src)
                inst.then_inc(ctx.wv.h, 1)
                ctx._note_sync(inst, ctx.wv.h, q_mark[ch])
            sec("vector", f_q)
        prev_q = q_mark[nch - 1]

    E["tern_st_mark"] = ctx.sg.n
    return thunks


def build_proj(ctx, E, s0):
    taps, tap_d = E["taps"], E["tap_d"]
    xT, qT, kT, Vaug, WT_o = E["xT"], E["qT"], E["kT"], E["Vaug"], E["WT_o"]
    fbT, fbrows, ones128, epsqn = E["fbT"], E["fbrows"], E["ones128"], E["epsqn"]

    if True:
        qWs = sb(ctx, "qWs", [128, 4 * 256], F32, s0)
        WTx = sb(ctx, "WTx", [128, NDC * D], BF16, s0)
        sqs = sb(ctx, "sqs", [128, 4 * 512], F32, s0)
        nb = sb(ctx, "nb", [128, 512], F32, s0)
        invb = sb(ctx, "invb", [128, 512], F32, s0)
        vbbc = sqs.ap()[:, 0:D]
        pq = psum(ctx, "pq", [128, 4 * 512], F32, s0)
        pv = psum(ctx, "pv", [128, 2 * 512], F32, s0)
        pbb = pq.ap()[:, 0:512]

        for nm in ("oW", "qW", "kW", "vW"):
            pidx = {"qW": 0, "kW": 1, "vW": 2, "oW": 3}[nm]
            WT_dst = WT_o if nm == "oW" else WTx
            mld = ctx.marks()

            for wh in range(2):
                mld2 = ctx.marks()

                def ld_w(sy, w_d=E[nm + "_d"], wh=wh, mld=mld, mld2=mld2):
                    if wh == 0:
                        ctx.wait_marks(sy, mld, ("pe", "dv"))
                    else:
                        sy.wait_ge(ctx.dv.h, mld2["dv"])
                    ctx.qw.inc(sy.dma_start(
                        out=qWs.ap().rearrange("p (w dh o) -> p w dh o", w=2, dh=2),
                        in_=w_d.rearrange("w (dh p) o -> p w dh o",
                                          p=128)[:, wh * 2:(wh + 1) * 2]), 16)
                ctx.sec("sync", ld_w)
                mw = ctx.marks()

                def asm(v, WT_dst=WT_dst, wh=wh, mw=mw):
                    v.wait_ge(ctx.qw.h, mw["qw"])
                    if wh == 0:
                        v.wait_ge(ctx.pe.h, mw["pe"])
                    for dc in range(NDC):
                        c, dh = dc // 2, dc % 2
                        for p in range(4):
                            w, sign = SIGN_TABLE[p][c]
                            if w // 2 != wh:
                                continue
                            last = v.tensor_scalar_mul(
                                out=WT_dst[:, dc * D + p * 256:dc * D + (p + 1) * 256],
                                in0=qWs[:, ((w % 2) * 2 + dh) * 256:((w % 2) * 2 + dh + 1) * 256],
                                scalar1=sign)
                    ctx.dv.inc(last)
                ctx.sec("vector", asm)
            masm = ctx.marks()

            if nm == "oW":
                continue

            if nm in ("qW", "kW"):
                dstT = qT if nm == "qW" else kT
                sched = {}
                pe_n, sc_n, dv_n = ctx.pe.n, ctx.sc.n, ctx.dv.n
                for i, (g, ts_) in enumerate([(g, s_) for g in range(2) for s_ in range(4)]):
                    pe_n += 1
                    sc_n += 1
                    dv_n += 1
                    sc_n += 1
                    dv_n += 4
                    sched[(g, ts_)] = dict(mm=pe_n, sq=sc_n - 1, ssum=dv_n - 4,
                                           sqrt=sc_n, out=dv_n)

                def qk_te(t, masm=masm, sched=sched):
                    t.wait_ge(ctx.dv.h, masm["dv"])
                    for g in range(2):
                        for ts_ in range(4):
                            if (g, ts_) != (0, 0):
                                prev = (g, ts_ - 1) if ts_ > 0 else (g - 1, 3)
                                t.wait_ge(ctx.dv.h, sched[prev]["out"])
                            for jj in range(4):
                                j = g + 2 * jj
                                for dc in range(NDC):
                                    mm = t.matmul(
                                        out=pq[:, jj * 512:(jj + 1) * 512],
                                        lhsT=WTx[:, dc * D + j * 128:dc * D + (j + 1) * 128],
                                        rhs=xT[:, dc * T + ts_ * 512:dc * T + (ts_ + 1) * 512],
                                        start=(dc == 0), stop=(dc == NDC - 1))
                            got = ctx.pe.inc(mm)
                            assert got == sched[(g, ts_)]["mm"]
                ctx.sec("tensor", qk_te)

                def qk_sc(s, pidx=pidx, sched=sched):
                    for g in range(2):
                        for ts_ in range(4):
                            sd = sched[(g, ts_)]
                            s.wait_ge(ctx.pe.h, sd["mm"])
                            for jj in range(4):
                                j = g + 2 * jj
                                a = s.activation(out=sqs[:, jj * 512:(jj + 1) * 512],
                                                 in_=pq[:, jj * 512:(jj + 1) * 512],
                                                 func=AF.Square,
                                                 bias=fbT[:, pidx * 8 + j:pidx * 8 + j + 1])
                            got = ctx.sc.inc(a)
                            assert got == sd["sq"]
                            s.wait_ge(ctx.dv.h, sd["ssum"])
                            got = ctx.sc.inc(s.activation(out=nb[:], in_=sqs[:, 0:512],
                                                          func=AF.Sqrt, bias=epsqn[:]))
                            assert got == sd["sqrt"]
                ctx.sec("scalar", qk_sc)

                def qk_ve(v, pidx=pidx, dstT=dstT, sched=sched):
                    for g in range(2):
                        for ts_ in range(4):
                            sd = sched[(g, ts_)]
                            v.wait_ge(ctx.sc.h, sd["sq"])
                            v.tensor_add(out=sqs[:, 0:512], in0=sqs[:, 0:512],
                                         in1=sqs[:, 512:1024])
                            v.tensor_add(out=sqs[:, 0:512], in0=sqs[:, 0:512],
                                         in1=sqs[:, 1024:1536])
                            got = ctx.dv.inc(v.tensor_add(out=sqs[:, 0:512],
                                                          in0=sqs[:, 0:512],
                                                          in1=sqs[:, 1536:2048]))
                            assert got == sd["ssum"]
                            v.wait_ge(ctx.sc.h, sd["sqrt"])
                            v.reciprocal(out=invb[:], in_=nb[:])
                            for jj in range(4):
                                j = g + 2 * jj
                                got = ctx.dv.inc(v.scalar_tensor_tensor(
                                    out=dstT[:, j * T + ts_ * 512:j * T + (ts_ + 1) * 512],
                                    in0=pq[:, jj * 512:(jj + 1) * 512],
                                    scalar=fbT[:, pidx * 8 + j:pidx * 8 + j + 1],
                                    in1=invb[:], op0=ALU.add, op1=ALU.mult))
                            assert got == sd["out"]
                ctx.sec("vector", qk_ve)
            else:
                mv0 = ctx.marks()
                bb_pe = {0: mv0["pe"] + 1, 1: mv0["pe"] + 2}
                bb_dv = {0: mv0["dv"] + 1, 1: mv0["dv"] + 2}

                def vb_te(t, mv0=mv0, bb_dv=bb_dv):
                    t.wait_ge(ctx.dv.h, mv0["dv"])
                    for os in range(2):
                        if os == 1:
                            t.wait_ge(ctx.dv.h, bb_dv[0])
                        ctx.pe.inc(t.matmul(
                            out=pbb[:], lhsT=ones128[0:1, :],
                            rhs=fbrows[:, os * 512:(os + 1) * 512],
                            start=True, stop=True))
                ctx.sec("tensor", vb_te)

                def vb_ve(v, bb_pe=bb_pe):
                    for os in range(2):
                        v.wait_ge(ctx.pe.h, bb_pe[os])
                        ctx.dv.inc(v.tensor_copy(out=vbbc[:, os * 512:(os + 1) * 512],
                                                 in_=pbb[:]))
                ctx.sec("vector", vb_ve)
                mvb = ctx.marks()

                Vr = Vaug.ap().rearrange("p (tt h c) -> p tt h c", tt=NTT, h=H)
                vb_r = vbbc.rearrange("p (os h c) -> p os h c", os=2, c=64)
                pv_r = pv.ap().rearrange("p (b h c) -> p b h c", b=2, c=64)
                vmm = {i: mvb["pe"] + i + 1 for i in range(32)}
                vev = {i: mvb["dv"] + i + 1 for i in range(32)}

                def v_te(t, masm=masm, mvb=mvb, vmm=vmm, vev=vev):
                    t.wait_ge(ctx.dv.h, masm["dv"])
                    for tt in range(NTT):
                        for os in range(2):
                            i = tt * 2 + os
                            if i >= 2:
                                t.wait_ge(ctx.dv.h, vev[i - 2])
                            for dc in range(NDC):
                                mm = t.matmul(
                                    out=pv[:, (i % 2) * 512:(i % 2 + 1) * 512],
                                    lhsT=xT[:, dc * T + tt * 128:dc * T + (tt + 1) * 128],
                                    rhs=WTx[:, dc * D + os * 512:dc * D + (os + 1) * 512],
                                    start=(dc == 0), stop=(dc == NDC - 1))
                            got = ctx.pe.inc(mm)
                            assert got == vmm[i]
                ctx.sec("tensor", v_te)

                def v_ve(v, vmm=vmm, vev=vev):
                    for tt in range(NTT):
                        for os in range(2):
                            i = tt * 2 + os
                            v.wait_ge(ctx.pe.h, vmm[i])
                            got = ctx.dv.inc(v.tensor_tensor(
                                out=Vr[:, tt, os * 8:(os + 1) * 8, 0:64],
                                in0=pv_r[:, i % 2, :, :],
                                in1=vb_r[:, os, :, :], op=ALU.add))
                            assert got == vev[i]
                ctx.sec("vector", v_ve)

        if "qT" in taps:
            _tap(ctx, tap_d["qT"], qT[:])
        if "kT" in taps:
            _tap(ctx, tap_d["kT"], kT[:])
        if "Vaug" in taps:
            _tap(ctx, tap_d["Vaug"], Vaug[:])
        if "WTo" in taps:
            _tap(ctx, tap_d["WTo"], WT_o[:])


def build_attn(ctx, E, stT, n_heads, wq_thunks=()):
    taps, tap_d = E["taps"], E["tap_d"]
    qT, kT, Vaug, attnoutT = E["qT"], E["kT"], E["Vaug"], E["attnoutT"]
    cb128, rscr_d = E["cb128"], E["rscr_d"]

    NSLOT = 8
    PT = sb(ctx, "PT", [128, NSLOT * T], BF16, stT)
    rrow = sb(ctx, "rrow", [128, 2 * 512], F32, stT)
    rbc = sb(ctx, "rbc", [128, 2 * 512], F32, stT)
    pS0 = psum(ctx, "pS0", [128, 1024], F32, stT)
    pS1 = psum(ctx, "pS1", [128, 1024], F32, stT)
    pS = (pS0, pS1)
    pPV = psum(ctx, "pPV", [128, 4 * 512], F32, stT)

    m0 = ctx.marks()
    Vr = Vaug.ap().rearrange("p (tt hc) -> p tt hc", tt=NTT)
    NH2 = n_heads * 2

    smm, expm, pvm = {}, {}, {}
    rrm, evm = {}, {}
    rs_store, rs_read = {}, {}
    pe_n, sc_n, dv_n, rs_n = ctx.pe.n, ctx.sc.n, ctx.dv.n, ctx.rs.n
    rl_n = {0: ctx.rl0.n, 1: ctx.rl1.n}
    for gh in range(NH2 + 1):
        for kc8 in range(8):
            if gh >= 1:
                pe_n += 1
                pvm[(gh - 1, kc8)] = pe_n
            if gh < NH2:
                for qh in range(2):
                    pe_n += 1
                    smm[(gh, kc8, qh)] = pe_n
                    sc_n += 1
                    expm[(gh, kc8, qh)] = sc_n
        if gh < NH2:
            expm[gh] = sc_n
        if gh >= 1:
            pvm[gh - 1] = pvm[(gh - 1, 7)]
        if gh >= 2 and gh % 2 == 0:
            h = (gh - 2) // 2
            for qb in range(4):
                dv_n += 1
                rrm[(h, qb)] = dv_n
                rs_n += 16
                rs_store[(h, qb)] = rs_n
                rl_n[qb % 2] += 16
                rs_read[(h, qb)] = rl_n[qb % 2]
            for qb in range(4):
                dv_n += 1
                evm[(h, qb)] = dv_n
            evm[h] = dv_n
    def emit_half_pe(gh):
        def f(t, gh=gh):
            if gh < NH2:
                h, ha = gh // 2, gh % 2
                p0 = 64 * (h % 2)
                bq = (h // 2) * T
            if gh >= 1:
                hp, hap = (gh - 1) // 2, (gh - 1) % 2
                oddp = hp % 2
            for kc8 in range(8):
                if gh >= 1:
                    kcp = hap * 8 + kc8
                    t.wait_ge(ctx.sc.h, expm[(gh - 1, kc8, 1)])
                    if kcp == 0 and hp > 0:
                        t.wait_ge(ctx.dv.h, evm[hp - 1])
                    for qb in range(4):
                        rhsPT = PT[:, kc8 * T + qb * 512:kc8 * T + (qb + 1) * 512]
                        if not oddp:
                            mm = t.matmul(
                                out=pPV[0:65, qb * 512:(qb + 1) * 512],
                                lhsT=Vr[:, kcp, hp * 65:(hp + 1) * 65],
                                rhs=rhsPT, start=(kcp == 0), stop=(kcp == 15))
                        else:
                            t.matmul(
                                out=pPV[64:128, qb * 512:(qb + 1) * 512],
                                lhsT=Vr[:, kcp, hp * 65:hp * 65 + 64],
                                rhs=rhsPT, start=(kcp == 0), stop=(kcp == 15),
                                tile_position=(0, 64))
                            mm = t.matmul(
                                out=pPV[0:1, qb * 512:(qb + 1) * 512],
                                lhsT=Vr[:, kcp, hp * 65 + 64:hp * 65 + 65],
                                rhs=rhsPT, start=(kcp == 0), stop=(kcp == 15),
                                tile_position=(0, 0))
                    got = ctx.pe.inc(mm)
                    assert got == pvm[(gh - 1, kc8)]
                if gh < NH2:
                    kc = ha * 8 + kc8
                    for qh in range(2):
                        gp_ = gh * 16 + kc8 * 2 + qh
                        if gp_ >= 2:
                            pg = gp_ - 2
                            t.wait_ge(ctx.sc.h, expm[(pg // 16, (pg % 16) // 2, pg % 2)])
                        else:
                            ctx.wait_marks(t, m0, ("dv", "sc"))
                        buf = pS[gp_ % 2]
                        for qs in (2 * qh, 2 * qh + 1):
                            mm = t.matmul(
                                out=buf[:, (qs % 2) * 512:(qs % 2 + 1) * 512],
                                lhsT=kT[p0:p0 + 64, bq + kc * 128:bq + (kc + 1) * 128],
                                rhs=qT[p0:p0 + 64, bq + qs * 512:bq + (qs + 1) * 512],
                                start=True, stop=True)
                        got = ctx.pe.inc(mm)
                        assert got == smm[(gh, kc8, qh)]
        ctx.sec("tensor", f)

    def emit_all_exps():
        def f(s):
            for gh in range(NH2):
                for kc8 in range(8):
                    gkc = gh * 8 + kc8
                    slot = gkc % NSLOT
                    for qh in range(2):
                        gp_ = gh * 16 + kc8 * 2 + qh
                        s.wait_ge(ctx.pe.h, smm[(gh, kc8, qh)])
                        if gh >= 1 and qh == 0:
                            s.wait_ge(ctx.pe.h, pvm[(gh - 1, kc8)])
                        a = s.activation(
                            out=PT[:, slot * T + qh * 1024:slot * T + (qh + 1) * 1024],
                            in_=pS[gp_ % 2][:], func=AF.Exp, scale=0.125,
                            bias=cb128[:])
                        got = ctx.sc.inc(a)
                        assert got == expm[(gh, kc8, qh)]
        ctx.sec("scalar", f)

    rr_free = [0]
    rb_free = {0: 0, 1: 0}

    def emit_tail(h):
        odd = h % 2
        p0 = 64 * odd
        srow = 0 if odd else 64
        dcq = h // 2

        def fv1(v, h=h, srow=srow):
            v.wait_ge(ctx.pe.h, pvm[2 * h + 1])
            for qb in range(4):
                if rr_free[0]:
                    v.wait_ge(ctx.rs.h, rr_free[0])
                r = v.reciprocal(out=rrow[srow:srow + 1, (qb % 2) * 512:(qb % 2 + 1) * 512],
                                 in_=pPV[srow:srow + 1, qb * 512:(qb + 1) * 512])
                r.then_inc(ctx.dv.h, 1)
                ctx._note_sync(r, ctx.dv.h, rrm[(h, qb)])
                rr_free[0] = rs_store[(h, qb)]
            ctx.dv.n = max(ctx.dv.n, rrm[(h, 3)])
        ctx.sec("vector", fv1)

        def fs1(sy, h=h, srow=srow, p0=p0):
            for qb in range(4):
                rl = (ctx.rl0, ctx.rl1)[qb % 2]
                sy.wait_ge(ctx.dv.h, rrm[(h, qb)])
                sy.dma_start(out=rscr_d[h * 4 + qb:h * 4 + qb + 1, :],
                             in_=rrow[srow:srow + 1, (qb % 2) * 512:(qb % 2 + 1) * 512]
                             ).then_inc(ctx.rs.h, 16)
                sy.wait_ge(ctx.rs.h, rs_store[(h, qb)])
                if rb_free[qb % 2]:
                    sy.wait_ge(ctx.dv.h, rb_free[qb % 2])
                bsrc = rscr_d[h * 4 + qb:h * 4 + qb + 1, :]
                _, bap = bass.broadcast_tensor_aps(rbc[p0:p0 + 64, (qb % 2) * 512:(qb % 2 + 1) * 512], bsrc)
                sy.dma_start(out=rbc[p0:p0 + 64, (qb % 2) * 512:(qb % 2 + 1) * 512],
                             in_=bap).then_inc(rl.h, 16)
                rb_free[qb % 2] = evm[(h, qb)]
        ctx.sec("sync", fs1)
        ctx.rs.n = rs_store[(h, 3)]
        ctx.rl0.n = rs_read[(h, 2)]
        ctx.rl1.n = rs_read[(h, 3)]

        def fv2(v, h=h, p0=p0, dcq=dcq):
            for qb in range(4):
                rl = (ctx.rl0, ctx.rl1)[qb % 2]
                v.wait_ge(rl.h, rs_read[(h, qb)])
                ev = v.tensor_tensor(
                    out=attnoutT[p0:p0 + 64, dcq * T + qb * 512:dcq * T + (qb + 1) * 512],
                    in0=pPV[p0:p0 + 64, qb * 512:(qb + 1) * 512],
                    in1=rbc[p0:p0 + 64, (qb % 2) * 512:(qb % 2 + 1) * 512],
                    op=ALU.mult)
                ev.then_inc(ctx.dv.h, 1)
                ctx._note_sync(ev, ctx.dv.h, evm[(h, qb)])
            ctx.dv.n = max(ctx.dv.n, evm[(h, 3)])
        ctx.sec("vector", fv2)

    wq_i = [0]
    wq_thunks = list(wq_thunks)

    def emit_wq_batch(frac_done):
        want = int(len(wq_thunks) * frac_done)
        while wq_i[0] < want:
            e, f = wq_thunks[wq_i[0]]
            ctx.sec(e, f)
            wq_i[0] += 1

    emit_all_exps()
    for gh in range(NH2 + 1):
        emit_half_pe(gh)
        if gh >= 2 and gh % 2 == 0:
            emit_tail((gh - 2) // 2)
        emit_wq_batch(min(1.0, (gh + 1) / (NH2 * 3 // 4)))
    emit_wq_batch(1.0)
    assert ctx.pe.n == pe_n and ctx.sc.n == sc_n, (ctx.pe.n, pe_n, ctx.sc.n, sc_n)
    assert ctx.dv.n == dv_n and ctx.rs.n == rs_n, (ctx.dv.n, dv_n, ctx.rs.n, rs_n)

    if "PT0" in taps:
        _tap(ctx, tap_d["PT0"], PT[:, 0:8 * T])
    if "attnoutT" in taps:
        _tap(ctx, tap_d["attnoutT"], attnoutT[:])


def build_ffn(ctx, E, stF):
    taps, tap_d = E["taps"], E["tap_d"]
    attnoutT, WT_o = E["attnoutT"], E["WT_o"]
    fbrows, ones128, epsln = E["fbrows"], E["ones128"], E["epsln"]
    identb = E["identb"]
    wsc128 = E["wsc128"]
    tg_d, tu_d, td_d, h_d, x1_d, y_d = (E["tg_d"], E["tu_d"], E["td_d"],
                                        E["h_d"], E["x1_d"], E["y_d"])
    x_d = E["x_d"]
    tern_st = E["tern_st_mark"]

    yt = sb(ctx, "yt", [128, 2 * D], F32, stF)
    x1r = sb(ctx, "x1r", [128, 2 * D], F32, stF)
    bObc = sb(ctx, "bObc", [128, D], F32, stF)
    xr = sb(ctx, "xr", [128, 2 * D], F32, stF)
    r1 = sb(ctx, "r1", [128, 2 * D], F32, stF)
    tmp = sb(ctx, "tmp", [128, D], F32, stF)
    lnt = sb(ctx, "lnt", [128, 8], F32, stF)
    vars_ = sb(ctx, "vars", [128, NTT], F32, stF)
    rstd_a = sb(ctx, "rstd_a", [128, NTT], F32, stF)
    mdq = sb(ctx, "mdq", [128, NTT], F32, stF)
    dqg = sb(ctx, "dqg", [128, NTT], F32, stF)
    dqu = sb(ctx, "dqu", [128, NTT], F32, stF)
    dqd2 = sb(ctx, "dqd2", [128, NTT], F32, stF)
    xqb = sb(ctx, "xqb", [128, 2 * D], BF16, stF)
    xqTg = sb(ctx, "xqTg", [128, NDC * 512], BF16, stF)
    gss = sb(ctx, "gss", [128, 512], F32, stF)
    uss = sb(ctx, "uss", [128, 512], F32, stF)
    hst = sb(ctx, "hst", [128, 2 * 512], F32, stF)
    wgs = sb(ctx, "wgs", [128, 2 * NDC * 512], BF16, stF)
    wus = sb(ctx, "wus", [128, 2 * NDC * 512], BF16, stF)
    wds = sb(ctx, "wds", [128, 8 * D], BF16, stF)
    hrl = sb(ctx, "hrl", [128, INNER], F32, stF)
    hqb = sb(ctx, "hqb", [128, 2 * INNER], BF16, stF)
    hqT8 = sb(ctx, "hqT8", [128, 2 * 1024], BF16, stF)
    pO = psum(ctx, "pO", [128, 2 * 512], F32, stF)
    pGUD = psum(ctx, "pGUD", [128, 4 * 512], F32, stF)
    pXQ = psum(ctx, "pXQ", [128, 2 * 1024], BF16, stF)

    mo = ctx.marks()

    def ob_te(t, mo=mo):
        ctx.wait_marks(t, mo, ("dv",))
        for os in range(2):
            if os == 1:
                t.wait_ge(ctx.dv.h, mo["dv"] + 1)
            ctx.pe.inc(t.matmul(out=pO[:, 0:512], lhsT=ones128[0:1, :],
                                rhs=fbrows[:, D + os * 512:D + (os + 1) * 512],
                                start=True, stop=True))
    ctx.sec("tensor", ob_te)

    def ob_ve(v, mo=mo):
        for os in range(2):
            v.wait_ge(ctx.pe.h, mo["pe"] + os + 1)
            ctx.dv.inc(v.tensor_copy(out=bObc[:, os * 512:(os + 1) * 512],
                                     in_=pO[:, 0:512]))
    ctx.sec("vector", ob_ve)
    ctx.barrier(on=("pe", "dv"))

    xq_par = [0]
    pxq_free = {0: 0, 1: 0}
    prev = dict(dev=0, r1ev={0: 0, 1: 0}, xrfree={0: 0, 1: 0},
                tp_pe=0, mm_pe=0, x1st={0: 0, 1: 0}, y_st={0: 0, 1: 0})

    for g in range(4):
        tilesched = []
        for k4 in range(4):
            tt = g * 4 + k4
            s = {}
            la = (ctx.la0, ctx.la1)[k4 % 2]
            s["lda"] = la.n + 16
            la.n += 16
            tilesched.append(s)
        pe_s, dv_s, st_s = ctx.pe.n, ctx.dv.n, ctx.st.n

        for k4, off in enumerate((1, 2, 4, 6)):
            tilesched[k4]["omm"] = pe_s + off
        for k4, off in enumerate((3, 5, 7, 8)):
            tilesched[k4]["tp"] = pe_s + off
        for k4 in range(4):
            s = tilesched[k4]
            s["r1ev"] = ctx.dv.n + 1
            s["dev"] = ctx.dv.n + 2
            s["q"] = ctx.dv.n + 3
            s["xqev"] = ctx.dv.n + 4
            ctx.dv.n += 4
            s["par"] = xq_par[0] % 2
            xq_par[0] += 1
            sx = (ctx.sx0, ctx.sx1)[k4 % 2]
            s["x1st"] = sx.n + 16
            sx.n += 16
        ctx.dv.n = dv_s

        def p_ldr(sy, g=g, tilesched=tilesched, pxr=dict(prev["xrfree"]),
                  pdev=prev.get("lastdev", 0)):
            for k4 in range(4):
                tt = g * 4 + k4
                if k4 >= 2:
                    sy.wait_ge(ctx.dv.h, tilesched[k4 - 2]["dev"])
                elif pxr[k4 % 2]:
                    sy.wait_ge(ctx.dv.h, pxr[k4 % 2])
                sy.dma_start(out=xr[:, (k4 % 2) * D:(k4 % 2 + 1) * D],
                             in_=x_d[tt * 128:(tt + 1) * 128, :]
                             ).then_inc((ctx.la0, ctx.la1)[k4 % 2].h, 16)
        ctx.sec("sync", p_ldr)

        def p_te(t, g=g, tilesched=tilesched, prl=dict(prev["r1ev"]),
                 ppx=dict(pxq_free)):
            def omm(k4):
                tt = g * 4 + k4
                s = tilesched[k4]
                if k4 >= 1:
                    t.wait_ge(ctx.dv.h, tilesched[k4 - 1]["r1ev"])
                elif prl[0]:
                    t.wait_ge(ctx.dv.h, prl[0])
                for os in range(2):
                    for dc in range(NDC):
                        mm = t.matmul(
                            out=pO[:, os * 512:(os + 1) * 512],
                            lhsT=attnoutT[:, dc * T + tt * 128:dc * T + (tt + 1) * 128],
                            rhs=WT_o[:, dc * D + os * 512:dc * D + (os + 1) * 512],
                            start=(dc == 0), stop=(dc == NDC - 1))
                got = ctx.pe.inc(mm)
                assert got == s["omm"], (got, s["omm"], k4)

            def tp(k4):
                s = tilesched[k4]
                par = s["par"]
                t.wait_ge(ctx.dv.h, s["q"])
                if k4 >= 2:
                    t.wait_ge(ctx.dv.h, tilesched[k4 - 2]["xqev"])
                elif ppx[par]:
                    t.wait_ge(ctx.dv.h, ppx[par])
                    ppx[par] = 0
                for dc in range(NDC):
                    mm = t.transpose(
                        out=pXQ[:, par * 1024 + dc * 128:par * 1024 + (dc + 1) * 128],
                        in_=xqb[:, (k4 % 2) * D + dc * 128:(k4 % 2) * D + (dc + 1) * 128],
                        identity=identb[:])
                got = ctx.pe.inc(mm)
                assert got == s["tp"], (got, s["tp"], k4)

            omm(0); omm(1); tp(0); omm(2); tp(1); omm(3); tp(2); tp(3)
        ctx.sec("tensor", p_te)

        def p_ve(v, g=g, tilesched=tilesched, px1=dict(prev["x1st"])):
            for k4 in range(4):
                tt = g * 4 + k4
                s = tilesched[k4]
                v.wait_ge((ctx.la0, ctx.la1)[k4 % 2].h, s["lda"])
                v.tensor_add(out=tmp[:], in0=xr[:, (k4 % 2) * D:(k4 % 2 + 1) * D],
                             in1=bObc[:])
                if k4 >= 2:
                    v.wait_ge((ctx.sx0, ctx.sx1)[k4 % 2].h,
                              tilesched[k4 - 2]["x1st"])
                elif px1[k4 % 2]:
                    v.wait_ge((ctx.sx0, ctx.sx1)[k4 % 2].h, px1[k4 % 2])
                v.wait_ge(ctx.pe.h, s["omm"])
                got = ctx.dv.inc(v.tensor_add(
                    out=r1[:, (k4 % 2) * D:(k4 % 2 + 1) * D],
                    in0=pO[:, 0:1024], in1=tmp[:]))
                assert got == s["r1ev"]
                rs = r1[:, (k4 % 2) * D:(k4 % 2 + 1) * D]
                v.reduce_sum(out=lnt[:, 0:1], in_=rs, axis=AX.X)
                v.tensor_scalar_mul(out=lnt[:, 0:1], in0=lnt[:, 0:1], scalar1=1.0 / D)
                got = ctx.dv.inc(v.tensor_scalar_sub(out=rs, in0=rs,
                                                     scalar1=lnt[:, 0:1]))
                assert got == s["dev"]
                v.scalar_tensor_tensor(out=tmp[:], in0=rs, scalar=1.0, in1=rs,
                                       op0=ALU.mult, op1=ALU.mult,
                                       accum_out=lnt[:, 1:2])
                v.tensor_scalar_mul(out=vars_[:, tt:tt + 1], in0=lnt[:, 1:2],
                                    scalar1=1.0 / D)
                v.reduce_max(out=lnt[:, 2:3], in_=rs, axis=AX.X,
                             apply_absolute_value=True)
                v.tensor_scalar_max(out=lnt[:, 2:3], in0=lnt[:, 2:3], scalar1=1e-30)
                v.tensor_scalar_mul(out=mdq[:, tt:tt + 1], in0=lnt[:, 2:3],
                                    scalar1=1.0 / 127.0)
                v.reciprocal(out=lnt[:, 3:4], in_=lnt[:, 2:3])
                v.tensor_scalar_mul(out=lnt[:, 3:4], in0=lnt[:, 3:4], scalar1=127.0)
                v.tensor_scalar(out=tmp[:], in0=rs, scalar1=lnt[:, 3:4],
                                scalar2=MAGIC, op0=ALU.mult, op1=ALU.add)
                v.tensor_single_scalar(out=tmp[:], in_=tmp[:], scalar=MAGIC,
                                       op=ALU.subtract)
                v.tensor_copy(out=tmp.bitcast(I32)[:], in_=tmp.bitcast(I32)[:])
                got = ctx.dv.inc(v.tensor_copy(
                    out=xqb[:, (k4 % 2) * D:(k4 % 2 + 1) * D], in_=tmp[:]))
                assert got == s["q"]
                v.wait_ge(ctx.pe.h, s["tp"])
                xqr = xqTg.ap().rearrange("p (dc t) -> p dc t", dc=NDC)
                pxr_ = pXQ.ap().rearrange("p (b dc t) -> p b dc t", b=2, dc=NDC)
                got = ctx.dv.inc(v.tensor_copy(
                    out=xqr[:, :, k4 * 128:(k4 + 1) * 128], in_=pxr_[:, s["par"]]))
                assert got == s["xqev"]
        ctx.sec("vector", p_ve)

        def p_st(sy, g=g, tilesched=tilesched):
            for k4 in range(4):
                tt = g * 4 + k4
                s = tilesched[k4]
                sy.wait_ge(ctx.dv.h, s["q"])
                sy.dma_start(
                    out=x1_d[tt * 128:(tt + 1) * 128, :],
                    in_=r1[:, (k4 % 2) * D:(k4 % 2 + 1) * D]
                ).then_inc((ctx.sx0, ctx.sx1)[k4 % 2].h, 16)
        ctx.sec("sync", p_st)

        for k4 in range(4):
            pxq_free[tilesched[k4]["par"]] = tilesched[k4]["xqev"]
        prev["r1ev"][0] = tilesched[3]["r1ev"]
        prev["xrfree"][0] = tilesched[2]["dev"]
        prev["xrfree"][1] = tilesched[3]["dev"]
        prev["x1st"][0] = tilesched[2]["x1st"]
        prev["x1st"][1] = tilesched[3]["x1st"]

        msq = ctx.marks()
        sq_sc = {k4: msq["sc"] + k4 + 1 for k4 in range(4)}

        def p_sq(s, g=g, tilesched=tilesched):
            for k4 in range(4):
                tt = g * 4 + k4
                s.wait_ge(ctx.dv.h, tilesched[k4]["q"])
                a = s.activation(out=lnt[:, 4 + k4:5 + k4], in_=vars_[:, tt:tt + 1],
                                 func=AF.Sqrt, bias=epsln[:])
                inst = a
                inst.then_inc(ctx.sc.h, 1)
                ctx._note_sync(inst, ctx.sc.h, sq_sc[k4])
            ctx.sc.n = sq_sc[3]
        ctx.sec("scalar", p_sq)

        def p_dq(v, g=g, tilesched=tilesched):
            for k4 in range(4):
                tt = g * 4 + k4
                v.wait_ge(ctx.sc.h, sq_sc[k4])
                v.reciprocal(out=rstd_a[:, tt:tt + 1], in_=lnt[:, 4 + k4:5 + k4])
                v.tensor_mul(out=lnt[:, 0:1], in0=mdq[:, tt:tt + 1],
                             in1=rstd_a[:, tt:tt + 1])
                v.tensor_mul(out=dqg[:, tt:tt + 1], in0=lnt[:, 0:1], in1=wsc128[:, 0:1])
                ctx.dv.inc(v.tensor_mul(out=dqu[:, tt:tt + 1], in0=lnt[:, 0:1],
                                        in1=wsc128[:, 1:2]))
        ctx.sec("vector", p_dq)
        dq_done = ctx.dv.n

        mg = ctx.marks()
        gu_mm, gu_sc, gu_ev = {}, {}, {}
        pe_n, sc_n, dv_n = mg["pe"], mg["sc"], mg["dv"]
        for is_ in range(8):
            for k4 in range(4):
                pe_n += 2
                gu_mm[(is_, k4)] = pe_n
                sc_n += 1
                gu_sc[(is_, k4)] = sc_n
                dv_n += 2
                gu_ev[(is_, k4)] = dv_n
        ctx.sc.n, ctx.dv.n = sc_n, dv_n
        hst_mark = {}
        st_n = mg["st"]
        for i in range(32):
            st_n += 16
            hst_mark[i] = st_n
        ctx.st.n = st_n
        h_all_stored = st_n
        lw_base = mg["lw"]
        ctx.lw.n += 16 * 16

        def gu_sy(sy, g=g, mg=mg):
            for is_ in range(8):
                if is_ >= 2:
                    sy.wait_ge(ctx.pe.h, gu_mm[(is_ - 2, 3)])
                b = is_ % 2
                if g == 0 and is_ == 0:
                    sy.wait_ge(ctx.sg.h, tern_st)
                sy.dma_start(
                    out=wgs[:, b * 4096:(b + 1) * 4096].rearrange(
                        "p (ic o) -> p ic o", ic=NDC),
                    in_=tg_d[:, is_ * 512:(is_ + 1) * 512].rearrange(
                        "(ic p) o -> p ic o", p=128)).then_inc(ctx.lw.h, 16)
                sy.dma_start(
                    out=wus[:, b * 4096:(b + 1) * 4096].rearrange(
                        "p (ic o) -> p ic o", ic=NDC),
                    in_=tu_d[:, is_ * 512:(is_ + 1) * 512].rearrange(
                        "(ic p) o -> p ic o", p=128)).then_inc(ctx.lw.h, 16)
                if is_ >= 1:
                    for k4 in range(4):
                        i = (is_ - 1) * 4 + k4
                        sy.wait_ge(ctx.dv.h, gu_ev[(is_ - 1, k4)])
                        sy.dma_start(
                            out=h_d[(g * 4 + k4) * 128:(g * 4 + k4 + 1) * 128,
                                    (is_ - 1) * 512:is_ * 512],
                            in_=hst[:, (i % 2) * 512:(i % 2 + 1) * 512]
                            ).then_inc(ctx.st.h, 16)
            for k4 in range(4):
                i = 7 * 4 + k4
                sy.wait_ge(ctx.dv.h, gu_ev[(7, k4)])
                sy.dma_start(
                    out=h_d[(g * 4 + k4) * 128:(g * 4 + k4 + 1) * 128, 7 * 512:8 * 512],
                    in_=hst[:, (i % 2) * 512:(i % 2 + 1) * 512]).then_inc(ctx.st.h, 16)
        ctx.sec("sync", gu_sy)

        def gu_te(t, g=g, mg=mg, lw_base=lw_base):
            for is_ in range(8):
                b = is_ % 2
                t.wait_ge(ctx.lw.h, lw_base + 32 * min(is_ + 2, 8))
                for k4 in range(4):
                    i = is_ * 4 + k4
                    if i >= 2:
                        pr2 = (is_, k4 - 2) if k4 >= 2 else (is_ - 1, k4 + 2)
                        t.wait_ge(ctx.dv.h, gu_ev[pr2])
                    else:
                        t.wait_ge(ctx.dv.h, mg["dv"])
                    sG = (i % 2) * 512
                    sU = (2 + i % 2) * 512
                    for ic in range(NDC):
                        t.matmul(out=pGUD[:, sG:sG + 512],
                                 lhsT=xqTg[:, ic * 512 + k4 * 128:ic * 512 + (k4 + 1) * 128],
                                 rhs=wgs[:, b * 4096 + ic * 512:b * 4096 + (ic + 1) * 512],
                                 start=(ic == 0), stop=(ic == NDC - 1))
                        mm = t.matmul(out=pGUD[:, sU:sU + 512],
                                      lhsT=xqTg[:, ic * 512 + k4 * 128:ic * 512 + (k4 + 1) * 128],
                                      rhs=wus[:, b * 4096 + ic * 512:b * 4096 + (ic + 1) * 512],
                                      start=(ic == 0), stop=(ic == NDC - 1))
                    got = ctx.pe.inc(mm, 2)
                    assert got == gu_mm[(is_, k4)]
        ctx.sec("tensor", gu_te)

        def gu_scf(s, g=g):
            s.wait_ge(ctx.dv.h, dq_done)
            for is_ in range(8):
                for k4 in range(4):
                    tt = g * 4 + k4
                    i = is_ * 4 + k4
                    s.wait_ge(ctx.pe.h, gu_mm[(is_, k4)])
                    if (is_, k4) != (0, 0):
                        prevk = (is_, k4 - 1) if k4 > 0 else (is_ - 1, 3)
                        s.wait_ge(ctx.dv.h, gu_ev[prevk])
                    a = s.activation(out=gss[:], in_=pGUD[:, (i % 2) * 512:(i % 2 + 1) * 512],
                                     func=AF.Silu, scale=dqg[:, tt:tt + 1])
                    inst = a
                    inst.then_inc(ctx.sc.h, 1)
                    ctx._note_sync(inst, ctx.sc.h, gu_sc[(is_, k4)])
        ctx.sec("scalar", gu_scf)

        def gu_ve(v, g=g):
            for is_ in range(8):
                for k4 in range(4):
                    tt = g * 4 + k4
                    i = is_ * 4 + k4
                    v.wait_ge(ctx.pe.h, gu_mm[(is_, k4)])
                    v.tensor_scalar_mul(out=uss[:], in0=pGUD[:, (2 + i % 2) * 512:(3 + i % 2) * 512],
                                        scalar1=dqu[:, tt:tt + 1])
                    v.wait_ge(ctx.sc.h, gu_sc[(is_, k4)])
                    if i >= 2:
                        v.wait_ge(ctx.st.h, hst_mark[i - 1])
                    ev = v.tensor_mul(out=hst[:, (i % 2) * 512:(i % 2 + 1) * 512],
                                      in0=gss[:], in1=uss[:])
                    ev.then_inc(ctx.dv.h, 2)
                    ctx._note_sync(ev, ctx.dv.h, gu_ev[(is_, k4)])
        ctx.sec("vector", gu_ve)

        for pr_ in range(2):
            tts = [g * 4 + pr_ * 2, g * 4 + pr_ * 2 + 1]
            hq_dv = {}
            for tk, tt in enumerate(tts):
                mh = ctx.marks()
                hld = mh["ld"] + 16
                ctx.ld.n += 16

                def h_ld(sy, tt=tt, mh=mh, pp=prev["tp_pe"]):
                    sy.wait_ge(ctx.dv.h, mh["dv"])
                    if pp:
                        sy.wait_ge(ctx.pe.h, pp)
                    sy.wait_ge(ctx.st.h, h_all_stored)
                    sy.dma_start(out=hrl[:], in_=h_d[tt * 128:(tt + 1) * 128, :]
                                 ).then_inc(ctx.ld.h, 16)
                ctx.sec("sync", h_ld)

                def h_q(v, tt=tt, tk=tk, hld=hld):
                    v.wait_ge(ctx.ld.h, hld)
                    v.reduce_max(out=lnt[:, 0:1], in_=hrl[:], axis=AX.X,
                                 apply_absolute_value=True)
                    v.tensor_scalar_max(out=lnt[:, 0:1], in0=lnt[:, 0:1], scalar1=1e-5)
                    v.tensor_scalar_mul(out=lnt[:, 1:2], in0=lnt[:, 0:1],
                                        scalar1=1.0 / 127.0)
                    v.tensor_mul(out=dqd2[:, tt:tt + 1], in0=lnt[:, 1:2],
                                 in1=wsc128[:, 2:3])
                    v.reciprocal(out=lnt[:, 2:3], in_=lnt[:, 0:1])
                    v.tensor_scalar_mul(out=lnt[:, 2:3], in0=lnt[:, 2:3], scalar1=127.0)
                    v.tensor_scalar(out=hrl[:], in0=hrl[:], scalar1=lnt[:, 2:3],
                                    scalar2=MAGIC, op0=ALU.mult, op1=ALU.add)
                    v.tensor_single_scalar(out=hrl[:], in_=hrl[:], scalar=MAGIC,
                                           op=ALU.subtract)
                    v.tensor_copy(out=hrl.bitcast(I32)[:], in_=hrl.bitcast(I32)[:])
                    hq_dv[tk] = ctx.dv.inc(v.tensor_copy(
                        out=hqb[:, tk * INNER:(tk + 1) * INNER], in_=hrl[:]))
                ctx.sec("vector", h_q)

            mp = ctx.marks()
            dn_lw = {icg: mp["lw"] + 16 * (icg + 1) for icg in range(4)}
            ctx.lw.n += 64
            dn_tp, dn_mm, dn_hq = {}, {}, {}
            pe_n = mp["pe"]
            dv_n = mp["dv"]
            for icg in range(4):
                for tk in range(2):
                    pe_n += 1
                    dn_tp[(icg, tk)] = pe_n
                    dv_n += 1
                    dn_hq[(icg, tk)] = dv_n
                    pe_n += 1
                    dn_mm[(icg, tk)] = pe_n
            ctx.dv.n = dv_n

            def dn_sy(sy, mp=mp, wds_free=prev["mm_pe"]):
                for icg in range(4):
                    if icg >= 1:
                        sy.wait_ge(ctx.pe.h, dn_mm[(icg - 1, 1)])
                    else:
                        sy.wait_ge(ctx.st.h, h_all_stored)
                        if wds_free:
                            sy.wait_ge(ctx.pe.h, wds_free)
                    sy.dma_start(
                        out=wds.ap().rearrange("p (ic o) -> p ic o", ic=8),
                        in_=td_d[icg * 1024:(icg + 1) * 1024, :].rearrange(
                            "(ic p) o -> p ic o", p=128)).then_inc(ctx.lw.h, 16)
            ctx.sec("sync", dn_sy)

            def dn_te(t, mp=mp, ppx=dict(pxq_free)):
                for icg in range(4):
                    t.wait_ge(ctx.lw.h, dn_lw[icg])
                    for tk in range(2):
                        if (icg, tk) == (0, 0):
                            t.wait_ge(ctx.dv.h, hq_dv[1])
                        j = icg * 2 + tk
                        parj = j % 2
                        if j >= 2:
                            pj = j - 2
                            t.wait_ge(ctx.dv.h, dn_hq[(pj // 2, pj % 2)])
                        elif ppx[parj]:
                            t.wait_ge(ctx.dv.h, ppx[parj])
                            ppx[parj] = 0
                        for i8 in range(8):
                            ic = icg * 8 + i8
                            mm = t.transpose(
                                out=pXQ[:, parj * 1024 + i8 * 128:parj * 1024 + (i8 + 1) * 128],
                                in_=hqb[:, tk * INNER + ic * 128:tk * INNER + (ic + 1) * 128],
                                identity=identb[:])
                        got = ctx.pe.inc(mm)
                        assert got == dn_tp[(icg, tk)]
                        t.wait_ge(ctx.dv.h, dn_hq[(icg, tk)])
                        for i8 in range(8):
                            for os in range(2):
                                mm = t.matmul(
                                    out=pGUD[:, (tk * 2 + os) * 512:(tk * 2 + os + 1) * 512],
                                    lhsT=hqT8[:, tk * 1024 + i8 * 128:tk * 1024 + (i8 + 1) * 128],
                                    rhs=wds[:, i8 * D + os * 512:i8 * D + (os + 1) * 512],
                                    start=(icg == 0 and i8 == 0),
                                    stop=(icg == 3 and i8 == 7))
                        got = ctx.pe.inc(mm)
                        assert got == dn_mm[(icg, tk)]
            ctx.sec("tensor", dn_te)

            def dn_ve(v, mp=mp):
                for icg in range(4):
                    for tk in range(2):
                        v.wait_ge(ctx.pe.h, dn_tp[(icg, tk)])
                        parj = (icg * 2 + tk) % 2
                        ev = v.tensor_copy(out=hqT8[:, tk * 1024:(tk + 1) * 1024],
                                           in_=pXQ[:, parj * 1024:(parj + 1) * 1024])
                        ev.then_inc(ctx.dv.h, 1)
                        ctx._note_sync(ev, ctx.dv.h, dn_hq[(icg, tk)])
            ctx.sec("vector", dn_ve)
            prev["tp_pe"] = dn_tp[(3, 1)]
            prev["mm_pe"] = dn_mm[(3, 1)]
            pxq_free[0] = dn_hq[(3, 0)]
            pxq_free[1] = dn_hq[(3, 1)]

            me = ctx.marks()
            x1_ld = {tk: me[("la0", "la1")[tk]] + 16 for tk in range(2)}
            ctx.la0.n += 16
            ctx.la1.n += 16
            ev_dv = {tk: me["dv"] + 3 * tk + 3 for tk in range(2)}
            ctx.dv.n += 6
            sq2 = {tk: me["sc"] + tk + 1 for tk in range(2)}
            ctx.sc.n += 2
            y_dv = {tk: ctx.dv.n + tk + 1 for tk in range(2)}
            ctx.dv.n += 2
            y_st = {tk: ctx.sy.n + 16 * (tk + 1) for tk in range(2)}
            ctx.sy.n += 32

            def e_ld(sy, tts=tts, me=me):
                sy.wait_ge(ctx.sx0.h, me["sx0"])
                sy.wait_ge(ctx.sx1.h, me["sx1"])
                for tk, tt in enumerate(tts):
                    sy.wait_ge(ctx.dv.h, me["dv"])
                    sy.dma_start(out=x1r[:, tk * D:(tk + 1) * D],
                                 in_=x1_d[tt * 128:(tt + 1) * 128, :]
                                 ).then_inc((ctx.la0, ctx.la1)[tk].h, 16)
            ctx.sec("sync", e_ld)

            def e_ev(v, tts=tts, me=me, pr_=pr_):
                for tk, tt in enumerate(tts):
                    v.wait_ge((ctx.la0, ctx.la1)[tk].h, x1_ld[tk])
                    v.wait_ge(ctx.pe.h, dn_mm[(3, 1)])
                    v.tensor_scalar_mul(out=x1r[:, tk * D:(tk + 1) * D],
                                        in0=x1r[:, tk * D:(tk + 1) * D],
                                        scalar1=rstd_a[:, tt:tt + 1])
                    rs2 = r1[:, tk * D:(tk + 1) * D]
                    for os in range(2):
                        v.scalar_tensor_tensor(
                            out=r1[:, tk * D + os * 512:tk * D + (os + 1) * 512],
                            in0=pGUD[:, (tk * 2 + os) * 512:(tk * 2 + os + 1) * 512],
                            scalar=dqd2[:, tt:tt + 1],
                            in1=x1r[:, tk * D + os * 512:tk * D + (os + 1) * 512],
                            op0=ALU.mult, op1=ALU.add)
                    v.reduce_sum(out=lnt[:, 0:1], in_=rs2, axis=AX.X)
                    v.tensor_scalar_mul(out=lnt[:, 0:1], in0=lnt[:, 0:1], scalar1=1.0 / D)
                    i1 = v.tensor_scalar_sub(out=rs2, in0=rs2, scalar1=lnt[:, 0:1])
                    i1.then_inc(ctx.dv.h, 1)
                    ctx._note_sync(i1, ctx.dv.h, me["dv"] + 3 * tk + 1)
                    v.scalar_tensor_tensor(out=tmp[:], in0=rs2, scalar=1.0, in1=rs2,
                                           op0=ALU.mult, op1=ALU.mult,
                                           accum_out=lnt[:, 1:2])
                    i2 = v.tensor_scalar_mul(out=lnt[:, 4 + tk:5 + tk],
                                             in0=lnt[:, 1:2], scalar1=1.0 / D)
                    i2.then_inc(ctx.dv.h, 2)
                    ctx._note_sync(i2, ctx.dv.h, ev_dv[tk])
            ctx.sec("vector", e_ev)

            def e_sq(s, me=me):
                for tk in range(2):
                    s.wait_ge(ctx.dv.h, ev_dv[tk])
                    a = s.activation(out=lnt[:, 2 + tk:3 + tk], in_=lnt[:, 4 + tk:5 + tk],
                                     func=AF.Sqrt, bias=epsln[:])
                    a.then_inc(ctx.sc.h, 1)
                    ctx._note_sync(a, ctx.sc.h, sq2[tk])
            ctx.sec("scalar", e_sq)

            def e_out(v, tts=tts, pys=dict(prev["y_st"])):
                for tk, tt in enumerate(tts):
                    v.wait_ge(ctx.sc.h, sq2[tk])
                    if pys[tk]:
                        v.wait_ge(ctx.sy.h, pys[tk])
                    v.reciprocal(out=lnt[:, 0:1], in_=lnt[:, 2 + tk:3 + tk])
                    ev = v.tensor_scalar_mul(out=yt[:, tk * D:(tk + 1) * D],
                                             in0=r1[:, tk * D:(tk + 1) * D],
                                             scalar1=lnt[:, 0:1])
                    ev.then_inc(ctx.dv.h, 1)
                    ctx._note_sync(ev, ctx.dv.h, y_dv[tk])
            ctx.sec("vector", e_out)

            def e_st(sy, tts=tts):
                for tk, tt in enumerate(tts):
                    sy.wait_ge(ctx.dv.h, y_dv[tk])
                    sy.dma_start(out=y_d[tt * 128:(tt + 1) * 128, :],
                                 in_=yt[:, tk * D:(tk + 1) * D]).then_inc(ctx.sy.h, 16)
            ctx.sec("sync", e_st)
            prev["y_st"][0] = y_st[1]
            prev["y_st"][1] = y_st[1]
        prev["dev"] = tilesched[3]["dev"]

    if "x1" in taps:
        def fx(sy):
            m = ctx.marks()
            ctx.wait_marks(sy, m, ("pe", "dv", "sc", "gp", "ld", "lw",
                                   "la0", "la1", "sx0", "sx1", "sy"))
            sy.wait_ge(ctx.st.h, m["st"])
            ctx.st.inc(sy.dma_start(out=tap_d["x1"], in_=x1_d), 16)
        ctx.sec("sync", fx)
    if "hsc" in taps:
        def fh(sy):
            m = ctx.marks()
            ctx.wait_marks(sy, m, ("pe", "dv", "sc", "gp", "ld", "lw",
                                   "la0", "la1", "sx0", "sx1", "sy"))
            sy.wait_ge(ctx.st.h, m["st"])
            ctx.st.inc(sy.dma_start(out=tap_d["hsc"], in_=h_d), 16)
        ctx.sec("sync", fh)


def make_inputs(L0_attn_x, L0_attn_qW, L0_attn_qb, L0_attn_kW, L0_attn_kb,
                L0_attn_vW, L0_attn_vb, L0_attn_oW, L0_attn_ob, L0_attn_J,
                L0_ffn_w_gate, L0_ffn_w_up, L0_ffn_w_down):
    import ml_dtypes
    common = {
        "qW": np.ascontiguousarray(np.asarray(L0_attn_qW, dtype=np.float32)),
        "kW": np.ascontiguousarray(np.asarray(L0_attn_kW, dtype=np.float32)),
        "vW": np.ascontiguousarray(np.asarray(L0_attn_vW, dtype=np.float32)),
        "oW": np.ascontiguousarray(np.asarray(L0_attn_oW, dtype=np.float32)),
        "qb": np.ascontiguousarray(np.asarray(L0_attn_qb, dtype=np.float32)),
        "kb": np.ascontiguousarray(np.asarray(L0_attn_kb, dtype=np.float32)),
        "vb": np.ascontiguousarray(np.asarray(L0_attn_vb, dtype=np.float32)),
        "ob": np.ascontiguousarray(np.asarray(L0_attn_ob, dtype=np.float32)),
        "J": np.ascontiguousarray(np.asarray(L0_attn_J, dtype=np.float32)),
        "wgT": np.ascontiguousarray(np.asarray(L0_ffn_w_gate, dtype=np.float32).T),
        "wuT": np.ascontiguousarray(np.asarray(L0_ffn_w_up, dtype=np.float32).T),
        "wdT": np.ascontiguousarray(np.asarray(L0_ffn_w_down, dtype=np.float32).T),
        "idb": np.eye(128, dtype=ml_dtypes.bfloat16),
        "idf": np.eye(128, dtype=np.float32),
    }
    xs = np.asarray(L0_attn_x, dtype=np.float32)
    return [dict(common, x=np.ascontiguousarray(xs[i])) for i in range(xs.shape[0])]


def kernel(L0_attn_x, L0_attn_qW, L0_attn_qb, L0_attn_kW, L0_attn_kb,
           L0_attn_vW, L0_attn_vb, L0_attn_oW, L0_attn_ob, L0_attn_J,
           L0_ffn_w_gate, L0_ffn_w_up, L0_ffn_w_down):
    nc = build()
    in_maps = make_inputs(L0_attn_x, L0_attn_qW, L0_attn_qb, L0_attn_kW,
                          L0_attn_kb, L0_attn_vW, L0_attn_vb, L0_attn_oW,
                          L0_attn_ob, L0_attn_J, L0_ffn_w_gate, L0_ffn_w_up,
                          L0_ffn_w_down)
    res = run_bass_kernel_spmd(nc, in_maps, core_ids=list(range(len(in_maps))))
    return np.stack([r["y"] for r in res.results], axis=0)


# version 14
# speedup vs baseline: 1.0363x; 1.0363x over previous
import numpy as np
from contextlib import ExitStack
import concourse.bass as bass
import concourse.mybir as mybir
from concourse.bass_utils import run_bass_kernel_spmd

F32 = mybir.dt.float32
BF16 = mybir.dt.bfloat16
I32 = mybir.dt.int32
AF = mybir.ActivationFunctionType
ALU = mybir.AluOpType
AX = mybir.AxisListType

T = 2048
D = 1024
H = 16
INNER = 4096
NTT = T // 128
NDC = D // 128
NIC = INNER // 128
MAGIC = 12582912.0
EPS_LN = 1e-5
EPS_QN = 1e-6
NELEM_W = D * INNER

SIGN_TABLE = {
    0: [(0, 1.0), (1, -1.0), (2, -1.0), (3, -1.0)],
    1: [(1, 1.0), (0, 1.0), (3, -1.0), (2, 1.0)],
    2: [(2, 1.0), (3, 1.0), (0, 1.0), (1, -1.0)],
    3: [(3, 1.0), (2, -1.0), (1, 1.0), (0, 1.0)],
}


class Ctr:
    def __init__(self, stack, nc, name, ctx=None):
        self.h = stack.enter_context(nc.semaphore(name))
        self.n = 0
        self.ctx = ctx

    def inc(self, inst, k=1):
        inst.then_inc(self.h, k)
        self.n += k
        if self.ctx is not None:
            self.ctx._note_inc(inst, self)
        return self.n


_SERIAL_OPS = frozenset((
    "tensor_copy", "tensor_tensor", "tensor_add", "tensor_sub", "tensor_mul",
    "tensor_max", "tensor_scalar", "tensor_single_scalar", "tensor_scalar_add",
    "tensor_scalar_sub", "tensor_scalar_mul", "tensor_scalar_max",
    "tensor_scalar_min", "tensor_relu", "tensor_reduce", "reduce_max",
    "reduce_sum", "scalar_tensor_tensor", "tensor_tensor_scan", "reciprocal",
    "memset", "memzero", "activation", "sign", "square", "sqrt", "mul",
    "copy", "add", "select", "copy_predicated", "transpose", "iota",
    "partition_broadcast", "partition_all_reduce", "engine_nop", "pool",
    "pool_avg", "pool_max", "bn_stats", "bn_aggr", "max", "stream_shuffle",
))


class SerialEng:
    def __init__(self, ctr):
        self._eng = None
        self._ctr = ctr
        self._pend = None
        self._sync = None

    def _rebind(self, eng):
        self._eng = eng
        return self

    def _user_inc(self, inst, ctr):
        if self._pend is inst:
            self._pend = None
            self._sync = (ctr.h, ctr.n)

    def _fence(self):
        if self._pend is not None:
            v = self._ctr.inc(self._pend)
            self._sync = (self._ctr.h, v)
            self._pend = None
        if self._sync is not None:
            self._eng.wait_ge(*self._sync)
            self._sync = None

    def __getattr__(self, name):
        f = getattr(self._eng, name)
        if name not in _SERIAL_OPS:
            return f

        def g(*a, **k):
            self._fence()
            inst = f(*a, **k)
            self._pend = inst
            return inst
        return g


class Ctx:
    def __init__(self, nc, block, stack):
        self.nc = nc
        self.block = block
        self.stack = stack
        for c in ("pe", "dv", "sc", "gp", "ld", "lw", "st",
                  "hdr", "fb", "qw", "lx0", "lx1",
                  "rs", "rl0", "rl1", "la0", "la1",
                  "sx0", "sx1", "sy",
                  "wa", "wv", "wp", "lg", "sg", "w0", "w1", "wr",
                  "dvs", "scs", "gps"):
            setattr(self, c, Ctr(stack, nc, "s_" + c, ctx=self))
        self._proxies = {"vector": SerialEng(self.dvs),
                         "scalar": SerialEng(self.scs),
                         "gpsimd": SerialEng(self.gps)}

    def _note_inc(self, inst, ctr):
        for p in self._proxies.values():
            p._user_inc(inst, ctr)

    def _note_sync(self, inst, h, v):
        for p in self._proxies.values():
            if p._pend is inst:
                p._pend = None
                p._sync = (h, v)

    def sec(self, eng, fn):
        if eng in self._proxies:
            getattr(self.block, eng)(lambda e: fn(self._proxies[eng]._rebind(e)))
        else:
            getattr(self.block, eng)(fn)

    def marks(self):
        return {c: getattr(self, c).n
                for c in ("pe", "dv", "sc", "gp", "ld", "lw", "st", "hdr", "fb",
                          "qw", "lx0", "lx1", "rs", "rl0", "rl1", "la0", "la1",
                          "sx0", "sx1", "sy", "wa", "wv", "wp", "lg", "sg",
                          "w0", "w1", "wr")}

    def wait_marks(self, eng, m, which=("pe", "dv", "sc", "gp")):
        for w in which:
            if m[w] > 0:
                eng.wait_ge(getattr(self, w).h, m[w])

    def barrier(self, engines=("sync", "vector", "scalar", "tensor", "gpsimd"),
                on=("pe", "dv", "sc", "gp", "ld", "lw", "hdr", "fb", "qw",
                    "lx0", "lx1", "rs", "wa", "wv", "wp")):
        m = self.marks()
        for e in engines:
            self.sec(e, lambda eng, m=m: self.wait_marks(eng, m, on))


def sb(ctx, name, shape, dt, stack=None):
    return (stack or ctx.stack).enter_context(ctx.nc.sbuf_tensor(name, shape, dt))


def psum(ctx, name, shape, dt=F32, stack=None):
    return (stack or ctx.stack).enter_context(ctx.nc.psum_tensor(name, shape, dt))


def _tap(ctx, dst, src_ap):
    def f(sy):
        m = ctx.marks()
        ctx.wait_marks(sy, m, ("pe", "dv", "sc", "gp", "ld", "lw", "hdr",
                               "fb", "qw", "lx0", "lx1", "rs", "rl0", "rl1",
                               "la0", "la1", "sx0", "sx1", "sy",
                               "wa", "wv", "wp"))
        ctx.st.inc(sy.dma_start(out=dst, in_=src_ap), 16)
    ctx.sec("sync", f)


def build(n_heads=H, taps=(), phase_stop=None):
    nc = bass.Bass("TRN2", target_bir_lowering=False, detect_race_conditions=False)
    taps = set(taps)
    E = {}

    for nm, shape in (("x", [T, D]), ("qW", [4, 256, 256]), ("kW", [4, 256, 256]),
                      ("vW", [4, 256, 256]), ("oW", [4, 256, 256]),
                      ("qb", [4, 256]), ("kb", [4, 256]), ("vb", [4, 256]),
                      ("ob", [4, 256]), ("J", [D]), ("wgT", [D, INNER]),
                      ("wuT", [D, INNER]), ("wdT", [INNER, D])):
        E[nm + "_d"] = nc.dram_tensor(nm, shape, F32, kind="ExternalInput").ap()
    E["idb_d"] = nc.dram_tensor("idb", [128, 128], BF16, kind="ExternalInput").ap()
    E["idf_d"] = nc.dram_tensor("idf", [128, 128], F32, kind="ExternalInput").ap()
    E["y_d"] = nc.dram_tensor("y", [T, D], F32, kind="ExternalOutput").ap()

    E["fbs_d"] = nc.dram_tensor("fbs", [4, D], F32, kind="Internal").ap()
    E["tg_d"] = nc.dram_tensor("tern_g", [D, INNER], BF16, kind="Internal").ap()
    E["tu_d"] = nc.dram_tensor("tern_u", [D, INNER], BF16, kind="Internal").ap()
    E["td_d"] = nc.dram_tensor("tern_d", [INNER, D], BF16, kind="Internal").ap()
    E["h_d"] = nc.dram_tensor("h_sc", [T, INNER], F32, kind="Internal").ap()
    E["rscr_d"] = nc.dram_tensor("rscr", [64, 512], F32, kind="Internal").ap()
    E["x1_d"] = nc.dram_tensor("x1_sc", [T, D], F32, kind="Internal").ap()
    E["wsum_d"] = nc.dram_tensor("wsum", [3, 128], F32, kind="Internal").ap()
    E["sc_d"] = nc.dram_tensor("wsc_sc", [3, 2], F32, kind="Internal").ap()

    tap_shapes = {
        "xT": ([128, NDC * T], BF16), "qT": ([128, NDC * T], BF16),
        "kT": ([128, NDC * T], BF16), "Vaug": ([128, NTT * 1040], BF16),
        "attnoutT": ([128, NDC * T], BF16), "x1": ([T, D], F32),
        "PT0": ([128, 8 * T], BF16), "cb": ([128, 1], F32),
        "fbT": ([128, 32], F32), "WTo": ([128, NDC * D], BF16),
        "terng": ([D, INNER], BF16), "ternd": ([INNER, D], BF16),
        "wsc": ([128, 3], F32), "hsc": ([T, INNER], F32),
        "wacc": ([128, 2], F32), "wrow": ([1, 128], F32), "ws11": ([1, 2], F32),
    }
    E["tap_d"] = {name: nc.dram_tensor("tap_" + name, *tap_shapes[name],
                                       kind="ExternalOutput").ap() for name in taps}
    E["taps"] = taps

    with ExitStack() as st0:
        block = st0.enter_context(nc.Block())
        ctx = Ctx(nc, block, st0)

        for nm, shape, dt in (
                ("ones128", [128, 128], F32), ("identb", [128, 128], BF16),
                ("identf", [128, 128], F32), ("cb128", [128, 1], F32),
                ("epsqn", [128, 1], F32), ("epsln", [128, 1], F32),
                ("fbT", [128, 32], F32), ("fbrows", [1, 2 * D], F32),
                ("c11", [1, 1], F32), ("wsc128", [128, 3], F32),
                ("swq128", [128, 3], F32),
                ("WT_o", [128, NDC * D], BF16),
                ("xT", [128, NDC * T], BF16)):
            E[nm] = sb(ctx, nm, shape, dt)
        E["attnoutT"] = E["xT"]

        stA = st0.enter_context(ExitStack())
        E["qT"] = sb(ctx, "qT", [128, NDC * T], BF16, stA)
        E["kT"] = sb(ctx, "kT", [128, NDC * T], BF16, stA)
        E["Vaug"] = sb(ctx, "Vaug", [128, NTT * 1040], BF16, stA)

        stWQ = st0.enter_context(ExitStack())
        CW = 512
        E["wq_bufs"] = dict(
            CW=CW,
            wbuf=sb(ctx, "wq_wbuf", [128, 2 * CW], F32, stWQ),
            t1=sb(ctx, "wq_t1", [128, 2 * CW], F32, stWQ),
            tbuf=sb(ctx, "wq_tbuf", [128, 2 * CW], BF16, stWQ),
            accs=sb(ctx, "wq_acc", [128, 2], F32, stWQ),
            rowx=sb(ctx, "wq_row", [1, 128], F32, stWQ),
            s11=sb(ctx, "wq_s11", [1, 2], F32, stWQ),
        )
        build_p0(ctx, E, st0)
        if phase_stop == "P0":
            finish(ctx); return nc
        stPW = st0.enter_context(ExitStack())
        build_proj(ctx, E, stPW)
        if phase_stop == "PROJ":
            finish(ctx); return nc
        wq_thunks = build_wq(ctx, E, stWQ, stPW)
        if phase_stop == "WQ":
            for e, f in wq_thunks:
                ctx.sec(e, f)
            finish(ctx); return nc
        m = ctx.marks()
        for e in ("vector", "scalar", "tensor", "gpsimd"):
            ctx.sec(e, lambda eng, m=m: ctx.wait_marks(
                eng, m, ("pe", "dv", "sc", "gp")))
        stPW.close()

        stT = st0.enter_context(ExitStack())
        build_attn(ctx, E, stT, n_heads, wq_thunks)
        ctx.barrier(on=("pe", "dv", "sc", "gp", "rs", "rl0", "rl1",
                        "wv", "wa", "wp", "lw", "wr"))
        if phase_stop == "ATTN":
            finish(ctx); return nc
        stT.close()
        stWQ.close()
        stA.close()

        stF = st0.enter_context(ExitStack())
        build_ffn(ctx, E, stF)
        finish(ctx)
    return nc


def finish(ctx):
    def f(sy):
        m = ctx.marks()
        ctx.wait_marks(sy, m, ("pe", "dv", "sc", "gp", "ld", "lw", "hdr", "fb",
                               "qw", "lx0", "lx1", "rs", "rl0", "rl1",
                               "la0", "la1", "sx0", "sx1", "sy",
                               "wa", "wv", "wp", "lg", "sg", "w0", "w1", "wr"))
        sy.wait_ge(ctx.st.h, ctx.st.n)
    ctx.sec("sync", f)


def build_p0(ctx, E, stX):
    taps, tap_d = E["taps"], E["tap_d"]
    ones128, identb, identf = E["ones128"], E["identb"], E["identf"]
    cb128, fbT, fbrows, c11 = E["cb128"], E["fbT"], E["fbrows"], E["c11"]
    Vaug, xT = E["Vaug"], E["xT"]

    with ExitStack() as s0:
        xld = sb(ctx, "xld", [128, 2 * D], F32, s0)
        xbf = sb(ctx, "xbf", [128, 2 * D], BF16, s0)
        fb32 = sb(ctx, "fb32", [32, 128], F32, s0)
        brow = sb(ctx, "brow", [1, D], F32, s0)
        jrow = sb(ctx, "jrow", [1, D], F32, s0)
        braw = sb(ctx, "braw", [1, 4 * D], F32, s0)
        pxt = psum(ctx, "pxt", [128, 2 * D], BF16, s0)
        pfb = psum(ctx, "pfb", [128, 32], F32, s0)
        pbc = psum(ctx, "pbc", [128, 1], F32, s0)

        cast_mark, evacT_mark = {}, {}
        n = ctx.dv.n + 2
        for tt in range(NTT):
            n += 1
            cast_mark[tt] = n
            if tt >= 1:
                n += 1
                evacT_mark[tt - 1] = n
        n += 1
        evacT_mark[NTT - 1] = n
        cb_mark, fbT_mark = n + 1, n + 2
        pe0 = ctx.pe.n
        xpose_mark = {tt: pe0 + tt + 1 for tt in range(NTT)}
        cbc_mark, pfb_mark = pe0 + NTT + 1, pe0 + NTT + 2
        HDRN = 7 * 16

        def xwait(eng, tt):
            lx = ctx.lx0 if tt % 2 == 0 else ctx.lx1
            eng.wait_ge(lx.h, 16 * (tt // 2 + 1))

        def sy1(sy):
            ctx.hdr.inc(sy.dma_start(out=jrow[:], in_=E["J_d"].rearrange("(p n) -> p n", p=1)), 16)
            for P, nm in enumerate(("qb", "kb", "vb", "ob")):
                ctx.hdr.inc(sy.dma_start(out=braw[:, P * D:(P + 1) * D],
                                         in_=E[nm + "_d"].rearrange("(o a) b -> o (a b)", o=1)), 16)
            ctx.hdr.inc(sy.dma_start(out=identb[:], in_=E["idb_d"]), 16)
            ctx.hdr.inc(sy.dma_start(out=identf[:], in_=E["idf_d"]), 16)
            for tt in range(NTT):
                if tt >= 2:
                    sy.wait_ge(ctx.dv.h, cast_mark[tt - 2])
                lx = ctx.lx0 if tt % 2 == 0 else ctx.lx1
                lx.inc(sy.dma_start(out=xld[:, (tt % 2) * D:(tt % 2 + 1) * D],
                                    in_=E["x_d"][tt * 128:(tt + 1) * 128, :]), 16)
        ctx.sec("sync", sy1)

        def sc1(s):
            s.wait_ge(ctx.hdr.h, HDRN)
            ctx.sc.inc(s.activation(out=jrow[:], in_=jrow[:], func=AF.Sin))
        ctx.sec("scalar", sc1)
        sc_sin = ctx.sc.n

        def ve1(v):
            v.memset(ones128[:], 1.0)
            v.memset(E["epsqn"][:], EPS_QN)
            v.memset(E["epsln"][:], EPS_LN)
            v.memset(Vaug[:], 1.0)
            v.wait_ge(ctx.sc.h, sc_sin)
            v.reduce_sum(out=c11[:], in_=jrow[:], axis=AX.X)
            got = ctx.dv.inc(v.tensor_scalar(out=c11[:], in0=c11[:], scalar1=0.1 / D,
                                             scalar2=-8.0, op0=ALU.mult, op1=ALU.add))
            assert got == 1
            v.wait_ge(ctx.hdr.h, HDRN)
            for P in range(4):
                base = P * D
                b = lambda w: braw[:, base + w * 256: base + (w + 1) * 256]
                t01, t01m = brow[:, 0:256], brow[:, 256:512]
                t23, t23m = brow[:, 512:768], brow[:, 768:1024]
                v.tensor_add(out=t01, in0=b(0), in1=b(1))
                v.tensor_sub(out=t01m, in0=b(0), in1=b(1))
                v.tensor_add(out=t23, in0=b(2), in1=b(3))
                v.tensor_sub(out=t23m, in0=b(2), in1=b(3))
                v.tensor_sub(out=b(0), in0=t01m, in1=t23)
                v.tensor_add(out=b(1), in0=t01, in1=t23m)
                v.tensor_add(out=b(2), in0=t01m, in1=t23)
                last = v.tensor_sub(out=b(3), in0=t01, in1=t23m)
            ctx.dv.inc(last)
            v.tensor_copy(out=fbrows[:, 0:D], in_=braw[:, 2 * D:3 * D])
            v.tensor_copy(out=fbrows[:, D:2 * D], in_=braw[:, 3 * D:4 * D])
            xTr = xT.ap().rearrange("p (dc t) -> p dc t", dc=NDC)
            pxtr = pxt.ap().rearrange("p (b dc t) -> p b dc t", b=2, dc=NDC)

            def evac(tt):
                v.wait_ge(ctx.pe.h, xpose_mark[tt])
                got = ctx.dv.inc(v.tensor_copy(out=xTr[:, :, tt * 128:(tt + 1) * 128],
                                               in_=pxtr[:, tt % 2, :, :]))
                assert got == evacT_mark[tt], (got, evacT_mark[tt])
            for tt in range(NTT):
                xwait(v, tt)
                got = ctx.dv.inc(v.tensor_copy(out=xbf[:, (tt % 2) * D:(tt % 2 + 1) * D],
                                               in_=xld[:, (tt % 2) * D:(tt % 2 + 1) * D]))
                assert got == cast_mark[tt], (got, cast_mark[tt])
                if tt >= 1:
                    evac(tt - 1)
            evac(NTT - 1)
            v.wait_ge(ctx.pe.h, cbc_mark)
            got = ctx.dv.inc(v.tensor_copy(out=cb128[:], in_=pbc[:]))
            assert got == cb_mark
            v.wait_ge(ctx.pe.h, pfb_mark)
            got = ctx.dv.inc(v.tensor_copy(out=fbT[:], in_=pfb[:]))
            assert got == fbT_mark
        ctx.sec("vector", ve1)

        def te1(t):
            for tt in range(NTT):
                t.wait_ge(ctx.dv.h, cast_mark[tt])
                if tt >= 2:
                    t.wait_ge(ctx.dv.h, evacT_mark[tt - 2])
                for dc in range(NDC):
                    mm = t.transpose(
                        out=pxt[:, (tt % 2) * D + dc * 128:(tt % 2) * D + (dc + 1) * 128],
                        in_=xbf[:, (tt % 2) * D + dc * 128:(tt % 2) * D + (dc + 1) * 128],
                        identity=identb[:])
                got = ctx.pe.inc(mm)
                assert got == xpose_mark[tt]
            t.wait_ge(ctx.dv.h, 1)
            got = ctx.pe.inc(t.matmul(out=pbc[:], lhsT=ones128[0:1, :], rhs=c11[:],
                                      start=True, stop=True))
            assert got == cbc_mark
            t.wait_ge(ctx.fb.h, 16)
            got = ctx.pe.inc(t.transpose(out=pfb[:], in_=fb32[:], identity=identf[0:32, 0:32]))
            assert got == pfb_mark
        ctx.sec("tensor", te1)

        def sy2(sy):
            sy.wait_ge(ctx.dv.h, 2)
            base = ctx.st.n
            for P in range(4):
                ctx.st.inc(sy.dma_start(out=E["fbs_d"][P:P + 1, :],
                                        in_=braw[:, P * D:(P + 1) * D]), 16)
            sy.wait_ge(ctx.st.h, base + 64)
            ctx.fb.inc(sy.dma_start(out=fb32[:],
                                    in_=E["fbs_d"].rearrange("a (c p) -> (a c) p", p=128)), 16)
        ctx.sec("sync", sy2)

        if "xT" in taps:
            _tap(ctx, tap_d["xT"], xT[:])
        if "cb" in taps:
            _tap(ctx, tap_d["cb"], cb128[:])
        if "fbT" in taps:
            _tap(ctx, tap_d["fbT"], fbT[:])
        ctx.barrier()


def build_wq(ctx, E, s0, sP):
    wsc128, swq128 = E["wsc128"], E["swq128"]

    B = E["wq_bufs"]
    CW = B["CW"]
    wbuf, t1, tbuf = B["wbuf"], B["t1"], B["tbuf"]
    accs, rowx, s11 = B["accs"], B["rowx"], B["s11"]
    wsum_d, sc_d = E["wsum_d"], E["sc_d"]

    mats = []
    for wk, tk_ in (("wgT_d", "tg_d"), ("wuT_d", "tu_d"), ("wdT_d", "td_d")):
        w_d, t_d = E[wk], E[tk_]
        rows, cols = w_d.shape
        chunks = [(r, c) for r in range(rows // 128) for c in range(cols // CW)]
        mats.append((w_d, t_d, chunks))

    thunks = []
    sec = lambda e, f: thunks.append((e, f))

    sc_done = {}
    prev_p1 = {0: 0, 1: 0}
    prev_acc_store = 0
    for mi, (w_d, t_d, chunks) in enumerate(mats):
        nch = len(chunks)
        p1_mark = {ch: ctx.wv.n + ch + 1 for ch in range(nch)}
        ctx.wv.n += nch
        ld_mark = {}
        for ch in range(nch):
            lp = (ctx.w0, ctx.w1)[ch % 2]
            ld_mark[ch] = lp.n + 16
            lp.n += 16

        for ch, (r, c) in enumerate(chunks):
            def f_l(sy, ch=ch, r=r, c=c, w_d=w_d, p1_mark=p1_mark,
                    pp=dict(prev_p1)):
                if ch >= 2:
                    sy.wait_ge(ctx.wv.h, p1_mark[ch - 2])
                elif pp[ch % 2]:
                    sy.wait_ge(ctx.wv.h, pp[ch % 2])
                sy.dma_start(out=wbuf[:, (ch % 2) * CW:(ch % 2 + 1) * CW],
                             in_=w_d[r * 128:(r + 1) * 128, c * CW:(c + 1) * CW]
                             ).then_inc((ctx.w0, ctx.w1)[ch % 2].h, 16)
            sec("gpsimd", f_l)

            def f_s(v, ch=ch, ld_mark=ld_mark, p1_mark=p1_mark,
                    pas=prev_acc_store):
                v.wait_ge((ctx.w0, ctx.w1)[ch % 2].h, ld_mark[ch])
                if ch == 0 and pas:
                    v.wait_ge(ctx.wr.h, pas)
                src = wbuf[:, (ch % 2) * CW:(ch % 2 + 1) * CW]
                v.scalar_tensor_tensor(out=t1[:, 0:CW], in0=src, scalar=-1.0,
                                       in1=src, op0=ALU.mult, op1=ALU.max,
                                       accum_out=accs[:, 1:2])
                if ch == 0:
                    inst = v.tensor_copy(out=accs[:, 0:1], in_=accs[:, 1:2])
                else:
                    inst = v.tensor_add(out=accs[:, 0:1], in0=accs[:, 0:1],
                                        in1=accs[:, 1:2])
                inst.then_inc(ctx.wv.h, 1)
                ctx._note_sync(inst, ctx.wv.h, p1_mark[ch])
            sec("vector", f_s)
        prev_p1 = {0: p1_mark[nch - 2], 1: p1_mark[nch - 1]}

        acc_done = p1_mark[nch - 1]
        wr0 = ctx.wr.n
        m_strow, m_ldrow = wr0 + 16, wr0 + 32
        s11_mark = ctx.wv.n + 1
        ctx.wv.n += 1
        m_sts11, m_bc = wr0 + 48, wr0 + 80
        ctx.wr.n = m_bc
        prev_acc_store = m_strow

        def f_rt1(sy, mi=mi, acc_done=acc_done, m_strow=m_strow, wr0=wr0):
            sy.wait_ge(ctx.wv.h, acc_done)
            if wr0:
                sy.wait_ge(ctx.wr.h, wr0)
            sy.dma_start(
                out=wsum_d[mi:mi + 1, :].rearrange("a b -> b a"),
                in_=accs[:, 0:1]).then_inc(ctx.wr.h, 16)
            sy.wait_ge(ctx.wr.h, m_strow)
            sy.dma_start(out=rowx[:], in_=wsum_d[mi:mi + 1, :]
                         ).then_inc(ctx.wr.h, 16)
        sec("gpsimd", f_rt1)

        def f_sum(v, m_ldrow=m_ldrow, s11_mark=s11_mark):
            v.wait_ge(ctx.wr.h, m_ldrow)
            v.reduce_sum(out=s11[:, 0:1], in_=rowx[:], axis=AX.X)
            v.tensor_scalar(out=s11[:, 0:1], in0=s11[:, 0:1],
                            scalar1=1.0 / NELEM_W, scalar2=1e-5,
                            op0=ALU.mult, op1=ALU.max)
            inst = v.reciprocal(out=s11[:, 1:2], in_=s11[:, 0:1])
            inst.then_inc(ctx.wv.h, 1)
            ctx._note_sync(inst, ctx.wv.h, s11_mark)
        sec("vector", f_sum)

        def f_rt2(sy, mi=mi, s11_mark=s11_mark, m_sts11=m_sts11):
            sy.wait_ge(ctx.wv.h, s11_mark)
            sy.dma_start(out=sc_d[mi:mi + 1, :], in_=s11[:]
                         ).then_inc(ctx.wr.h, 16)
            sy.wait_ge(ctx.wr.h, m_sts11)
            _, bap0 = bass.broadcast_tensor_aps(wsc128[:, mi:mi + 1],
                                                sc_d[mi:mi + 1, 0:1])
            sy.dma_start(out=wsc128[:, mi:mi + 1], in_=bap0
                         ).then_inc(ctx.wr.h, 16)
            _, bap1 = bass.broadcast_tensor_aps(swq128[:, mi:mi + 1],
                                                sc_d[mi:mi + 1, 1:2])
            sy.dma_start(out=swq128[:, mi:mi + 1], in_=bap1
                         ).then_inc(ctx.wr.h, 16)
        sec("gpsimd", f_rt2)
        sc_done[mi] = m_bc

    p1_final = dict(prev_p1)
    E["wq_scales_wr"] = ctx.wr.n

    prev_q = 0
    for mi, (w_d, t_d, chunks) in enumerate(mats):
        nch = len(chunks)
        st_base = ctx.sg.n
        q_mark = {ch: ctx.wv.n + ch + 1 for ch in range(nch)}
        ctx.wv.n += nch
        ctx.sg.n += 16 * nch
        ld2_mark = {}
        for ch in range(nch):
            lp = (ctx.w0, ctx.w1)[ch % 2]
            ld2_mark[ch] = lp.n + 16
            lp.n += 16

        for ch, (r, c) in enumerate(chunks):
            def f_ls(sy, ch=ch, r=r, c=c, chunks=chunks, q_mark=q_mark,
                     nch=nch, t_d=t_d, w_d=w_d, prev_q=prev_q):
                if ch >= 2:
                    sy.wait_ge(ctx.wv.h, q_mark[ch - 2])
                elif prev_q:
                    sy.wait_ge(ctx.wv.h, prev_q)
                else:
                    sy.wait_ge(ctx.wv.h, p1_final[ch % 2])
                sy.dma_start(out=wbuf[:, (ch % 2) * CW:(ch % 2 + 1) * CW],
                             in_=w_d[r * 128:(r + 1) * 128, c * CW:(c + 1) * CW]
                             ).then_inc((ctx.w0, ctx.w1)[ch % 2].h, 16)
                if ch >= 1:
                    r0, c0 = chunks[ch - 1]
                    sy.wait_ge(ctx.wv.h, q_mark[ch - 1])
                    sy.dma_start(out=t_d[r0 * 128:(r0 + 1) * 128, c0 * CW:(c0 + 1) * CW],
                                 in_=tbuf[:, ((ch - 1) % 2) * CW:((ch - 1) % 2 + 1) * CW]
                                 ).then_inc(ctx.sg.h, 16)
            sec("gpsimd", f_ls)

            def f_q(v, ch=ch, mi=mi, ld2_mark=ld2_mark, sc_done=sc_done,
                    nch=nch, st_base=st_base, q_mark=q_mark):
                v.wait_ge((ctx.w0, ctx.w1)[ch % 2].h, ld2_mark[ch])
                if ch == 0:
                    v.wait_ge(ctx.wr.h, sc_done[mi])
                if ch >= 2:
                    v.wait_ge(ctx.sg.h, st_base + 16 * ch)
                else:
                    v.wait_ge(ctx.sg.h, st_base)
                src = wbuf[:, (ch % 2) * CW:(ch % 2 + 1) * CW]
                tc = t1[:, (ch % 2) * CW:(ch % 2 + 1) * CW]
                v.tensor_scalar(out=tc, in0=src, scalar1=swq128[:, mi:mi + 1],
                                scalar2=MAGIC, op0=ALU.mult, op1=ALU.add)
                v.tensor_scalar(out=tc, in0=tc, scalar1=MAGIC, scalar2=1.0,
                                op0=ALU.subtract, op1=ALU.min)
                v.tensor_single_scalar(out=tc, in_=tc, scalar=-1.0, op=ALU.max)
                v.tensor_copy(out=src.bitcast(I32), in_=tc.bitcast(I32))
                inst = v.tensor_copy(out=tbuf[:, (ch % 2) * CW:(ch % 2 + 1) * CW],
                                     in_=src)
                inst.then_inc(ctx.wv.h, 1)
                ctx._note_sync(inst, ctx.wv.h, q_mark[ch])
            sec("vector", f_q)

        def f_st_last(sy, t_d=t_d, chunks=chunks, nch=nch, q_mark=q_mark):
            r, c = chunks[nch - 1]
            sy.wait_ge(ctx.wv.h, q_mark[nch - 1])
            sy.dma_start(out=t_d[r * 128:(r + 1) * 128, c * CW:(c + 1) * CW],
                         in_=tbuf[:, ((nch - 1) % 2) * CW:((nch - 1) % 2 + 1) * CW]
                         ).then_inc(ctx.sg.h, 16)
        sec("gpsimd", f_st_last)
        prev_q = q_mark[nch - 1]

    E["tern_st_mark"] = ctx.sg.n
    return thunks


def build_proj(ctx, E, s0):
    taps, tap_d = E["taps"], E["tap_d"]
    xT, qT, kT, Vaug, WT_o = E["xT"], E["qT"], E["kT"], E["Vaug"], E["WT_o"]
    fbT, fbrows, ones128, epsqn = E["fbT"], E["fbrows"], E["ones128"], E["epsqn"]

    if True:
        qWs = sb(ctx, "qWs", [128, 4 * 256], F32, s0)
        WTx = sb(ctx, "WTx", [128, NDC * D], BF16, s0)
        sqs = sb(ctx, "sqs", [128, 4 * 512], F32, s0)
        nb = sb(ctx, "nb", [128, 512], F32, s0)
        invb = sb(ctx, "invb", [128, 512], F32, s0)
        vbbc = sqs.ap()[:, 0:D]
        pq = psum(ctx, "pq", [128, 4 * 512], F32, s0)
        pv = psum(ctx, "pv", [128, 2 * 512], F32, s0)
        pbb = pq.ap()[:, 0:512]

        for nm in ("oW", "qW", "kW", "vW"):
            pidx = {"qW": 0, "kW": 1, "vW": 2, "oW": 3}[nm]
            WT_dst = WT_o if nm == "oW" else WTx
            mld = ctx.marks()

            for wh in range(2):
                mld2 = ctx.marks()

                def ld_w(sy, w_d=E[nm + "_d"], wh=wh, mld=mld, mld2=mld2):
                    if wh == 0:
                        ctx.wait_marks(sy, mld, ("pe", "dv"))
                    else:
                        sy.wait_ge(ctx.dv.h, mld2["dv"])
                    ctx.qw.inc(sy.dma_start(
                        out=qWs.ap().rearrange("p (w dh o) -> p w dh o", w=2, dh=2),
                        in_=w_d.rearrange("w (dh p) o -> p w dh o",
                                          p=128)[:, wh * 2:(wh + 1) * 2]), 16)
                ctx.sec("sync", ld_w)
                mw = ctx.marks()

                def asm(v, WT_dst=WT_dst, wh=wh, mw=mw):
                    v.wait_ge(ctx.qw.h, mw["qw"])
                    if wh == 0:
                        v.wait_ge(ctx.pe.h, mw["pe"])
                    for dc in range(NDC):
                        c, dh = dc // 2, dc % 2
                        for p in range(4):
                            w, sign = SIGN_TABLE[p][c]
                            if w // 2 != wh:
                                continue
                            last = v.tensor_scalar_mul(
                                out=WT_dst[:, dc * D + p * 256:dc * D + (p + 1) * 256],
                                in0=qWs[:, ((w % 2) * 2 + dh) * 256:((w % 2) * 2 + dh + 1) * 256],
                                scalar1=sign)
                    ctx.dv.inc(last)
                ctx.sec("vector", asm)
            masm = ctx.marks()

            if nm == "oW":
                continue

            if nm in ("qW", "kW"):
                dstT = qT if nm == "qW" else kT
                sched = {}
                pe_n, sc_n, dv_n = ctx.pe.n, ctx.sc.n, ctx.dv.n
                for i, (g, ts_) in enumerate([(g, s_) for g in range(2) for s_ in range(4)]):
                    pe_n += 1
                    sc_n += 1
                    dv_n += 1
                    sc_n += 1
                    dv_n += 4
                    sched[(g, ts_)] = dict(mm=pe_n, sq=sc_n - 1, ssum=dv_n - 4,
                                           sqrt=sc_n, out=dv_n)

                def qk_te(t, masm=masm, sched=sched):
                    t.wait_ge(ctx.dv.h, masm["dv"])
                    for g in range(2):
                        for ts_ in range(4):
                            if (g, ts_) != (0, 0):
                                prev = (g, ts_ - 1) if ts_ > 0 else (g - 1, 3)
                                t.wait_ge(ctx.dv.h, sched[prev]["out"])
                            for jj in range(4):
                                j = g + 2 * jj
                                for dc in range(NDC):
                                    mm = t.matmul(
                                        out=pq[:, jj * 512:(jj + 1) * 512],
                                        lhsT=WTx[:, dc * D + j * 128:dc * D + (j + 1) * 128],
                                        rhs=xT[:, dc * T + ts_ * 512:dc * T + (ts_ + 1) * 512],
                                        start=(dc == 0), stop=(dc == NDC - 1))
                            got = ctx.pe.inc(mm)
                            assert got == sched[(g, ts_)]["mm"]
                ctx.sec("tensor", qk_te)

                def qk_sc(s, pidx=pidx, sched=sched):
                    for g in range(2):
                        for ts_ in range(4):
                            sd = sched[(g, ts_)]
                            s.wait_ge(ctx.pe.h, sd["mm"])
                            for jj in range(4):
                                j = g + 2 * jj
                                a = s.activation(out=sqs[:, jj * 512:(jj + 1) * 512],
                                                 in_=pq[:, jj * 512:(jj + 1) * 512],
                                                 func=AF.Square,
                                                 bias=fbT[:, pidx * 8 + j:pidx * 8 + j + 1])
                            got = ctx.sc.inc(a)
                            assert got == sd["sq"]
                            s.wait_ge(ctx.dv.h, sd["ssum"])
                            got = ctx.sc.inc(s.activation(out=nb[:], in_=sqs[:, 0:512],
                                                          func=AF.Sqrt, bias=epsqn[:]))
                            assert got == sd["sqrt"]
                ctx.sec("scalar", qk_sc)

                def qk_ve(v, pidx=pidx, dstT=dstT, sched=sched):
                    for g in range(2):
                        for ts_ in range(4):
                            sd = sched[(g, ts_)]
                            v.wait_ge(ctx.sc.h, sd["sq"])
                            v.tensor_add(out=sqs[:, 0:512], in0=sqs[:, 0:512],
                                         in1=sqs[:, 512:1024])
                            v.tensor_add(out=sqs[:, 0:512], in0=sqs[:, 0:512],
                                         in1=sqs[:, 1024:1536])
                            got = ctx.dv.inc(v.tensor_add(out=sqs[:, 0:512],
                                                          in0=sqs[:, 0:512],
                                                          in1=sqs[:, 1536:2048]))
                            assert got == sd["ssum"]
                            v.wait_ge(ctx.sc.h, sd["sqrt"])
                            v.reciprocal(out=invb[:], in_=nb[:])
                            for jj in range(4):
                                j = g + 2 * jj
                                got = ctx.dv.inc(v.scalar_tensor_tensor(
                                    out=dstT[:, j * T + ts_ * 512:j * T + (ts_ + 1) * 512],
                                    in0=pq[:, jj * 512:(jj + 1) * 512],
                                    scalar=fbT[:, pidx * 8 + j:pidx * 8 + j + 1],
                                    in1=invb[:], op0=ALU.add, op1=ALU.mult))
                            assert got == sd["out"]
                ctx.sec("vector", qk_ve)
            else:
                mv0 = ctx.marks()
                bb_pe = {0: mv0["pe"] + 1, 1: mv0["pe"] + 2}
                bb_dv = {0: mv0["dv"] + 1, 1: mv0["dv"] + 2}

                def vb_te(t, mv0=mv0, bb_dv=bb_dv):
                    t.wait_ge(ctx.dv.h, mv0["dv"])
                    for os in range(2):
                        if os == 1:
                            t.wait_ge(ctx.dv.h, bb_dv[0])
                        ctx.pe.inc(t.matmul(
                            out=pbb[:], lhsT=ones128[0:1, :],
                            rhs=fbrows[:, os * 512:(os + 1) * 512],
                            start=True, stop=True))
                ctx.sec("tensor", vb_te)

                def vb_ve(v, bb_pe=bb_pe):
                    for os in range(2):
                        v.wait_ge(ctx.pe.h, bb_pe[os])
                        ctx.dv.inc(v.tensor_copy(out=vbbc[:, os * 512:(os + 1) * 512],
                                                 in_=pbb[:]))
                ctx.sec("vector", vb_ve)
                mvb = ctx.marks()

                Vr = Vaug.ap().rearrange("p (tt h c) -> p tt h c", tt=NTT, h=H)
                vb_r = vbbc.rearrange("p (os h c) -> p os h c", os=2, c=64)
                pv_r = pv.ap().rearrange("p (b h c) -> p b h c", b=2, c=64)
                vmm = {i: mvb["pe"] + i + 1 for i in range(32)}
                vev = {i: mvb["dv"] + i + 1 for i in range(32)}

                def v_te(t, masm=masm, mvb=mvb, vmm=vmm, vev=vev):
                    t.wait_ge(ctx.dv.h, masm["dv"])
                    for tt in range(NTT):
                        for os in range(2):
                            i = tt * 2 + os
                            if i >= 2:
                                t.wait_ge(ctx.dv.h, vev[i - 2])
                            for dc in range(NDC):
                                mm = t.matmul(
                                    out=pv[:, (i % 2) * 512:(i % 2 + 1) * 512],
                                    lhsT=xT[:, dc * T + tt * 128:dc * T + (tt + 1) * 128],
                                    rhs=WTx[:, dc * D + os * 512:dc * D + (os + 1) * 512],
                                    start=(dc == 0), stop=(dc == NDC - 1))
                            got = ctx.pe.inc(mm)
                            assert got == vmm[i]
                ctx.sec("tensor", v_te)

                def v_ve(v, vmm=vmm, vev=vev):
                    for tt in range(NTT):
                        for os in range(2):
                            i = tt * 2 + os
                            v.wait_ge(ctx.pe.h, vmm[i])
                            got = ctx.dv.inc(v.tensor_tensor(
                                out=Vr[:, tt, os * 8:(os + 1) * 8, 0:64],
                                in0=pv_r[:, i % 2, :, :],
                                in1=vb_r[:, os, :, :], op=ALU.add))
                            assert got == vev[i]
                ctx.sec("vector", v_ve)

        if "qT" in taps:
            _tap(ctx, tap_d["qT"], qT[:])
        if "kT" in taps:
            _tap(ctx, tap_d["kT"], kT[:])
        if "Vaug" in taps:
            _tap(ctx, tap_d["Vaug"], Vaug[:])
        if "WTo" in taps:
            _tap(ctx, tap_d["WTo"], WT_o[:])


def build_attn(ctx, E, stT, n_heads, wq_thunks=()):
    taps, tap_d = E["taps"], E["tap_d"]
    qT, kT, Vaug, attnoutT = E["qT"], E["kT"], E["Vaug"], E["attnoutT"]
    cb128, rscr_d = E["cb128"], E["rscr_d"]

    NSLOT = 8
    PT = sb(ctx, "PT", [128, NSLOT * T], BF16, stT)
    rrow = sb(ctx, "rrow", [128, 2 * 512], F32, stT)
    rbc = sb(ctx, "rbc", [128, 2 * 512], F32, stT)
    pS0 = psum(ctx, "pS0", [128, 1024], F32, stT)
    pS1 = psum(ctx, "pS1", [128, 1024], F32, stT)
    pS = (pS0, pS1)
    pPV = psum(ctx, "pPV", [128, 4 * 512], F32, stT)

    m0 = ctx.marks()
    Vr = Vaug.ap().rearrange("p (tt hc) -> p tt hc", tt=NTT)
    NH2 = n_heads * 2

    smm, expm, pvm = {}, {}, {}
    rrm, evm = {}, {}
    rs_store, rs_read = {}, {}
    pe_n, sc_n, dv_n, rs_n = ctx.pe.n, ctx.sc.n, ctx.dv.n, ctx.rs.n
    rl_n = {0: ctx.rl0.n, 1: ctx.rl1.n}
    for gh in range(NH2 + 1):
        for kc8 in range(8):
            if gh >= 1:
                pe_n += 1
                pvm[(gh - 1, kc8)] = pe_n
            if gh < NH2:
                for qh in range(2):
                    pe_n += 1
                    smm[(gh, kc8, qh)] = pe_n
                    sc_n += 1
                    expm[(gh, kc8, qh)] = sc_n
        if gh < NH2:
            expm[gh] = sc_n
        if gh >= 1:
            pvm[gh - 1] = pvm[(gh - 1, 7)]
        if gh >= 2 and gh % 2 == 0:
            h = (gh - 2) // 2
            for qb in range(4):
                dv_n += 1
                rrm[(h, qb)] = dv_n
                rs_n += 16
                rs_store[(h, qb)] = rs_n
                rl_n[qb % 2] += 16
                rs_read[(h, qb)] = rl_n[qb % 2]
            for qb in range(4):
                dv_n += 1
                evm[(h, qb)] = dv_n
            evm[h] = dv_n
    def emit_half_pe(gh):
        def f(t, gh=gh):
            if gh < NH2:
                h, ha = gh // 2, gh % 2
                p0 = 64 * (h % 2)
                bq = (h // 2) * T
            if gh >= 1:
                hp, hap = (gh - 1) // 2, (gh - 1) % 2
                oddp = hp % 2
            for kc8 in range(8):
                if gh >= 1:
                    kcp = hap * 8 + kc8
                    t.wait_ge(ctx.sc.h, expm[(gh - 1, kc8, 1)])
                    if kcp == 0 and hp > 0:
                        t.wait_ge(ctx.dv.h, evm[hp - 1])
                    for qb in range(4):
                        rhsPT = PT[:, kc8 * T + qb * 512:kc8 * T + (qb + 1) * 512]
                        if not oddp:
                            mm = t.matmul(
                                out=pPV[0:65, qb * 512:(qb + 1) * 512],
                                lhsT=Vr[:, kcp, hp * 65:(hp + 1) * 65],
                                rhs=rhsPT, start=(kcp == 0), stop=(kcp == 15))
                        else:
                            t.matmul(
                                out=pPV[64:128, qb * 512:(qb + 1) * 512],
                                lhsT=Vr[:, kcp, hp * 65:hp * 65 + 64],
                                rhs=rhsPT, start=(kcp == 0), stop=(kcp == 15),
                                tile_position=(0, 64))
                            mm = t.matmul(
                                out=pPV[0:1, qb * 512:(qb + 1) * 512],
                                lhsT=Vr[:, kcp, hp * 65 + 64:hp * 65 + 65],
                                rhs=rhsPT, start=(kcp == 0), stop=(kcp == 15),
                                tile_position=(0, 0))
                    got = ctx.pe.inc(mm)
                    assert got == pvm[(gh - 1, kc8)]
                if gh < NH2:
                    kc = ha * 8 + kc8
                    for qh in range(2):
                        gp_ = gh * 16 + kc8 * 2 + qh
                        if gp_ >= 2:
                            pg = gp_ - 2
                            t.wait_ge(ctx.sc.h, expm[(pg // 16, (pg % 16) // 2, pg % 2)])
                        else:
                            ctx.wait_marks(t, m0, ("dv", "sc"))
                        buf = pS[gp_ % 2]
                        for qs in (2 * qh, 2 * qh + 1):
                            mm = t.matmul(
                                out=buf[:, (qs % 2) * 512:(qs % 2 + 1) * 512],
                                lhsT=kT[p0:p0 + 64, bq + kc * 128:bq + (kc + 1) * 128],
                                rhs=qT[p0:p0 + 64, bq + qs * 512:bq + (qs + 1) * 512],
                                start=True, stop=True)
                        got = ctx.pe.inc(mm)
                        assert got == smm[(gh, kc8, qh)]
        ctx.sec("tensor", f)

    def emit_all_exps():
        def f(s):
            for gh in range(NH2):
                for kc8 in range(8):
                    gkc = gh * 8 + kc8
                    slot = gkc % NSLOT
                    for qh in range(2):
                        gp_ = gh * 16 + kc8 * 2 + qh
                        s.wait_ge(ctx.pe.h, smm[(gh, kc8, qh)])
                        if gh >= 1 and qh == 0:
                            s.wait_ge(ctx.pe.h, pvm[(gh - 1, kc8)])
                        a = s.activation(
                            out=PT[:, slot * T + qh * 1024:slot * T + (qh + 1) * 1024],
                            in_=pS[gp_ % 2][:], func=AF.Exp, scale=0.125,
                            bias=cb128[:])
                        got = ctx.sc.inc(a)
                        assert got == expm[(gh, kc8, qh)]
        ctx.sec("scalar", f)

    rr_free = [0]
    rb_free = {0: 0, 1: 0}

    def emit_tail(h):
        odd = h % 2
        p0 = 64 * odd
        srow = 0 if odd else 64
        dcq = h // 2

        def fv1(v, h=h, srow=srow):
            v.wait_ge(ctx.pe.h, pvm[2 * h + 1])
            for qb in range(4):
                if rr_free[0]:
                    v.wait_ge(ctx.rs.h, rr_free[0])
                r = v.reciprocal(out=rrow[srow:srow + 1, (qb % 2) * 512:(qb % 2 + 1) * 512],
                                 in_=pPV[srow:srow + 1, qb * 512:(qb + 1) * 512])
                r.then_inc(ctx.dv.h, 1)
                ctx._note_sync(r, ctx.dv.h, rrm[(h, qb)])
                rr_free[0] = rs_store[(h, qb)]
            ctx.dv.n = max(ctx.dv.n, rrm[(h, 3)])
        ctx.sec("vector", fv1)

        def fs1(sy, h=h, srow=srow, p0=p0):
            for qb in range(4):
                rl = (ctx.rl0, ctx.rl1)[qb % 2]
                sy.wait_ge(ctx.dv.h, rrm[(h, qb)])
                sy.dma_start(out=rscr_d[h * 4 + qb:h * 4 + qb + 1, :],
                             in_=rrow[srow:srow + 1, (qb % 2) * 512:(qb % 2 + 1) * 512]
                             ).then_inc(ctx.rs.h, 16)
                sy.wait_ge(ctx.rs.h, rs_store[(h, qb)])
                if rb_free[qb % 2]:
                    sy.wait_ge(ctx.dv.h, rb_free[qb % 2])
                bsrc = rscr_d[h * 4 + qb:h * 4 + qb + 1, :]
                _, bap = bass.broadcast_tensor_aps(rbc[p0:p0 + 64, (qb % 2) * 512:(qb % 2 + 1) * 512], bsrc)
                sy.dma_start(out=rbc[p0:p0 + 64, (qb % 2) * 512:(qb % 2 + 1) * 512],
                             in_=bap).then_inc(rl.h, 16)
                rb_free[qb % 2] = evm[(h, qb)]
        ctx.sec("sync", fs1)
        ctx.rs.n = rs_store[(h, 3)]
        ctx.rl0.n = rs_read[(h, 2)]
        ctx.rl1.n = rs_read[(h, 3)]

        def fv2(v, h=h, p0=p0, dcq=dcq):
            for qb in range(4):
                rl = (ctx.rl0, ctx.rl1)[qb % 2]
                v.wait_ge(rl.h, rs_read[(h, qb)])
                ev = v.tensor_tensor(
                    out=attnoutT[p0:p0 + 64, dcq * T + qb * 512:dcq * T + (qb + 1) * 512],
                    in0=pPV[p0:p0 + 64, qb * 512:(qb + 1) * 512],
                    in1=rbc[p0:p0 + 64, (qb % 2) * 512:(qb % 2 + 1) * 512],
                    op=ALU.mult)
                ev.then_inc(ctx.dv.h, 1)
                ctx._note_sync(ev, ctx.dv.h, evm[(h, qb)])
            ctx.dv.n = max(ctx.dv.n, evm[(h, 3)])
        ctx.sec("vector", fv2)

    wq_i = [0]
    wq_thunks = list(wq_thunks)

    def emit_wq_batch(frac_done):
        want = int(len(wq_thunks) * frac_done)
        while wq_i[0] < want:
            e, f = wq_thunks[wq_i[0]]
            ctx.sec(e, f)
            wq_i[0] += 1

    emit_all_exps()
    for gh in range(NH2 + 1):
        emit_half_pe(gh)
        if gh >= 2 and gh % 2 == 0:
            emit_tail((gh - 2) // 2)
        emit_wq_batch((gh + 1) / (NH2 + 1))
    emit_wq_batch(1.0)
    assert ctx.pe.n == pe_n and ctx.sc.n == sc_n, (ctx.pe.n, pe_n, ctx.sc.n, sc_n)
    assert ctx.dv.n == dv_n and ctx.rs.n == rs_n, (ctx.dv.n, dv_n, ctx.rs.n, rs_n)

    if "PT0" in taps:
        _tap(ctx, tap_d["PT0"], PT[:, 0:8 * T])
    if "attnoutT" in taps:
        _tap(ctx, tap_d["attnoutT"], attnoutT[:])


def build_ffn(ctx, E, stF):
    taps, tap_d = E["taps"], E["tap_d"]
    attnoutT, WT_o = E["attnoutT"], E["WT_o"]
    fbrows, ones128, epsln = E["fbrows"], E["ones128"], E["epsln"]
    identb = E["identb"]
    wsc128 = E["wsc128"]
    tg_d, tu_d, td_d, h_d, x1_d, y_d = (E["tg_d"], E["tu_d"], E["td_d"],
                                        E["h_d"], E["x1_d"], E["y_d"])
    x_d = E["x_d"]
    tern_st = E["tern_st_mark"]

    yt = sb(ctx, "yt", [128, D], F32, stF)
    x1r = sb(ctx, "x1r", [128, 2 * D], F32, stF)
    bObc = sb(ctx, "bObc", [128, D], F32, stF)
    xr = sb(ctx, "xr", [128, 2 * D], F32, stF)
    r1 = sb(ctx, "r1", [128, 2 * D], F32, stF)
    tmp = sb(ctx, "tmp", [128, D], F32, stF)
    lnt = sb(ctx, "lnt", [128, 8], F32, stF)
    vars_ = sb(ctx, "vars", [128, 4], F32, stF)
    rstd_a = sb(ctx, "rstd_a", [128, 4], F32, stF)
    mdq = sb(ctx, "mdq", [128, 4], F32, stF)
    dqg = sb(ctx, "dqg", [128, 4], F32, stF)
    dqu = sb(ctx, "dqu", [128, 4], F32, stF)
    dqd2 = sb(ctx, "dqd2", [128, 4], F32, stF)
    xqb = sb(ctx, "xqb", [128, D], BF16, stF)
    xqTg = sb(ctx, "xqTg", [128, NDC * 512], BF16, stF)
    hst = sb(ctx, "hst", [128, 2 * 512], F32, stF)
    wgs = sb(ctx, "wgs", [128, 2 * NDC * 512], BF16, stF)
    wus = sb(ctx, "wus", [128, 2 * NDC * 512], BF16, stF)
    wds = sb(ctx, "wds", [128, 8 * D], BF16, stF)
    hrl = sb(ctx, "hrl", [128, 2 * INNER], F32, stF)
    hqb = sb(ctx, "hqb", [128, 2 * INNER], BF16, stF)
    hqT8 = sb(ctx, "hqT8", [128, 2 * 1024], BF16, stF)
    pO = psum(ctx, "pO", [128, 2 * 512], F32, stF)
    pGUD = psum(ctx, "pGUD", [128, 4 * 512], F32, stF)
    pXQ = psum(ctx, "pXQ", [128, 2 * 1024], BF16, stF)

    mo = ctx.marks()

    def ob_te(t, mo=mo):
        ctx.wait_marks(t, mo, ("dv",))
        for os in range(2):
            if os == 1:
                t.wait_ge(ctx.dv.h, mo["dv"] + 1)
            ctx.pe.inc(t.matmul(out=pO[:, 0:512], lhsT=ones128[0:1, :],
                                rhs=fbrows[:, D + os * 512:D + (os + 1) * 512],
                                start=True, stop=True))
    ctx.sec("tensor", ob_te)

    def ob_ve(v, mo=mo):
        for os in range(2):
            v.wait_ge(ctx.pe.h, mo["pe"] + os + 1)
            ctx.dv.inc(v.tensor_copy(out=bObc[:, os * 512:(os + 1) * 512],
                                     in_=pO[:, 0:512]))
    ctx.sec("vector", ob_ve)
    ctx.barrier(on=("pe", "dv"))

    xq_par = [0]
    pxq_free = {0: 0, 1: 0}
    prev = dict(dev=0, r1ev={0: 0, 1: 0}, xrfree={0: 0, 1: 0},
                tp_pe=0, mm_pe=0, x1st={0: 0, 1: 0}, y_st={0: 0, 1: 0},
                hq={0: 0, 1: 0})

    for g in range(4):
        tilesched = []
        for k4 in range(4):
            tt = g * 4 + k4
            s = {}
            la = (ctx.la0, ctx.la1)[k4 % 2]
            s["lda"] = la.n + 16
            la.n += 16
            tilesched.append(s)
        pe_s, dv_s, st_s = ctx.pe.n, ctx.dv.n, ctx.st.n

        for k4, off in enumerate((1, 2, 4, 6)):
            tilesched[k4]["omm"] = pe_s + off
        for k4, off in enumerate((3, 5, 7, 8)):
            tilesched[k4]["tp"] = pe_s + off
        for k4 in range(4):
            s = tilesched[k4]
            s["r1ev"] = ctx.dv.n + 1
            s["dev"] = ctx.dv.n + 2
            s["q"] = ctx.dv.n + 3
            s["xqev"] = ctx.dv.n + 4
            ctx.dv.n += 4
            s["par"] = xq_par[0] % 2
            xq_par[0] += 1
            sx = (ctx.sx0, ctx.sx1)[k4 % 2]
            s["x1st"] = sx.n + 16
            sx.n += 16
        ctx.dv.n = dv_s

        def p_ldr(sy, g=g, tilesched=tilesched, pxr=dict(prev["xrfree"]),
                  pdev=prev.get("lastdev", 0)):
            for k4 in range(4):
                tt = g * 4 + k4
                if k4 >= 2:
                    sy.wait_ge(ctx.dv.h, tilesched[k4 - 2]["dev"])
                elif pxr[k4 % 2]:
                    sy.wait_ge(ctx.dv.h, pxr[k4 % 2])
                sy.dma_start(out=xr[:, (k4 % 2) * D:(k4 % 2 + 1) * D],
                             in_=x_d[tt * 128:(tt + 1) * 128, :]
                             ).then_inc((ctx.la0, ctx.la1)[k4 % 2].h, 16)
        ctx.sec("sync", p_ldr)

        def p_te(t, g=g, tilesched=tilesched, prl=dict(prev["r1ev"]),
                 ppx=dict(pxq_free)):
            def omm(k4):
                tt = g * 4 + k4
                s = tilesched[k4]
                if k4 >= 1:
                    t.wait_ge(ctx.dv.h, tilesched[k4 - 1]["r1ev"])
                elif prl[0]:
                    t.wait_ge(ctx.dv.h, prl[0])
                for os in range(2):
                    for dc in range(NDC):
                        mm = t.matmul(
                            out=pO[:, os * 512:(os + 1) * 512],
                            lhsT=attnoutT[:, dc * T + tt * 128:dc * T + (tt + 1) * 128],
                            rhs=WT_o[:, dc * D + os * 512:dc * D + (os + 1) * 512],
                            start=(dc == 0), stop=(dc == NDC - 1))
                got = ctx.pe.inc(mm)
                assert got == s["omm"], (got, s["omm"], k4)

            def tp(k4):
                s = tilesched[k4]
                par = s["par"]
                t.wait_ge(ctx.dv.h, s["q"])
                if k4 >= 2:
                    t.wait_ge(ctx.dv.h, tilesched[k4 - 2]["xqev"])
                elif ppx[par]:
                    t.wait_ge(ctx.dv.h, ppx[par])
                    ppx[par] = 0
                for dc in range(NDC):
                    mm = t.transpose(
                        out=pXQ[:, par * 1024 + dc * 128:par * 1024 + (dc + 1) * 128],
                        in_=xqb[:, dc * 128:(dc + 1) * 128],
                        identity=identb[:])
                got = ctx.pe.inc(mm)
                assert got == s["tp"], (got, s["tp"], k4)

            omm(0); omm(1); tp(0); omm(2); tp(1); omm(3); tp(2); tp(3)
        ctx.sec("tensor", p_te)

        def p_ve(v, g=g, tilesched=tilesched, px1=dict(prev["x1st"])):
            for k4 in range(4):
                tt = g * 4 + k4
                s = tilesched[k4]
                v.wait_ge((ctx.la0, ctx.la1)[k4 % 2].h, s["lda"])
                v.tensor_add(out=tmp[:], in0=xr[:, (k4 % 2) * D:(k4 % 2 + 1) * D],
                             in1=bObc[:])
                if k4 >= 2:
                    v.wait_ge((ctx.sx0, ctx.sx1)[k4 % 2].h,
                              tilesched[k4 - 2]["x1st"])
                elif px1[k4 % 2]:
                    v.wait_ge((ctx.sx0, ctx.sx1)[k4 % 2].h, px1[k4 % 2])
                v.wait_ge(ctx.pe.h, s["omm"])
                got = ctx.dv.inc(v.tensor_add(
                    out=r1[:, (k4 % 2) * D:(k4 % 2 + 1) * D],
                    in0=pO[:, 0:1024], in1=tmp[:]))
                assert got == s["r1ev"]
                rs = r1[:, (k4 % 2) * D:(k4 % 2 + 1) * D]
                v.reduce_sum(out=lnt[:, 0:1], in_=rs, axis=AX.X)
                v.tensor_scalar_mul(out=lnt[:, 0:1], in0=lnt[:, 0:1], scalar1=1.0 / D)
                got = ctx.dv.inc(v.tensor_scalar_sub(out=rs, in0=rs,
                                                     scalar1=lnt[:, 0:1]))
                assert got == s["dev"]
                v.scalar_tensor_tensor(out=tmp[:], in0=rs, scalar=1.0, in1=rs,
                                       op0=ALU.mult, op1=ALU.mult,
                                       accum_out=lnt[:, 1:2])
                v.tensor_scalar_mul(out=vars_[:, tt % 4:tt % 4 + 1], in0=lnt[:, 1:2],
                                    scalar1=1.0 / D)
                v.reduce_max(out=lnt[:, 2:3], in_=rs, axis=AX.X,
                             apply_absolute_value=True)
                v.tensor_scalar_max(out=lnt[:, 2:3], in0=lnt[:, 2:3], scalar1=1e-30)
                v.tensor_scalar_mul(out=mdq[:, tt % 4:tt % 4 + 1], in0=lnt[:, 2:3],
                                    scalar1=1.0 / 127.0)
                v.reciprocal(out=lnt[:, 3:4], in_=lnt[:, 2:3])
                v.tensor_scalar_mul(out=lnt[:, 3:4], in0=lnt[:, 3:4], scalar1=127.0)
                v.tensor_scalar(out=tmp[:], in0=rs, scalar1=lnt[:, 3:4],
                                scalar2=MAGIC, op0=ALU.mult, op1=ALU.add)
                v.tensor_single_scalar(out=tmp[:], in_=tmp[:], scalar=MAGIC,
                                       op=ALU.subtract)
                v.tensor_copy(out=tmp.bitcast(I32)[:], in_=tmp.bitcast(I32)[:])
                got = ctx.dv.inc(v.tensor_copy(
                    out=xqb[:, 0:D], in_=tmp[:]))
                assert got == s["q"]
                v.wait_ge(ctx.pe.h, s["tp"])
                xqr = xqTg.ap().rearrange("p (dc t) -> p dc t", dc=NDC)
                pxr_ = pXQ.ap().rearrange("p (b dc t) -> p b dc t", b=2, dc=NDC)
                got = ctx.dv.inc(v.tensor_copy(
                    out=xqr[:, :, k4 * 128:(k4 + 1) * 128], in_=pxr_[:, s["par"]]))
                assert got == s["xqev"]
        ctx.sec("vector", p_ve)

        def p_st(sy, g=g, tilesched=tilesched):
            for k4 in range(4):
                tt = g * 4 + k4
                s = tilesched[k4]
                sy.wait_ge(ctx.dv.h, s["q"])
                sy.dma_start(
                    out=x1_d[tt * 128:(tt + 1) * 128, :],
                    in_=r1[:, (k4 % 2) * D:(k4 % 2 + 1) * D]
                ).then_inc((ctx.sx0, ctx.sx1)[k4 % 2].h, 16)
        ctx.sec("sync", p_st)

        for k4 in range(4):
            pxq_free[tilesched[k4]["par"]] = tilesched[k4]["xqev"]
        prev["r1ev"][0] = tilesched[3]["r1ev"]
        prev["xrfree"][0] = tilesched[2]["dev"]
        prev["xrfree"][1] = tilesched[3]["dev"]
        prev["x1st"][0] = tilesched[2]["x1st"]
        prev["x1st"][1] = tilesched[3]["x1st"]

        msq = ctx.marks()
        sq_sc = {k4: msq["sc"] + k4 + 1 for k4 in range(4)}

        def p_sq(s, g=g, tilesched=tilesched):
            for k4 in range(4):
                tt = g * 4 + k4
                s.wait_ge(ctx.dv.h, tilesched[k4]["q"])
                a = s.activation(out=lnt[:, 4 + k4:5 + k4], in_=vars_[:, tt % 4:tt % 4 + 1],
                                 func=AF.Sqrt, bias=epsln[:])
                inst = a
                inst.then_inc(ctx.sc.h, 1)
                ctx._note_sync(inst, ctx.sc.h, sq_sc[k4])
            ctx.sc.n = sq_sc[3]
        ctx.sec("scalar", p_sq)

        def p_dq(v, g=g, tilesched=tilesched):
            for k4 in range(4):
                tt = g * 4 + k4
                v.wait_ge(ctx.sc.h, sq_sc[k4])
                v.reciprocal(out=rstd_a[:, tt % 4:tt % 4 + 1], in_=lnt[:, 4 + k4:5 + k4])
                v.tensor_mul(out=lnt[:, 0:1], in0=mdq[:, tt % 4:tt % 4 + 1],
                             in1=rstd_a[:, tt % 4:tt % 4 + 1])
                v.tensor_mul(out=dqg[:, tt % 4:tt % 4 + 1], in0=lnt[:, 0:1], in1=wsc128[:, 0:1])
                ctx.dv.inc(v.tensor_mul(out=dqu[:, tt % 4:tt % 4 + 1], in0=lnt[:, 0:1],
                                        in1=wsc128[:, 1:2]))
        ctx.sec("vector", p_dq)
        dq_done = ctx.dv.n

        mg = ctx.marks()
        gu_mm, gu_sc, gu_ev = {}, {}, {}
        pe_n, sc_n, dv_n = mg["pe"], mg["sc"], mg["dv"]
        for is_ in range(8):
            for k4 in range(4):
                pe_n += 2
                gu_mm[(is_, k4)] = pe_n
                sc_n += 1
                gu_sc[(is_, k4)] = sc_n
                dv_n += 2
                gu_ev[(is_, k4)] = dv_n
        ctx.sc.n, ctx.dv.n = sc_n, dv_n
        hst_mark = {}
        st_n = mg["st"]
        for i in range(32):
            st_n += 16
            hst_mark[i] = st_n
        ctx.st.n = st_n
        h_all_stored = st_n
        lw_base = mg["lw"]
        ctx.lw.n += 16 * 16

        def gu_sy(sy, g=g, mg=mg):
            for is_ in range(8):
                if is_ >= 2:
                    sy.wait_ge(ctx.pe.h, gu_mm[(is_ - 2, 3)])
                b = is_ % 2
                if g == 0 and is_ == 0:
                    sy.wait_ge(ctx.sg.h, tern_st)
                sy.dma_start(
                    out=wgs[:, b * 4096:(b + 1) * 4096].rearrange(
                        "p (ic o) -> p ic o", ic=NDC),
                    in_=tg_d[:, is_ * 512:(is_ + 1) * 512].rearrange(
                        "(ic p) o -> p ic o", p=128)).then_inc(ctx.lw.h, 16)
                sy.dma_start(
                    out=wus[:, b * 4096:(b + 1) * 4096].rearrange(
                        "p (ic o) -> p ic o", ic=NDC),
                    in_=tu_d[:, is_ * 512:(is_ + 1) * 512].rearrange(
                        "(ic p) o -> p ic o", p=128)).then_inc(ctx.lw.h, 16)
                if is_ >= 1:
                    for k4 in range(4):
                        i = (is_ - 1) * 4 + k4
                        sy.wait_ge(ctx.dv.h, gu_ev[(is_ - 1, k4)])
                        sy.dma_start(
                            out=h_d[(g * 4 + k4) * 128:(g * 4 + k4 + 1) * 128,
                                    (is_ - 1) * 512:is_ * 512],
                            in_=hst[:, (i % 2) * 512:(i % 2 + 1) * 512]
                            ).then_inc(ctx.st.h, 16)
            for k4 in range(4):
                i = 7 * 4 + k4
                sy.wait_ge(ctx.dv.h, gu_ev[(7, k4)])
                sy.dma_start(
                    out=h_d[(g * 4 + k4) * 128:(g * 4 + k4 + 1) * 128, 7 * 512:8 * 512],
                    in_=hst[:, (i % 2) * 512:(i % 2 + 1) * 512]).then_inc(ctx.st.h, 16)
        ctx.sec("sync", gu_sy)

        def gu_te(t, g=g, mg=mg, lw_base=lw_base):
            for is_ in range(8):
                b = is_ % 2
                t.wait_ge(ctx.lw.h, lw_base + 32 * min(is_ + 2, 8))
                for k4 in range(4):
                    i = is_ * 4 + k4
                    if i >= 2:
                        pr2 = (is_, k4 - 2) if k4 >= 2 else (is_ - 1, k4 + 2)
                        t.wait_ge(ctx.dv.h, gu_ev[pr2])
                    else:
                        t.wait_ge(ctx.dv.h, mg["dv"])
                    sG = (i % 2) * 512
                    sU = (2 + i % 2) * 512
                    for ic in range(NDC):
                        t.matmul(out=pGUD[:, sG:sG + 512],
                                 lhsT=xqTg[:, ic * 512 + k4 * 128:ic * 512 + (k4 + 1) * 128],
                                 rhs=wgs[:, b * 4096 + ic * 512:b * 4096 + (ic + 1) * 512],
                                 start=(ic == 0), stop=(ic == NDC - 1))
                        mm = t.matmul(out=pGUD[:, sU:sU + 512],
                                      lhsT=xqTg[:, ic * 512 + k4 * 128:ic * 512 + (k4 + 1) * 128],
                                      rhs=wus[:, b * 4096 + ic * 512:b * 4096 + (ic + 1) * 512],
                                      start=(ic == 0), stop=(ic == NDC - 1))
                    got = ctx.pe.inc(mm, 2)
                    assert got == gu_mm[(is_, k4)]
        ctx.sec("tensor", gu_te)

        def gu_scf(s, g=g):
            s.wait_ge(ctx.dv.h, dq_done)
            for is_ in range(8):
                for k4 in range(4):
                    tt = g * 4 + k4
                    i = is_ * 4 + k4
                    s.wait_ge(ctx.pe.h, gu_mm[(is_, k4)])
                    if (is_, k4) != (0, 0):
                        prevk = (is_, k4 - 1) if k4 > 0 else (is_ - 1, 3)
                        s.wait_ge(ctx.dv.h, gu_ev[prevk])
                    if i >= 2:
                        s.wait_ge(ctx.st.h, hst_mark[i - 1])
                    a = s.activation(out=hst[:, (i % 2) * 512:(i % 2 + 1) * 512],
                                     in_=pGUD[:, (i % 2) * 512:(i % 2 + 1) * 512],
                                     func=AF.Silu, scale=dqg[:, tt % 4:tt % 4 + 1])
                    inst = a
                    inst.then_inc(ctx.sc.h, 1)
                    ctx._note_sync(inst, ctx.sc.h, gu_sc[(is_, k4)])
        ctx.sec("scalar", gu_scf)

        def gu_ve(v, g=g):
            for is_ in range(8):
                for k4 in range(4):
                    tt = g * 4 + k4
                    i = is_ * 4 + k4
                    v.wait_ge(ctx.pe.h, gu_mm[(is_, k4)])
                    v.wait_ge(ctx.sc.h, gu_sc[(is_, k4)])
                    ev = v.scalar_tensor_tensor(
                        out=hst[:, (i % 2) * 512:(i % 2 + 1) * 512],
                        in0=pGUD[:, (2 + i % 2) * 512:(3 + i % 2) * 512],
                        scalar=dqu[:, tt % 4:tt % 4 + 1],
                        in1=hst[:, (i % 2) * 512:(i % 2 + 1) * 512],
                        op0=ALU.mult, op1=ALU.mult)
                    ev.then_inc(ctx.dv.h, 2)
                    ctx._note_sync(ev, ctx.dv.h, gu_ev[(is_, k4)])
        ctx.sec("vector", gu_ve)

        hq_dv_all = {}
        hld_mark = {}

        def emit_h_ld(pr_, tk, g=g):
            tt = g * 4 + pr_ * 2 + tk
            lp = (ctx.w0, ctx.w1)[tk]
            hld_mark[(pr_, tk)] = lp.n + 16
            lp.n += 16
            prev_user = hq_dv_all[(pr_ - 1, tk)] if pr_ >= 1 else prev["hq"][tk]

            def h_ld(sy, tt=tt, tk=tk, prev_user=prev_user):
                if prev_user:
                    sy.wait_ge(ctx.dv.h, prev_user)
                sy.wait_ge(ctx.st.h, h_all_stored)
                sy.dma_start(out=hrl[:, tk * INNER:(tk + 1) * INNER],
                             in_=h_d[tt * 128:(tt + 1) * 128, :]
                             ).then_inc((ctx.w0, ctx.w1)[tk].h, 16)
            ctx.sec("gpsimd", h_ld)

        emit_h_ld(0, 0)
        emit_h_ld(0, 1)

        for pr_ in range(2):
            tts = [g * 4 + pr_ * 2, g * 4 + pr_ * 2 + 1]
            hq_dv = {}
            for tk, tt in enumerate(tts):
                def h_q(v, tt=tt, tk=tk, pr_=pr_, pp=prev["tp_pe"]):
                    v.wait_ge((ctx.w0, ctx.w1)[tk].h, hld_mark[(pr_, tk)])
                    if pp:
                        v.wait_ge(ctx.pe.h, pp)
                    hs = hrl[:, tk * INNER:(tk + 1) * INNER]
                    v.reduce_max(out=lnt[:, 0:1], in_=hs, axis=AX.X,
                                 apply_absolute_value=True)
                    v.tensor_scalar_max(out=lnt[:, 0:1], in0=lnt[:, 0:1], scalar1=1e-5)
                    v.tensor_scalar_mul(out=lnt[:, 1:2], in0=lnt[:, 0:1],
                                        scalar1=1.0 / 127.0)
                    v.tensor_mul(out=dqd2[:, tt % 4:tt % 4 + 1], in0=lnt[:, 1:2],
                                 in1=wsc128[:, 2:3])
                    v.reciprocal(out=lnt[:, 2:3], in_=lnt[:, 0:1])
                    v.tensor_scalar_mul(out=lnt[:, 2:3], in0=lnt[:, 2:3], scalar1=127.0)
                    v.tensor_scalar(out=hs, in0=hs, scalar1=lnt[:, 2:3],
                                    scalar2=MAGIC, op0=ALU.mult, op1=ALU.add)
                    v.tensor_single_scalar(out=hs, in_=hs, scalar=MAGIC,
                                           op=ALU.subtract)
                    v.tensor_copy(out=hs.bitcast(I32), in_=hs.bitcast(I32))
                    hq_dv[tk] = ctx.dv.inc(v.tensor_copy(
                        out=hqb[:, tk * INNER:(tk + 1) * INNER], in_=hs))
                    hq_dv_all[(pr_, tk)] = hq_dv[tk]
                ctx.sec("vector", h_q)
            if pr_ == 0:
                emit_h_ld(1, 0)
                emit_h_ld(1, 1)

            mp = ctx.marks()
            dn_lw = {icg: mp["lw"] + 16 * (icg + 1) for icg in range(4)}
            ctx.lw.n += 64
            dn_tp, dn_mm, dn_hq = {}, {}, {}
            pe_n = mp["pe"]
            dv_n = mp["dv"]
            for icg in range(4):
                for tk in range(2):
                    pe_n += 1
                    dn_tp[(icg, tk)] = pe_n
                    dv_n += 1
                    dn_hq[(icg, tk)] = dv_n
                    pe_n += 1
                    dn_mm[(icg, tk)] = pe_n
            ctx.dv.n = dv_n

            def dn_sy(sy, mp=mp, wds_free=prev["mm_pe"]):
                for icg in range(4):
                    if icg >= 1:
                        sy.wait_ge(ctx.pe.h, dn_mm[(icg - 1, 1)])
                    else:
                        sy.wait_ge(ctx.st.h, h_all_stored)
                        if wds_free:
                            sy.wait_ge(ctx.pe.h, wds_free)
                    sy.dma_start(
                        out=wds.ap().rearrange("p (ic o) -> p ic o", ic=8),
                        in_=td_d[icg * 1024:(icg + 1) * 1024, :].rearrange(
                            "(ic p) o -> p ic o", p=128)).then_inc(ctx.lw.h, 16)
            ctx.sec("sync", dn_sy)

            def dn_te(t, mp=mp, ppx=dict(pxq_free)):
                for icg in range(4):
                    t.wait_ge(ctx.lw.h, dn_lw[icg])
                    for tk in range(2):
                        if (icg, tk) == (0, 0):
                            t.wait_ge(ctx.dv.h, hq_dv[1])
                        j = icg * 2 + tk
                        parj = j % 2
                        if j >= 2:
                            pj = j - 2
                            t.wait_ge(ctx.dv.h, dn_hq[(pj // 2, pj % 2)])
                        elif ppx[parj]:
                            t.wait_ge(ctx.dv.h, ppx[parj])
                            ppx[parj] = 0
                        for i8 in range(8):
                            ic = icg * 8 + i8
                            mm = t.transpose(
                                out=pXQ[:, parj * 1024 + i8 * 128:parj * 1024 + (i8 + 1) * 128],
                                in_=hqb[:, tk * INNER + ic * 128:tk * INNER + (ic + 1) * 128],
                                identity=identb[:])
                        got = ctx.pe.inc(mm)
                        assert got == dn_tp[(icg, tk)]
                        t.wait_ge(ctx.dv.h, dn_hq[(icg, tk)])
                        for i8 in range(8):
                            for os in range(2):
                                mm = t.matmul(
                                    out=pGUD[:, (tk * 2 + os) * 512:(tk * 2 + os + 1) * 512],
                                    lhsT=hqT8[:, tk * 1024 + i8 * 128:tk * 1024 + (i8 + 1) * 128],
                                    rhs=wds[:, i8 * D + os * 512:i8 * D + (os + 1) * 512],
                                    start=(icg == 0 and i8 == 0),
                                    stop=(icg == 3 and i8 == 7))
                        got = ctx.pe.inc(mm)
                        assert got == dn_mm[(icg, tk)]
            ctx.sec("tensor", dn_te)

            def dn_ve(v, mp=mp):
                for icg in range(4):
                    for tk in range(2):
                        v.wait_ge(ctx.pe.h, dn_tp[(icg, tk)])
                        parj = (icg * 2 + tk) % 2
                        ev = v.tensor_copy(out=hqT8[:, tk * 1024:(tk + 1) * 1024],
                                           in_=pXQ[:, parj * 1024:(parj + 1) * 1024])
                        ev.then_inc(ctx.dv.h, 1)
                        ctx._note_sync(ev, ctx.dv.h, dn_hq[(icg, tk)])
            ctx.sec("vector", dn_ve)
            prev["tp_pe"] = dn_tp[(3, 1)]
            prev["mm_pe"] = dn_mm[(3, 1)]
            pxq_free[0] = dn_hq[(3, 0)]
            pxq_free[1] = dn_hq[(3, 1)]

            me = ctx.marks()
            x1_ld = {tk: me[("la0", "la1")[tk]] + 16 for tk in range(2)}
            ctx.la0.n += 16
            ctx.la1.n += 16
            ev_dv = {tk: me["dv"] + 3 * tk + 3 for tk in range(2)}
            ctx.dv.n += 6
            sq2 = {tk: me["sc"] + tk + 1 for tk in range(2)}
            ctx.sc.n += 2
            y_dv = {tk: ctx.dv.n + tk + 1 for tk in range(2)}
            ctx.dv.n += 2
            y_st = {tk: ctx.sy.n + 16 * (tk + 1) for tk in range(2)}
            ctx.sy.n += 32

            def e_ld(sy, tts=tts, me=me):
                sy.wait_ge(ctx.sx0.h, me["sx0"])
                sy.wait_ge(ctx.sx1.h, me["sx1"])
                for tk, tt in enumerate(tts):
                    sy.wait_ge(ctx.dv.h, me["dv"])
                    sy.dma_start(out=x1r[:, tk * D:(tk + 1) * D],
                                 in_=x1_d[tt * 128:(tt + 1) * 128, :]
                                 ).then_inc((ctx.la0, ctx.la1)[tk].h, 16)
            ctx.sec("sync", e_ld)

            def e_ev(v, tts=tts, me=me, pr_=pr_):
                for tk, tt in enumerate(tts):
                    v.wait_ge((ctx.la0, ctx.la1)[tk].h, x1_ld[tk])
                    v.wait_ge(ctx.pe.h, dn_mm[(3, 1)])
                    v.tensor_scalar_mul(out=x1r[:, tk * D:(tk + 1) * D],
                                        in0=x1r[:, tk * D:(tk + 1) * D],
                                        scalar1=rstd_a[:, tt % 4:tt % 4 + 1])
                    rs2 = r1[:, tk * D:(tk + 1) * D]
                    for os in range(2):
                        v.scalar_tensor_tensor(
                            out=r1[:, tk * D + os * 512:tk * D + (os + 1) * 512],
                            in0=pGUD[:, (tk * 2 + os) * 512:(tk * 2 + os + 1) * 512],
                            scalar=dqd2[:, tt % 4:tt % 4 + 1],
                            in1=x1r[:, tk * D + os * 512:tk * D + (os + 1) * 512],
                            op0=ALU.mult, op1=ALU.add)
                    v.reduce_sum(out=lnt[:, 0:1], in_=rs2, axis=AX.X)
                    v.tensor_scalar_mul(out=lnt[:, 0:1], in0=lnt[:, 0:1], scalar1=1.0 / D)
                    i1 = v.tensor_scalar_sub(out=rs2, in0=rs2, scalar1=lnt[:, 0:1])
                    i1.then_inc(ctx.dv.h, 1)
                    ctx._note_sync(i1, ctx.dv.h, me["dv"] + 3 * tk + 1)
                    v.scalar_tensor_tensor(out=tmp[:], in0=rs2, scalar=1.0, in1=rs2,
                                           op0=ALU.mult, op1=ALU.mult,
                                           accum_out=lnt[:, 1:2])
                    i2 = v.tensor_scalar_mul(out=lnt[:, 4 + tk:5 + tk],
                                             in0=lnt[:, 1:2], scalar1=1.0 / D)
                    i2.then_inc(ctx.dv.h, 2)
                    ctx._note_sync(i2, ctx.dv.h, ev_dv[tk])
            ctx.sec("vector", e_ev)

            def e_sq(s, me=me):
                for tk in range(2):
                    s.wait_ge(ctx.dv.h, ev_dv[tk])
                    a = s.activation(out=lnt[:, 2 + tk:3 + tk], in_=lnt[:, 4 + tk:5 + tk],
                                     func=AF.Sqrt, bias=epsln[:])
                    a.then_inc(ctx.sc.h, 1)
                    ctx._note_sync(a, ctx.sc.h, sq2[tk])
            ctx.sec("scalar", e_sq)

            def e_out(v, tts=tts, pys=dict(prev["y_st"])):
                for tk, tt in enumerate(tts):
                    v.wait_ge(ctx.sc.h, sq2[tk])
                    if tk == 0:
                        if pys[0]:
                            v.wait_ge(ctx.sy.h, pys[0])
                    else:
                        v.wait_ge(ctx.sy.h, y_st[0])
                    v.reciprocal(out=lnt[:, 0:1], in_=lnt[:, 2 + tk:3 + tk])
                    ev = v.tensor_scalar_mul(out=yt[:, 0:D],
                                             in0=r1[:, tk * D:(tk + 1) * D],
                                             scalar1=lnt[:, 0:1])
                    ev.then_inc(ctx.dv.h, 1)
                    ctx._note_sync(ev, ctx.dv.h, y_dv[tk])
            ctx.sec("vector", e_out)

            def e_st(sy, tts=tts):
                for tk, tt in enumerate(tts):
                    sy.wait_ge(ctx.dv.h, y_dv[tk])
                    sy.dma_start(out=y_d[tt * 128:(tt + 1) * 128, :],
                                 in_=yt[:, 0:D]).then_inc(ctx.sy.h, 16)
            ctx.sec("sync", e_st)
            prev["y_st"][0] = y_st[1]
            prev["y_st"][1] = y_st[1]
        prev["hq"][0] = hq_dv_all[(1, 0)]
        prev["hq"][1] = hq_dv_all[(1, 1)]
        prev["dev"] = tilesched[3]["dev"]

    if "x1" in taps:
        def fx(sy):
            m = ctx.marks()
            ctx.wait_marks(sy, m, ("pe", "dv", "sc", "gp", "ld", "lw",
                                   "la0", "la1", "sx0", "sx1", "sy"))
            sy.wait_ge(ctx.st.h, m["st"])
            ctx.st.inc(sy.dma_start(out=tap_d["x1"], in_=x1_d), 16)
        ctx.sec("sync", fx)
    if "hsc" in taps:
        def fh(sy):
            m = ctx.marks()
            ctx.wait_marks(sy, m, ("pe", "dv", "sc", "gp", "ld", "lw",
                                   "la0", "la1", "sx0", "sx1", "sy"))
            sy.wait_ge(ctx.st.h, m["st"])
            ctx.st.inc(sy.dma_start(out=tap_d["hsc"], in_=h_d), 16)
        ctx.sec("sync", fh)


def make_inputs(L0_attn_x, L0_attn_qW, L0_attn_qb, L0_attn_kW, L0_attn_kb,
                L0_attn_vW, L0_attn_vb, L0_attn_oW, L0_attn_ob, L0_attn_J,
                L0_ffn_w_gate, L0_ffn_w_up, L0_ffn_w_down):
    import ml_dtypes
    common = {
        "qW": np.ascontiguousarray(np.asarray(L0_attn_qW, dtype=np.float32)),
        "kW": np.ascontiguousarray(np.asarray(L0_attn_kW, dtype=np.float32)),
        "vW": np.ascontiguousarray(np.asarray(L0_attn_vW, dtype=np.float32)),
        "oW": np.ascontiguousarray(np.asarray(L0_attn_oW, dtype=np.float32)),
        "qb": np.ascontiguousarray(np.asarray(L0_attn_qb, dtype=np.float32)),
        "kb": np.ascontiguousarray(np.asarray(L0_attn_kb, dtype=np.float32)),
        "vb": np.ascontiguousarray(np.asarray(L0_attn_vb, dtype=np.float32)),
        "ob": np.ascontiguousarray(np.asarray(L0_attn_ob, dtype=np.float32)),
        "J": np.ascontiguousarray(np.asarray(L0_attn_J, dtype=np.float32)),
        "wgT": np.ascontiguousarray(np.asarray(L0_ffn_w_gate, dtype=np.float32).T),
        "wuT": np.ascontiguousarray(np.asarray(L0_ffn_w_up, dtype=np.float32).T),
        "wdT": np.ascontiguousarray(np.asarray(L0_ffn_w_down, dtype=np.float32).T),
        "idb": np.eye(128, dtype=ml_dtypes.bfloat16),
        "idf": np.eye(128, dtype=np.float32),
    }
    xs = np.asarray(L0_attn_x, dtype=np.float32)
    return [dict(common, x=np.ascontiguousarray(xs[i])) for i in range(xs.shape[0])]


def kernel(L0_attn_x, L0_attn_qW, L0_attn_qb, L0_attn_kW, L0_attn_kb,
           L0_attn_vW, L0_attn_vb, L0_attn_oW, L0_attn_ob, L0_attn_J,
           L0_ffn_w_gate, L0_ffn_w_up, L0_ffn_w_down):
    nc = build()
    in_maps = make_inputs(L0_attn_x, L0_attn_qW, L0_attn_qb, L0_attn_kW,
                          L0_attn_kb, L0_attn_vW, L0_attn_vb, L0_attn_oW,
                          L0_attn_ob, L0_attn_J, L0_ffn_w_gate, L0_ffn_w_up,
                          L0_ffn_w_down)
    res = run_bass_kernel_spmd(nc, in_maps, core_ids=list(range(len(in_maps))))
    return np.stack([r["y"] for r in res.results], axis=0)


# version 17
# speedup vs baseline: 1.1231x; 1.0837x over previous
import numpy as np
from contextlib import ExitStack
import concourse.bass as bass
import concourse.mybir as mybir
from concourse.bass_utils import run_bass_kernel_spmd

F32 = mybir.dt.float32
BF16 = mybir.dt.bfloat16
I32 = mybir.dt.int32
AF = mybir.ActivationFunctionType
ALU = mybir.AluOpType
AX = mybir.AxisListType

T = 2048
D = 1024
H = 16
INNER = 4096
NTT = T // 128
NDC = D // 128
NIC = INNER // 128
MAGIC = 12582912.0
EPS_LN = 1e-5
EPS_QN = 1e-6
NELEM_W = D * INNER

SIGN_TABLE = {
    0: [(0, 1.0), (1, -1.0), (2, -1.0), (3, -1.0)],
    1: [(1, 1.0), (0, 1.0), (3, -1.0), (2, 1.0)],
    2: [(2, 1.0), (3, 1.0), (0, 1.0), (1, -1.0)],
    3: [(3, 1.0), (2, -1.0), (1, 1.0), (0, 1.0)],
}


class Ctr:
    def __init__(self, stack, nc, name, ctx=None):
        self.h = stack.enter_context(nc.semaphore(name))
        self.n = 0
        self.ctx = ctx

    def inc(self, inst, k=1):
        inst.then_inc(self.h, k)
        self.n += k
        if self.ctx is not None:
            self.ctx._note_inc(inst, self)
        return self.n


_SERIAL_OPS = frozenset((
    "tensor_copy", "tensor_tensor", "tensor_add", "tensor_sub", "tensor_mul",
    "tensor_max", "tensor_scalar", "tensor_single_scalar", "tensor_scalar_add",
    "tensor_scalar_sub", "tensor_scalar_mul", "tensor_scalar_max",
    "tensor_scalar_min", "tensor_relu", "tensor_reduce", "reduce_max",
    "reduce_sum", "scalar_tensor_tensor", "tensor_tensor_scan", "reciprocal",
    "memset", "memzero", "activation", "sign", "square", "sqrt", "mul",
    "copy", "add", "select", "copy_predicated", "transpose", "iota",
    "partition_broadcast", "partition_all_reduce", "engine_nop", "pool",
    "pool_avg", "pool_max", "bn_stats", "bn_aggr", "max", "stream_shuffle",
))


class SerialEng:
    def __init__(self, ctr):
        self._eng = None
        self._ctr = ctr
        self._pend = None
        self._sync = None

    def _rebind(self, eng):
        self._eng = eng
        return self

    def _user_inc(self, inst, ctr):
        if self._pend is inst:
            self._pend = None
            self._sync = (ctr.h, ctr.n)

    def _fence(self):
        if self._pend is not None:
            v = self._ctr.inc(self._pend)
            self._sync = (self._ctr.h, v)
            self._pend = None
        if self._sync is not None:
            self._eng.wait_ge(*self._sync)
            self._sync = None

    def __getattr__(self, name):
        f = getattr(self._eng, name)
        if name not in _SERIAL_OPS:
            return f

        def g(*a, **k):
            self._fence()
            inst = f(*a, **k)
            self._pend = inst
            return inst
        return g


class Ctx:
    def __init__(self, nc, block, stack):
        self.nc = nc
        self.block = block
        self.stack = stack
        for c in ("pe", "dv", "sc", "gp", "ld", "lw", "st",
                  "hdr", "fb", "qw", "lx0", "lx1",
                  "rs", "rl0", "rl1", "la0", "la1",
                  "sx0", "sx1", "sy",
                  "wa", "wv", "wp", "lg", "sg", "w0", "w1", "wr",
                  "dvs", "scs", "gps"):
            setattr(self, c, Ctr(stack, nc, "s_" + c, ctx=self))
        self._proxies = {"vector": SerialEng(self.dvs),
                         "scalar": SerialEng(self.scs),
                         "gpsimd": SerialEng(self.gps)}

    def _note_inc(self, inst, ctr):
        for p in self._proxies.values():
            p._user_inc(inst, ctr)

    def _note_sync(self, inst, h, v):
        for p in self._proxies.values():
            if p._pend is inst:
                p._pend = None
                p._sync = (h, v)

    def sec(self, eng, fn):
        if eng in self._proxies:
            getattr(self.block, eng)(lambda e: fn(self._proxies[eng]._rebind(e)))
        else:
            getattr(self.block, eng)(fn)

    def marks(self):
        return {c: getattr(self, c).n
                for c in ("pe", "dv", "sc", "gp", "ld", "lw", "st", "hdr", "fb",
                          "qw", "lx0", "lx1", "rs", "rl0", "rl1", "la0", "la1",
                          "sx0", "sx1", "sy", "wa", "wv", "wp", "lg", "sg",
                          "w0", "w1", "wr")}

    def wait_marks(self, eng, m, which=("pe", "dv", "sc", "gp")):
        for w in which:
            if m[w] > 0:
                eng.wait_ge(getattr(self, w).h, m[w])

    def barrier(self, engines=("sync", "vector", "scalar", "tensor", "gpsimd"),
                on=("pe", "dv", "sc", "gp", "ld", "lw", "hdr", "fb", "qw",
                    "lx0", "lx1", "rs", "wa", "wv", "wp")):
        m = self.marks()
        for e in engines:
            self.sec(e, lambda eng, m=m: self.wait_marks(eng, m, on))


def sb(ctx, name, shape, dt, stack=None):
    return (stack or ctx.stack).enter_context(ctx.nc.sbuf_tensor(name, shape, dt))


def psum(ctx, name, shape, dt=F32, stack=None):
    return (stack or ctx.stack).enter_context(ctx.nc.psum_tensor(name, shape, dt))


def _tap(ctx, dst, src_ap):
    def f(sy):
        m = ctx.marks()
        ctx.wait_marks(sy, m, ("pe", "dv", "sc", "gp", "ld", "lw", "hdr",
                               "fb", "qw", "lx0", "lx1", "rs", "rl0", "rl1",
                               "la0", "la1", "sx0", "sx1", "sy",
                               "wa", "wv", "wp"))
        ctx.st.inc(sy.dma_start(out=dst, in_=src_ap), 16)
    ctx.sec("sync", f)


def build(n_heads=H, taps=(), phase_stop=None):
    nc = bass.Bass("TRN2", target_bir_lowering=False, detect_race_conditions=False)
    taps = set(taps)
    E = {}

    for nm, shape in (("x", [T, D]), ("qW", [4, 256, 256]), ("kW", [4, 256, 256]),
                      ("vW", [4, 256, 256]), ("oW", [4, 256, 256]),
                      ("qb", [4, 256]), ("kb", [4, 256]), ("vb", [4, 256]),
                      ("ob", [4, 256]), ("J", [D]), ("wgT", [D, INNER]),
                      ("wuT", [D, INNER]), ("wdT", [INNER, D])):
        E[nm + "_d"] = nc.dram_tensor(nm, shape, F32, kind="ExternalInput").ap()
    E["idb_d"] = nc.dram_tensor("idb", [128, 128], BF16, kind="ExternalInput").ap()
    E["idf_d"] = nc.dram_tensor("idf", [128, 128], F32, kind="ExternalInput").ap()
    E["y_d"] = nc.dram_tensor("y", [T, D], F32, kind="ExternalOutput").ap()

    E["fbs_d"] = nc.dram_tensor("fbs", [4, D], F32, kind="Internal").ap()
    E["tg_d"] = nc.dram_tensor("tern_g", [D, INNER], BF16, kind="Internal").ap()
    E["tu_d"] = nc.dram_tensor("tern_u", [D, INNER], BF16, kind="Internal").ap()
    E["td_d"] = nc.dram_tensor("tern_d", [INNER, D], BF16, kind="Internal").ap()
    E["h_d"] = nc.dram_tensor("h_sc", [T, INNER], F32, kind="Internal").ap()
    E["rscr_d"] = nc.dram_tensor("rscr", [64, 512], F32, kind="Internal").ap()
    E["x1_d"] = nc.dram_tensor("x1_sc", [T, D], F32, kind="Internal").ap()
    E["wsum_d"] = nc.dram_tensor("wsum", [3, 128], F32, kind="Internal").ap()
    E["sc_d"] = nc.dram_tensor("wsc_sc", [3, 2], F32, kind="Internal").ap()

    tap_shapes = {
        "xT": ([128, NDC * T], BF16), "qT": ([128, NDC * T], BF16),
        "kT": ([128, NDC * T], BF16), "Vaug": ([128, NTT * 1040], BF16),
        "attnoutT": ([128, NDC * T], BF16), "x1": ([T, D], F32),
        "PT0": ([128, 8 * T], BF16), "cb": ([128, 1], F32),
        "fbT": ([128, 32], F32), "WTo": ([128, NDC * D], BF16),
        "terng": ([D, INNER], BF16), "ternd": ([INNER, D], BF16),
        "wsc": ([128, 3], F32), "hsc": ([T, INNER], F32),
        "wacc": ([128, 2], F32), "wrow": ([1, 128], F32), "ws11": ([1, 2], F32),
    }
    E["tap_d"] = {name: nc.dram_tensor("tap_" + name, *tap_shapes[name],
                                       kind="ExternalOutput").ap() for name in taps}
    E["taps"] = taps

    with ExitStack() as st0:
        block = st0.enter_context(nc.Block())
        ctx = Ctx(nc, block, st0)

        for nm, shape, dt in (
                ("ones128", [128, 128], F32), ("identb", [128, 128], BF16),
                ("identf", [128, 128], F32), ("cb128", [128, 1], F32),
                ("epsqn", [128, 1], F32), ("epsln", [128, 1], F32),
                ("fbT", [128, 32], F32), ("fbrows", [1, 2 * D], F32),
                ("c11", [1, 1], F32), ("wsc128", [128, 3], F32),
                ("swq128", [128, 3], F32),
                ("WT_o", [128, NDC * D], BF16),
                ("xT", [128, NDC * T], BF16)):
            E[nm] = sb(ctx, nm, shape, dt)
        E["attnoutT"] = E["xT"]

        stA = st0.enter_context(ExitStack())
        E["qT"] = sb(ctx, "qT", [128, NDC * T], BF16, stA)
        E["kT"] = sb(ctx, "kT", [128, NDC * T], BF16, stA)
        E["Vaug"] = sb(ctx, "Vaug", [128, NTT * 1040], BF16, stA)

        stWQ = st0.enter_context(ExitStack())
        CW = 1024
        E["wq_bufs"] = dict(
            CW=CW,
            wbuf=sb(ctx, "wq_wbuf", [128, 2 * CW], F32, stWQ),
            tbuf=sb(ctx, "wq_tbuf", [128, 2 * CW], BF16, stWQ),
            accs=sb(ctx, "wq_acc", [128, 2], F32, stWQ),
            rowx=sb(ctx, "wq_row", [1, 128], F32, stWQ),
            s11=sb(ctx, "wq_s11", [1, 2], F32, stWQ),
        )
        build_p0(ctx, E, st0)
        if phase_stop == "P0":
            finish(ctx); return nc
        stPW = st0.enter_context(ExitStack())
        build_proj(ctx, E, stPW)
        if phase_stop == "PROJ":
            finish(ctx); return nc
        wq_thunks = build_wq(ctx, E, stWQ, stPW)
        if phase_stop == "WQ":
            for e, f in wq_thunks:
                ctx.sec(e, f)
            finish(ctx); return nc
        m = ctx.marks()
        for e in ("vector", "scalar", "tensor", "gpsimd"):
            ctx.sec(e, lambda eng, m=m: ctx.wait_marks(
                eng, m, ("pe", "dv", "sc", "gp")))
        stPW.close()

        stT = st0.enter_context(ExitStack())
        build_attn(ctx, E, stT, n_heads, wq_thunks)
        ctx.barrier(on=("pe", "dv", "sc", "gp", "rs", "rl0", "rl1",
                        "wv", "wa", "wp", "lw", "wr"))
        if phase_stop == "ATTN":
            finish(ctx); return nc
        stT.close()
        stWQ.close()
        stA.close()

        stF = st0.enter_context(ExitStack())
        build_ffn(ctx, E, stF)
        finish(ctx)
    return nc


def finish(ctx):
    def f(sy):
        m = ctx.marks()
        ctx.wait_marks(sy, m, ("pe", "dv", "sc", "gp", "ld", "lw", "hdr", "fb",
                               "qw", "lx0", "lx1", "rs", "rl0", "rl1",
                               "la0", "la1", "sx0", "sx1", "sy",
                               "wa", "wv", "wp", "lg", "sg", "w0", "w1", "wr"))
        sy.wait_ge(ctx.st.h, ctx.st.n)
    ctx.sec("sync", f)


def build_p0(ctx, E, stX):
    taps, tap_d = E["taps"], E["tap_d"]
    ones128, identb, identf = E["ones128"], E["identb"], E["identf"]
    cb128, fbT, fbrows, c11 = E["cb128"], E["fbT"], E["fbrows"], E["c11"]
    Vaug, xT = E["Vaug"], E["xT"]

    with ExitStack() as s0:
        xld = sb(ctx, "xld", [128, 2 * D], F32, s0)
        xbf = sb(ctx, "xbf", [128, 2 * D], BF16, s0)
        fb32 = sb(ctx, "fb32", [32, 128], F32, s0)
        brow = sb(ctx, "brow", [1, D], F32, s0)
        jrow = sb(ctx, "jrow", [1, D], F32, s0)
        braw = sb(ctx, "braw", [1, 4 * D], F32, s0)
        pxt = psum(ctx, "pxt", [128, 2 * D], BF16, s0)
        pfb = psum(ctx, "pfb", [128, 32], F32, s0)
        pbc = psum(ctx, "pbc", [128, 1], F32, s0)

        cast_mark, evacT_mark = {}, {}
        n = ctx.dv.n + 2
        for tt in range(NTT):
            n += 1
            cast_mark[tt] = n
            if tt >= 1:
                n += 1
                evacT_mark[tt - 1] = n
        n += 1
        evacT_mark[NTT - 1] = n
        cb_mark, fbT_mark = n + 1, n + 2
        pe0 = ctx.pe.n
        xpose_mark = {tt: pe0 + tt + 1 for tt in range(NTT)}
        cbc_mark, pfb_mark = pe0 + NTT + 1, pe0 + NTT + 2
        HDRN = 7 * 16

        def xwait(eng, tt):
            lx = ctx.lx0 if tt % 2 == 0 else ctx.lx1
            eng.wait_ge(lx.h, 16 * (tt // 2 + 1))

        def sy1(sy):
            ctx.hdr.inc(sy.dma_start(out=jrow[:], in_=E["J_d"].rearrange("(p n) -> p n", p=1)), 16)
            for P, nm in enumerate(("qb", "kb", "vb", "ob")):
                ctx.hdr.inc(sy.dma_start(out=braw[:, P * D:(P + 1) * D],
                                         in_=E[nm + "_d"].rearrange("(o a) b -> o (a b)", o=1)), 16)
            ctx.hdr.inc(sy.dma_start(out=identb[:], in_=E["idb_d"]), 16)
            ctx.hdr.inc(sy.dma_start(out=identf[:], in_=E["idf_d"]), 16)
            for tt in range(NTT):
                if tt >= 2:
                    sy.wait_ge(ctx.dv.h, cast_mark[tt - 2])
                lx = ctx.lx0 if tt % 2 == 0 else ctx.lx1
                lx.inc(sy.dma_start(out=xld[:, (tt % 2) * D:(tt % 2 + 1) * D],
                                    in_=E["x_d"][tt * 128:(tt + 1) * 128, :]), 16)
        ctx.sec("sync", sy1)

        def sc1(s):
            s.wait_ge(ctx.hdr.h, HDRN)
            ctx.sc.inc(s.activation(out=jrow[:], in_=jrow[:], func=AF.Sin))
        ctx.sec("scalar", sc1)
        sc_sin = ctx.sc.n

        def ve1(v):
            v.memset(ones128[:], 1.0)
            v.memset(E["epsqn"][:], EPS_QN)
            v.memset(E["epsln"][:], EPS_LN)
            v.memset(Vaug[:], 1.0)
            v.wait_ge(ctx.sc.h, sc_sin)
            v.reduce_sum(out=c11[:], in_=jrow[:], axis=AX.X)
            got = ctx.dv.inc(v.tensor_scalar(out=c11[:], in0=c11[:], scalar1=0.1 / D,
                                             scalar2=-8.0, op0=ALU.mult, op1=ALU.add))
            assert got == 1
            v.wait_ge(ctx.hdr.h, HDRN)
            for P in range(4):
                base = P * D
                b = lambda w: braw[:, base + w * 256: base + (w + 1) * 256]
                t01, t01m = brow[:, 0:256], brow[:, 256:512]
                t23, t23m = brow[:, 512:768], brow[:, 768:1024]
                v.tensor_add(out=t01, in0=b(0), in1=b(1))
                v.tensor_sub(out=t01m, in0=b(0), in1=b(1))
                v.tensor_add(out=t23, in0=b(2), in1=b(3))
                v.tensor_sub(out=t23m, in0=b(2), in1=b(3))
                v.tensor_sub(out=b(0), in0=t01m, in1=t23)
                v.tensor_add(out=b(1), in0=t01, in1=t23m)
                v.tensor_add(out=b(2), in0=t01m, in1=t23)
                last = v.tensor_sub(out=b(3), in0=t01, in1=t23m)
            ctx.dv.inc(last)
            v.tensor_copy(out=fbrows[:, 0:D], in_=braw[:, 2 * D:3 * D])
            v.tensor_copy(out=fbrows[:, D:2 * D], in_=braw[:, 3 * D:4 * D])
            xTr = xT.ap().rearrange("p (dc t) -> p dc t", dc=NDC)
            pxtr = pxt.ap().rearrange("p (b dc t) -> p b dc t", b=2, dc=NDC)

            def evac(tt):
                v.wait_ge(ctx.pe.h, xpose_mark[tt])
                got = ctx.dv.inc(v.tensor_copy(out=xTr[:, :, tt * 128:(tt + 1) * 128],
                                               in_=pxtr[:, tt % 2, :, :]))
                assert got == evacT_mark[tt], (got, evacT_mark[tt])
            for tt in range(NTT):
                xwait(v, tt)
                got = ctx.dv.inc(v.tensor_copy(out=xbf[:, (tt % 2) * D:(tt % 2 + 1) * D],
                                               in_=xld[:, (tt % 2) * D:(tt % 2 + 1) * D]))
                assert got == cast_mark[tt], (got, cast_mark[tt])
                if tt >= 1:
                    evac(tt - 1)
            evac(NTT - 1)
            v.wait_ge(ctx.pe.h, cbc_mark)
            got = ctx.dv.inc(v.tensor_copy(out=cb128[:], in_=pbc[:]))
            assert got == cb_mark
            v.wait_ge(ctx.pe.h, pfb_mark)
            got = ctx.dv.inc(v.tensor_copy(out=fbT[:], in_=pfb[:]))
            assert got == fbT_mark
        ctx.sec("vector", ve1)

        def te1(t):
            for tt in range(NTT):
                t.wait_ge(ctx.dv.h, cast_mark[tt])
                if tt >= 2:
                    t.wait_ge(ctx.dv.h, evacT_mark[tt - 2])
                for dc in range(NDC):
                    mm = t.transpose(
                        out=pxt[:, (tt % 2) * D + dc * 128:(tt % 2) * D + (dc + 1) * 128],
                        in_=xbf[:, (tt % 2) * D + dc * 128:(tt % 2) * D + (dc + 1) * 128],
                        identity=identb[:])
                got = ctx.pe.inc(mm)
                assert got == xpose_mark[tt]
            t.wait_ge(ctx.dv.h, 1)
            got = ctx.pe.inc(t.matmul(out=pbc[:], lhsT=ones128[0:1, :], rhs=c11[:],
                                      start=True, stop=True))
            assert got == cbc_mark
            t.wait_ge(ctx.fb.h, 16)
            got = ctx.pe.inc(t.transpose(out=pfb[:], in_=fb32[:], identity=identf[0:32, 0:32]))
            assert got == pfb_mark
        ctx.sec("tensor", te1)

        def sy2(sy):
            sy.wait_ge(ctx.dv.h, 2)
            base = ctx.st.n
            for P in range(4):
                ctx.st.inc(sy.dma_start(out=E["fbs_d"][P:P + 1, :],
                                        in_=braw[:, P * D:(P + 1) * D]), 16)
            sy.wait_ge(ctx.st.h, base + 64)
            ctx.fb.inc(sy.dma_start(out=fb32[:],
                                    in_=E["fbs_d"].rearrange("a (c p) -> (a c) p", p=128)), 16)
        ctx.sec("sync", sy2)

        if "xT" in taps:
            _tap(ctx, tap_d["xT"], xT[:])
        if "cb" in taps:
            _tap(ctx, tap_d["cb"], cb128[:])
        if "fbT" in taps:
            _tap(ctx, tap_d["fbT"], fbT[:])
        ctx.barrier()


def build_wq(ctx, E, s0, sP):
    wsc128, swq128 = E["wsc128"], E["swq128"]

    B = E["wq_bufs"]
    CW = B["CW"]
    wbuf, tbuf = B["wbuf"], B["tbuf"]
    accs, rowx, s11 = B["accs"], B["rowx"], B["s11"]
    wsum_d, sc_d = E["wsum_d"], E["sc_d"]

    mats = []
    for wk, tk_ in (("wgT_d", "tg_d"), ("wuT_d", "tu_d"), ("wdT_d", "td_d")):
        w_d, t_d = E[wk], E[tk_]
        rows, cols = w_d.shape
        chunks = [(r, c) for r in range(rows // 128) for c in range(cols // CW)]
        mats.append((w_d, t_d, chunks))

    thunks = []
    sec = lambda e, f: thunks.append((e, f))

    sc_done = {}
    prev_p1 = {0: 0, 1: 0}
    prev_acc_store = 0
    for mi, (w_d, t_d, chunks) in enumerate(mats):
        nch = len(chunks)
        p1_mark = {ch: ctx.wv.n + ch + 1 for ch in range(nch)}
        ctx.wv.n += nch
        ld_mark = {}
        for ch in range(nch):
            lp = (ctx.w0, ctx.w1)[ch % 2]
            ld_mark[ch] = lp.n + 16
            lp.n += 16

        for ch, (r, c) in enumerate(chunks):
            def f_l(sy, ch=ch, r=r, c=c, w_d=w_d, p1_mark=p1_mark,
                    pp=dict(prev_p1)):
                if ch >= 2:
                    sy.wait_ge(ctx.wv.h, p1_mark[ch - 2])
                elif pp[ch % 2]:
                    sy.wait_ge(ctx.wv.h, pp[ch % 2])
                sy.dma_start(out=wbuf[:, (ch % 2) * CW:(ch % 2 + 1) * CW],
                             in_=w_d[r * 128:(r + 1) * 128, c * CW:(c + 1) * CW]
                             ).then_inc((ctx.w0, ctx.w1)[ch % 2].h, 16)
            sec("gpsimd", f_l)

            def f_s(v, ch=ch, ld_mark=ld_mark, p1_mark=p1_mark,
                    pas=prev_acc_store):
                v.wait_ge((ctx.w0, ctx.w1)[ch % 2].h, ld_mark[ch])
                if ch == 0 and pas:
                    v.wait_ge(ctx.wr.h, pas)
                src = wbuf[:, (ch % 2) * CW:(ch % 2 + 1) * CW]
                v.scalar_tensor_tensor(out=src, in0=src, scalar=-1.0,
                                       in1=src, op0=ALU.mult, op1=ALU.max,
                                       accum_out=accs[:, 1:2])
                if ch == 0:
                    inst = v.tensor_copy(out=accs[:, 0:1], in_=accs[:, 1:2])
                else:
                    inst = v.tensor_add(out=accs[:, 0:1], in0=accs[:, 0:1],
                                        in1=accs[:, 1:2])
                inst.then_inc(ctx.wv.h, 1)
                ctx._note_sync(inst, ctx.wv.h, p1_mark[ch])
            sec("vector", f_s)
        prev_p1 = {0: p1_mark[nch - 2], 1: p1_mark[nch - 1]}

        acc_done = p1_mark[nch - 1]
        wr0 = ctx.wr.n
        m_strow, m_ldrow = wr0 + 16, wr0 + 32
        s11_mark = ctx.wv.n + 1
        ctx.wv.n += 1
        m_sts11, m_bc = wr0 + 48, wr0 + 80
        ctx.wr.n = m_bc
        prev_acc_store = m_strow

        def f_rt1(sy, mi=mi, acc_done=acc_done, m_strow=m_strow, wr0=wr0):
            sy.wait_ge(ctx.wv.h, acc_done)
            if wr0:
                sy.wait_ge(ctx.wr.h, wr0)
            sy.dma_start(
                out=wsum_d[mi:mi + 1, :].rearrange("a b -> b a"),
                in_=accs[:, 0:1]).then_inc(ctx.wr.h, 16)
            sy.wait_ge(ctx.wr.h, m_strow)
            sy.dma_start(out=rowx[:], in_=wsum_d[mi:mi + 1, :]
                         ).then_inc(ctx.wr.h, 16)
        sec("gpsimd", f_rt1)

        def f_sum(v, m_ldrow=m_ldrow, s11_mark=s11_mark):
            v.wait_ge(ctx.wr.h, m_ldrow)
            v.reduce_sum(out=s11[:, 0:1], in_=rowx[:], axis=AX.X)
            v.tensor_scalar(out=s11[:, 0:1], in0=s11[:, 0:1],
                            scalar1=1.0 / NELEM_W, scalar2=1e-5,
                            op0=ALU.mult, op1=ALU.max)
            inst = v.reciprocal(out=s11[:, 1:2], in_=s11[:, 0:1])
            inst.then_inc(ctx.wv.h, 1)
            ctx._note_sync(inst, ctx.wv.h, s11_mark)
        sec("vector", f_sum)

        def f_rt2(sy, mi=mi, s11_mark=s11_mark, m_sts11=m_sts11):
            sy.wait_ge(ctx.wv.h, s11_mark)
            sy.dma_start(out=sc_d[mi:mi + 1, :], in_=s11[:]
                         ).then_inc(ctx.wr.h, 16)
            sy.wait_ge(ctx.wr.h, m_sts11)
            _, bap0 = bass.broadcast_tensor_aps(wsc128[:, mi:mi + 1],
                                                sc_d[mi:mi + 1, 0:1])
            sy.dma_start(out=wsc128[:, mi:mi + 1], in_=bap0
                         ).then_inc(ctx.wr.h, 16)
            _, bap1 = bass.broadcast_tensor_aps(swq128[:, mi:mi + 1],
                                                sc_d[mi:mi + 1, 1:2])
            sy.dma_start(out=swq128[:, mi:mi + 1], in_=bap1
                         ).then_inc(ctx.wr.h, 16)
        sec("gpsimd", f_rt2)
        sc_done[mi] = m_bc

    p1_final = dict(prev_p1)
    E["wq_scales_wr"] = ctx.wr.n

    prev_q = 0
    for mi, (w_d, t_d, chunks) in enumerate(mats):
        nch = len(chunks)
        st_base = ctx.sg.n
        q_mark = {ch: ctx.wv.n + ch + 1 for ch in range(nch)}
        ctx.wv.n += nch
        ctx.sg.n += 16 * nch
        ld2_mark = {}
        for ch in range(nch):
            lp = (ctx.w0, ctx.w1)[ch % 2]
            ld2_mark[ch] = lp.n + 16
            lp.n += 16

        for ch, (r, c) in enumerate(chunks):
            def f_ls(sy, ch=ch, r=r, c=c, chunks=chunks, q_mark=q_mark,
                     nch=nch, t_d=t_d, w_d=w_d, prev_q=prev_q):
                if ch >= 2:
                    sy.wait_ge(ctx.wv.h, q_mark[ch - 2])
                elif prev_q:
                    sy.wait_ge(ctx.wv.h, prev_q)
                else:
                    sy.wait_ge(ctx.wv.h, p1_final[ch % 2])
                sy.dma_start(out=wbuf[:, (ch % 2) * CW:(ch % 2 + 1) * CW],
                             in_=w_d[r * 128:(r + 1) * 128, c * CW:(c + 1) * CW]
                             ).then_inc((ctx.w0, ctx.w1)[ch % 2].h, 16)
                if ch >= 1:
                    r0, c0 = chunks[ch - 1]
                    sy.wait_ge(ctx.wv.h, q_mark[ch - 1])
                    sy.dma_start(out=t_d[r0 * 128:(r0 + 1) * 128, c0 * CW:(c0 + 1) * CW],
                                 in_=tbuf[:, ((ch - 1) % 2) * CW:((ch - 1) % 2 + 1) * CW]
                                 ).then_inc(ctx.sg.h, 16)
                if ch == nch - 1:
                    sy.wait_ge(ctx.wv.h, q_mark[ch])
                    sy.dma_start(out=t_d[r * 128:(r + 1) * 128, c * CW:(c + 1) * CW],
                                 in_=tbuf[:, (ch % 2) * CW:(ch % 2 + 1) * CW]
                                 ).then_inc(ctx.sg.h, 16)
            sec("gpsimd", f_ls)

            def f_q(v, ch=ch, mi=mi, ld2_mark=ld2_mark, sc_done=sc_done,
                    nch=nch, st_base=st_base, q_mark=q_mark):
                v.wait_ge((ctx.w0, ctx.w1)[ch % 2].h, ld2_mark[ch])
                if ch == 0:
                    v.wait_ge(ctx.wr.h, sc_done[mi])
                if ch >= 2:
                    v.wait_ge(ctx.sg.h, st_base + 16 * ch)
                else:
                    v.wait_ge(ctx.sg.h, st_base)
                src = wbuf[:, (ch % 2) * CW:(ch % 2 + 1) * CW]
                v.tensor_scalar(out=src, in0=src, scalar1=swq128[:, mi:mi + 1],
                                scalar2=MAGIC, op0=ALU.mult, op1=ALU.add)
                v.tensor_scalar(out=src, in0=src, scalar1=MAGIC, scalar2=1.0,
                                op0=ALU.subtract, op1=ALU.min)
                v.tensor_single_scalar(out=src, in_=src, scalar=-1.0, op=ALU.max)
                v.tensor_copy(out=src.bitcast(I32), in_=src.bitcast(I32))
                inst = v.tensor_copy(out=tbuf[:, (ch % 2) * CW:(ch % 2 + 1) * CW],
                                     in_=src)
                inst.then_inc(ctx.wv.h, 1)
                ctx._note_sync(inst, ctx.wv.h, q_mark[ch])
            sec("vector", f_q)
        prev_q = q_mark[nch - 1]

    E["tern_st_mark"] = ctx.sg.n
    return thunks


def build_proj(ctx, E, s0):
    taps, tap_d = E["taps"], E["tap_d"]
    xT, qT, kT, Vaug, WT_o = E["xT"], E["qT"], E["kT"], E["Vaug"], E["WT_o"]
    fbT, fbrows, ones128, epsqn = E["fbT"], E["fbrows"], E["ones128"], E["epsqn"]

    if True:
        qWs = sb(ctx, "qWs", [128, 4 * 256], F32, s0)
        WTx = sb(ctx, "WTx", [128, NDC * D], BF16, s0)
        sqs = sb(ctx, "sqs", [128, 4 * 512], F32, s0)
        nb = sb(ctx, "nb", [128, 512], F32, s0)
        invb = sb(ctx, "invb", [128, 512], F32, s0)
        vbbc = sqs.ap()[:, 0:D]
        pq = psum(ctx, "pq", [128, 4 * 512], F32, s0)
        pv = psum(ctx, "pv", [128, 2 * 512], F32, s0)
        pbb = pq.ap()[:, 0:512]

        for nm in ("oW", "qW", "kW", "vW"):
            pidx = {"qW": 0, "kW": 1, "vW": 2, "oW": 3}[nm]
            WT_dst = WT_o if nm == "oW" else WTx
            mld = ctx.marks()

            for wh in range(2):
                mld2 = ctx.marks()

                def ld_w(sy, w_d=E[nm + "_d"], wh=wh, mld=mld, mld2=mld2):
                    if wh == 0:
                        ctx.wait_marks(sy, mld, ("pe", "dv"))
                    else:
                        sy.wait_ge(ctx.dv.h, mld2["dv"])
                    ctx.qw.inc(sy.dma_start(
                        out=qWs.ap().rearrange("p (w dh o) -> p w dh o", w=2, dh=2),
                        in_=w_d.rearrange("w (dh p) o -> p w dh o",
                                          p=128)[:, wh * 2:(wh + 1) * 2]), 16)
                ctx.sec("sync", ld_w)
                mw = ctx.marks()

                def asm(v, WT_dst=WT_dst, wh=wh, mw=mw):
                    v.wait_ge(ctx.qw.h, mw["qw"])
                    if wh == 0:
                        v.wait_ge(ctx.pe.h, mw["pe"])
                    for dc in range(NDC):
                        c, dh = dc // 2, dc % 2
                        for p in range(4):
                            w, sign = SIGN_TABLE[p][c]
                            if w // 2 != wh:
                                continue
                            last = v.tensor_scalar_mul(
                                out=WT_dst[:, dc * D + p * 256:dc * D + (p + 1) * 256],
                                in0=qWs[:, ((w % 2) * 2 + dh) * 256:((w % 2) * 2 + dh + 1) * 256],
                                scalar1=sign)
                    ctx.dv.inc(last)
                ctx.sec("vector", asm)
            masm = ctx.marks()

            if nm == "oW":
                continue

            if nm in ("qW", "kW"):
                dstT = qT if nm == "qW" else kT
                sched = {}
                pe_n, sc_n, dv_n = ctx.pe.n, ctx.sc.n, ctx.dv.n
                for i, (g, ts_) in enumerate([(g, s_) for g in range(2) for s_ in range(4)]):
                    pe_n += 1
                    sc_n += 1
                    dv_n += 1
                    sc_n += 1
                    dv_n += 4
                    sched[(g, ts_)] = dict(mm=pe_n, sq=sc_n - 1, ssum=dv_n - 4,
                                           sqrt=sc_n, out=dv_n)

                def qk_te(t, masm=masm, sched=sched):
                    t.wait_ge(ctx.dv.h, masm["dv"])
                    for g in range(2):
                        for ts_ in range(4):
                            if (g, ts_) != (0, 0):
                                prev = (g, ts_ - 1) if ts_ > 0 else (g - 1, 3)
                                t.wait_ge(ctx.dv.h, sched[prev]["out"])
                            for jj in range(4):
                                j = g + 2 * jj
                                for dc in range(NDC):
                                    mm = t.matmul(
                                        out=pq[:, jj * 512:(jj + 1) * 512],
                                        lhsT=WTx[:, dc * D + j * 128:dc * D + (j + 1) * 128],
                                        rhs=xT[:, dc * T + ts_ * 512:dc * T + (ts_ + 1) * 512],
                                        start=(dc == 0), stop=(dc == NDC - 1))
                            got = ctx.pe.inc(mm)
                            assert got == sched[(g, ts_)]["mm"]
                ctx.sec("tensor", qk_te)

                def qk_sc(s, pidx=pidx, sched=sched):
                    for g in range(2):
                        for ts_ in range(4):
                            sd = sched[(g, ts_)]
                            s.wait_ge(ctx.pe.h, sd["mm"])
                            for jj in range(4):
                                j = g + 2 * jj
                                a = s.activation(out=sqs[:, jj * 512:(jj + 1) * 512],
                                                 in_=pq[:, jj * 512:(jj + 1) * 512],
                                                 func=AF.Square,
                                                 bias=fbT[:, pidx * 8 + j:pidx * 8 + j + 1])
                            got = ctx.sc.inc(a)
                            assert got == sd["sq"]
                            s.wait_ge(ctx.dv.h, sd["ssum"])
                            got = ctx.sc.inc(s.activation(out=nb[:], in_=sqs[:, 0:512],
                                                          func=AF.Sqrt, bias=epsqn[:]))
                            assert got == sd["sqrt"]
                ctx.sec("scalar", qk_sc)

                def qk_ve(v, pidx=pidx, dstT=dstT, sched=sched):
                    for g in range(2):
                        for ts_ in range(4):
                            sd = sched[(g, ts_)]
                            v.wait_ge(ctx.sc.h, sd["sq"])
                            v.tensor_add(out=sqs[:, 0:512], in0=sqs[:, 0:512],
                                         in1=sqs[:, 512:1024])
                            v.tensor_add(out=sqs[:, 0:512], in0=sqs[:, 0:512],
                                         in1=sqs[:, 1024:1536])
                            got = ctx.dv.inc(v.tensor_add(out=sqs[:, 0:512],
                                                          in0=sqs[:, 0:512],
                                                          in1=sqs[:, 1536:2048]))
                            assert got == sd["ssum"]
                            v.wait_ge(ctx.sc.h, sd["sqrt"])
                            v.reciprocal(out=invb[:], in_=nb[:])
                            for jj in range(4):
                                j = g + 2 * jj
                                got = ctx.dv.inc(v.scalar_tensor_tensor(
                                    out=dstT[:, j * T + ts_ * 512:j * T + (ts_ + 1) * 512],
                                    in0=pq[:, jj * 512:(jj + 1) * 512],
                                    scalar=fbT[:, pidx * 8 + j:pidx * 8 + j + 1],
                                    in1=invb[:], op0=ALU.add, op1=ALU.mult))
                            assert got == sd["out"]
                ctx.sec("vector", qk_ve)
            else:
                mv0 = ctx.marks()
                bb_pe = {0: mv0["pe"] + 1, 1: mv0["pe"] + 2}
                bb_dv = {0: mv0["dv"] + 1, 1: mv0["dv"] + 2}

                def vb_te(t, mv0=mv0, bb_dv=bb_dv):
                    t.wait_ge(ctx.dv.h, mv0["dv"])
                    for os in range(2):
                        if os == 1:
                            t.wait_ge(ctx.dv.h, bb_dv[0])
                        ctx.pe.inc(t.matmul(
                            out=pbb[:], lhsT=ones128[0:1, :],
                            rhs=fbrows[:, os * 512:(os + 1) * 512],
                            start=True, stop=True))
                ctx.sec("tensor", vb_te)

                def vb_ve(v, bb_pe=bb_pe):
                    for os in range(2):
                        v.wait_ge(ctx.pe.h, bb_pe[os])
                        ctx.dv.inc(v.tensor_copy(out=vbbc[:, os * 512:(os + 1) * 512],
                                                 in_=pbb[:]))
                ctx.sec("vector", vb_ve)
                mvb = ctx.marks()

                Vr = Vaug.ap().rearrange("p (tt h c) -> p tt h c", tt=NTT, h=H)
                vb_r = vbbc.rearrange("p (os h c) -> p os h c", os=2, c=64)
                pv_r = pv.ap().rearrange("p (b h c) -> p b h c", b=2, c=64)
                vmm = {i: mvb["pe"] + i + 1 for i in range(32)}
                vev = {i: mvb["dv"] + i + 1 for i in range(32)}

                def v_te(t, masm=masm, mvb=mvb, vmm=vmm, vev=vev):
                    t.wait_ge(ctx.dv.h, masm["dv"])
                    for tt in range(NTT):
                        for os in range(2):
                            i = tt * 2 + os
                            if i >= 2:
                                t.wait_ge(ctx.dv.h, vev[i - 2])
                            for dc in range(NDC):
                                mm = t.matmul(
                                    out=pv[:, (i % 2) * 512:(i % 2 + 1) * 512],
                                    lhsT=xT[:, dc * T + tt * 128:dc * T + (tt + 1) * 128],
                                    rhs=WTx[:, dc * D + os * 512:dc * D + (os + 1) * 512],
                                    start=(dc == 0), stop=(dc == NDC - 1))
                            got = ctx.pe.inc(mm)
                            assert got == vmm[i]
                ctx.sec("tensor", v_te)

                def v_ve(v, vmm=vmm, vev=vev):
                    for tt in range(NTT):
                        for os in range(2):
                            i = tt * 2 + os
                            v.wait_ge(ctx.pe.h, vmm[i])
                            got = ctx.dv.inc(v.tensor_tensor(
                                out=Vr[:, tt, os * 8:(os + 1) * 8, 0:64],
                                in0=pv_r[:, i % 2, :, :],
                                in1=vb_r[:, os, :, :], op=ALU.add))
                            assert got == vev[i]
                ctx.sec("vector", v_ve)

        if "qT" in taps:
            _tap(ctx, tap_d["qT"], qT[:])
        if "kT" in taps:
            _tap(ctx, tap_d["kT"], kT[:])
        if "Vaug" in taps:
            _tap(ctx, tap_d["Vaug"], Vaug[:])
        if "WTo" in taps:
            _tap(ctx, tap_d["WTo"], WT_o[:])


def build_attn(ctx, E, stT, n_heads, wq_thunks=()):
    taps, tap_d = E["taps"], E["tap_d"]
    qT, kT, Vaug, attnoutT = E["qT"], E["kT"], E["Vaug"], E["attnoutT"]
    cb128, rscr_d = E["cb128"], E["rscr_d"]

    NSLOT = 8
    PT = sb(ctx, "PT", [128, NSLOT * T], BF16, stT)
    rrow = sb(ctx, "rrow", [128, 2 * 512], F32, stT)
    rbc = sb(ctx, "rbc", [128, 2 * 512], F32, stT)
    pS0 = psum(ctx, "pS0", [128, 1024], F32, stT)
    pS1 = psum(ctx, "pS1", [128, 1024], F32, stT)
    pS = (pS0, pS1)
    pPV = psum(ctx, "pPV", [128, 4 * 512], F32, stT)

    m0 = ctx.marks()
    Vr = Vaug.ap().rearrange("p (tt hc) -> p tt hc", tt=NTT)
    NH2 = n_heads * 2

    smm, expm, pvm = {}, {}, {}
    rrm, evm = {}, {}
    rs_store, rs_read = {}, {}
    pe_n, sc_n, dv_n, rs_n = ctx.pe.n, ctx.sc.n, ctx.dv.n, ctx.rs.n
    rl_n = {0: ctx.rl0.n, 1: ctx.rl1.n}
    for gh in range(NH2 + 1):
        for kc8 in range(8):
            if gh >= 1:
                pe_n += 1
                pvm[(gh - 1, kc8)] = pe_n
            if gh < NH2:
                for qh in range(2):
                    pe_n += 1
                    smm[(gh, kc8, qh)] = pe_n
                    sc_n += 1
                    expm[(gh, kc8, qh)] = sc_n
        if gh < NH2:
            expm[gh] = sc_n
        if gh >= 1:
            pvm[gh - 1] = pvm[(gh - 1, 7)]
        if gh >= 2 and gh % 2 == 0:
            h = (gh - 2) // 2
            for qb in range(4):
                dv_n += 1
                rrm[(h, qb)] = dv_n
                rs_n += 16
                rs_store[(h, qb)] = rs_n
                rl_n[qb % 2] += 16
                rs_read[(h, qb)] = rl_n[qb % 2]
            for qb in range(4):
                dv_n += 1
                evm[(h, qb)] = dv_n
            evm[h] = dv_n
    def emit_half_pe(gh):
        def f(t, gh=gh):
            if gh < NH2:
                h, ha = gh // 2, gh % 2
                p0 = 64 * (h % 2)
                bq = (h // 2) * T
            if gh >= 1:
                hp, hap = (gh - 1) // 2, (gh - 1) % 2
                oddp = hp % 2
            for kc8 in range(8):
                if gh >= 1:
                    kcp = hap * 8 + kc8
                    t.wait_ge(ctx.sc.h, expm[(gh - 1, kc8, 1)])
                    if kcp == 0 and hp > 0:
                        t.wait_ge(ctx.dv.h, evm[hp - 1])
                    for qb in range(4):
                        rhsPT = PT[:, kc8 * T + qb * 512:kc8 * T + (qb + 1) * 512]
                        if not oddp:
                            mm = t.matmul(
                                out=pPV[0:65, qb * 512:(qb + 1) * 512],
                                lhsT=Vr[:, kcp, hp * 65:(hp + 1) * 65],
                                rhs=rhsPT, start=(kcp == 0), stop=(kcp == 15))
                        else:
                            t.matmul(
                                out=pPV[64:128, qb * 512:(qb + 1) * 512],
                                lhsT=Vr[:, kcp, hp * 65:hp * 65 + 64],
                                rhs=rhsPT, start=(kcp == 0), stop=(kcp == 15),
                                tile_position=(0, 64))
                            mm = t.matmul(
                                out=pPV[0:1, qb * 512:(qb + 1) * 512],
                                lhsT=Vr[:, kcp, hp * 65 + 64:hp * 65 + 65],
                                rhs=rhsPT, start=(kcp == 0), stop=(kcp == 15),
                                tile_position=(0, 0))
                    got = ctx.pe.inc(mm)
                    assert got == pvm[(gh - 1, kc8)]
                if gh < NH2:
                    kc = ha * 8 + kc8
                    for qh in range(2):
                        gp_ = gh * 16 + kc8 * 2 + qh
                        if gp_ >= 2:
                            pg = gp_ - 2
                            t.wait_ge(ctx.sc.h, expm[(pg // 16, (pg % 16) // 2, pg % 2)])
                        else:
                            ctx.wait_marks(t, m0, ("dv", "sc"))
                        buf = pS[gp_ % 2]
                        for qs in (2 * qh, 2 * qh + 1):
                            mm = t.matmul(
                                out=buf[:, (qs % 2) * 512:(qs % 2 + 1) * 512],
                                lhsT=kT[p0:p0 + 64, bq + kc * 128:bq + (kc + 1) * 128],
                                rhs=qT[p0:p0 + 64, bq + qs * 512:bq + (qs + 1) * 512],
                                start=True, stop=True)
                        got = ctx.pe.inc(mm)
                        assert got == smm[(gh, kc8, qh)]
        ctx.sec("tensor", f)

    def emit_all_exps():
        def f(s):
            for gh in range(NH2):
                for kc8 in range(8):
                    gkc = gh * 8 + kc8
                    slot = gkc % NSLOT
                    for qh in range(2):
                        gp_ = gh * 16 + kc8 * 2 + qh
                        s.wait_ge(ctx.pe.h, smm[(gh, kc8, qh)])
                        if gh >= 1 and qh == 0:
                            s.wait_ge(ctx.pe.h, pvm[(gh - 1, kc8)])
                        a = s.activation(
                            out=PT[:, slot * T + qh * 1024:slot * T + (qh + 1) * 1024],
                            in_=pS[gp_ % 2][:], func=AF.Exp, scale=0.125,
                            bias=cb128[:])
                        got = ctx.sc.inc(a)
                        assert got == expm[(gh, kc8, qh)]
        ctx.sec("scalar", f)

    rr_free = [0]
    rb_free = {0: 0, 1: 0}

    def emit_tail(h):
        odd = h % 2
        p0 = 64 * odd
        srow = 0 if odd else 64
        dcq = h // 2

        def fv1(v, h=h, srow=srow):
            v.wait_ge(ctx.pe.h, pvm[2 * h + 1])
            for qb in range(4):
                if rr_free[0]:
                    v.wait_ge(ctx.rs.h, rr_free[0])
                r = v.reciprocal(out=rrow[srow:srow + 1, (qb % 2) * 512:(qb % 2 + 1) * 512],
                                 in_=pPV[srow:srow + 1, qb * 512:(qb + 1) * 512])
                r.then_inc(ctx.dv.h, 1)
                ctx._note_sync(r, ctx.dv.h, rrm[(h, qb)])
                rr_free[0] = rs_store[(h, qb)]
            ctx.dv.n = max(ctx.dv.n, rrm[(h, 3)])
        ctx.sec("vector", fv1)

        def fs1(sy, h=h, srow=srow, p0=p0):
            for qb in range(4):
                rl = (ctx.rl0, ctx.rl1)[qb % 2]
                sy.wait_ge(ctx.dv.h, rrm[(h, qb)])
                sy.dma_start(out=rscr_d[h * 4 + qb:h * 4 + qb + 1, :],
                             in_=rrow[srow:srow + 1, (qb % 2) * 512:(qb % 2 + 1) * 512]
                             ).then_inc(ctx.rs.h, 16)
                sy.wait_ge(ctx.rs.h, rs_store[(h, qb)])
                if rb_free[qb % 2]:
                    sy.wait_ge(ctx.dv.h, rb_free[qb % 2])
                bsrc = rscr_d[h * 4 + qb:h * 4 + qb + 1, :]
                _, bap = bass.broadcast_tensor_aps(rbc[p0:p0 + 64, (qb % 2) * 512:(qb % 2 + 1) * 512], bsrc)
                sy.dma_start(out=rbc[p0:p0 + 64, (qb % 2) * 512:(qb % 2 + 1) * 512],
                             in_=bap).then_inc(rl.h, 16)
                rb_free[qb % 2] = evm[(h, qb)]
        ctx.sec("sync", fs1)
        ctx.rs.n = rs_store[(h, 3)]
        ctx.rl0.n = rs_read[(h, 2)]
        ctx.rl1.n = rs_read[(h, 3)]

        def fv2(v, h=h, p0=p0, dcq=dcq):
            for qb in range(4):
                rl = (ctx.rl0, ctx.rl1)[qb % 2]
                v.wait_ge(rl.h, rs_read[(h, qb)])
                ev = v.tensor_tensor(
                    out=attnoutT[p0:p0 + 64, dcq * T + qb * 512:dcq * T + (qb + 1) * 512],
                    in0=pPV[p0:p0 + 64, qb * 512:(qb + 1) * 512],
                    in1=rbc[p0:p0 + 64, (qb % 2) * 512:(qb % 2 + 1) * 512],
                    op=ALU.mult)
                ev.then_inc(ctx.dv.h, 1)
                ctx._note_sync(ev, ctx.dv.h, evm[(h, qb)])
            ctx.dv.n = max(ctx.dv.n, evm[(h, 3)])
        ctx.sec("vector", fv2)

    wq_i = [0]
    wq_thunks = list(wq_thunks)

    def emit_wq_batch(frac_done):
        want = int(len(wq_thunks) * frac_done)
        while wq_i[0] < want:
            e, f = wq_thunks[wq_i[0]]
            ctx.sec(e, f)
            wq_i[0] += 1

    emit_all_exps()
    for gh in range(NH2 + 1):
        emit_half_pe(gh)
        if gh >= 2 and gh % 2 == 0:
            emit_tail((gh - 2) // 2)
        emit_wq_batch((gh + 1) / (NH2 + 1))
    emit_wq_batch(1.0)
    assert ctx.pe.n == pe_n and ctx.sc.n == sc_n, (ctx.pe.n, pe_n, ctx.sc.n, sc_n)
    assert ctx.dv.n == dv_n and ctx.rs.n == rs_n, (ctx.dv.n, dv_n, ctx.rs.n, rs_n)

    if "PT0" in taps:
        _tap(ctx, tap_d["PT0"], PT[:, 0:8 * T])
    if "attnoutT" in taps:
        _tap(ctx, tap_d["attnoutT"], attnoutT[:])


def build_ffn(ctx, E, stF):
    taps, tap_d = E["taps"], E["tap_d"]
    attnoutT, WT_o = E["attnoutT"], E["WT_o"]
    fbrows, ones128, epsln = E["fbrows"], E["ones128"], E["epsln"]
    identb = E["identb"]
    wsc128 = E["wsc128"]
    tg_d, tu_d, td_d, h_d, x1_d, y_d = (E["tg_d"], E["tu_d"], E["td_d"],
                                        E["h_d"], E["x1_d"], E["y_d"])
    x_d = E["x_d"]
    tern_st = E["tern_st_mark"]

    yt = sb(ctx, "yt", [128, 2 * D], F32, stF)
    x1r = sb(ctx, "x1r", [128, 2 * D], F32, stF)
    bObc = sb(ctx, "bObc", [128, D], F32, stF)
    xr = sb(ctx, "xr", [128, 2 * D], F32, stF)
    r1 = sb(ctx, "r1", [128, 2 * D], F32, stF)
    tmp = sb(ctx, "tmp", [128, D], F32, stF)
    lnt = sb(ctx, "lnt", [128, 8], F32, stF)
    vars_ = sb(ctx, "vars", [128, NTT], F32, stF)
    rstd_a = sb(ctx, "rstd_a", [128, NTT], F32, stF)
    mdq = sb(ctx, "mdq", [128, NTT], F32, stF)
    dqg = sb(ctx, "dqg", [128, NTT], F32, stF)
    dqu = sb(ctx, "dqu", [128, NTT], F32, stF)
    dqd2 = sb(ctx, "dqd2", [128, NTT], F32, stF)
    xqb = sb(ctx, "xqb", [128, 2 * D], BF16, stF)
    xqTg = sb(ctx, "xqTg", [128, NDC * 512], BF16, stF)
    gss = sb(ctx, "gss", [128, 512], F32, stF)
    uss = sb(ctx, "uss", [128, 512], F32, stF)
    hst = sb(ctx, "hst", [128, 2 * 512], F32, stF)
    wgs = sb(ctx, "wgs", [128, 2 * NDC * 512], BF16, stF)
    wus = sb(ctx, "wus", [128, 2 * NDC * 512], BF16, stF)
    wds = sb(ctx, "wds", [128, 8 * D], BF16, stF)
    hrl = sb(ctx, "hrl", [128, INNER], F32, stF)
    hqb = sb(ctx, "hqb", [128, 2 * INNER], BF16, stF)
    hqT8 = sb(ctx, "hqT8", [128, 2 * 1024], BF16, stF)
    pO = psum(ctx, "pO", [128, 2 * 512], F32, stF)
    pGUD = psum(ctx, "pGUD", [128, 4 * 512], F32, stF)
    pXQ = psum(ctx, "pXQ", [128, 2 * 1024], BF16, stF)

    mo = ctx.marks()

    def ob_te(t, mo=mo):
        ctx.wait_marks(t, mo, ("dv",))
        for os in range(2):
            if os == 1:
                t.wait_ge(ctx.dv.h, mo["dv"] + 1)
            ctx.pe.inc(t.matmul(out=pO[:, 0:512], lhsT=ones128[0:1, :],
                                rhs=fbrows[:, D + os * 512:D + (os + 1) * 512],
                                start=True, stop=True))
    ctx.sec("tensor", ob_te)

    def ob_ve(v, mo=mo):
        for os in range(2):
            v.wait_ge(ctx.pe.h, mo["pe"] + os + 1)
            ctx.dv.inc(v.tensor_copy(out=bObc[:, os * 512:(os + 1) * 512],
                                     in_=pO[:, 0:512]))
    ctx.sec("vector", ob_ve)
    ctx.barrier(on=("pe", "dv"))

    xq_par = [0]
    pxq_free = {0: 0, 1: 0}
    prev = dict(dev=0, r1ev={0: 0, 1: 0}, xrfree={0: 0, 1: 0},
                tp_pe=0, mm_pe=0, x1st={0: 0, 1: 0}, y_st={0: 0, 1: 0})

    for g in range(4):
        tilesched = []
        for k4 in range(4):
            tt = g * 4 + k4
            s = {}
            la = (ctx.la0, ctx.la1)[k4 % 2]
            s["lda"] = la.n + 16
            la.n += 16
            tilesched.append(s)
        pe_s, dv_s, st_s = ctx.pe.n, ctx.dv.n, ctx.st.n

        for k4, off in enumerate((1, 2, 4, 6)):
            tilesched[k4]["omm"] = pe_s + off
        for k4, off in enumerate((3, 5, 7, 8)):
            tilesched[k4]["tp"] = pe_s + off
        for k4 in range(4):
            s = tilesched[k4]
            s["r1ev"] = ctx.dv.n + 1
            s["dev"] = ctx.dv.n + 2
            s["q"] = ctx.dv.n + 3
            s["xqev"] = ctx.dv.n + 4
            ctx.dv.n += 4
            s["par"] = xq_par[0] % 2
            xq_par[0] += 1
            sx = (ctx.sx0, ctx.sx1)[k4 % 2]
            s["x1st"] = sx.n + 16
            sx.n += 16
        ctx.dv.n = dv_s

        def p_ldr(sy, g=g, tilesched=tilesched, pxr=dict(prev["xrfree"]),
                  pdev=prev.get("lastdev", 0)):
            for k4 in range(4):
                tt = g * 4 + k4
                if k4 >= 2:
                    sy.wait_ge(ctx.dv.h, tilesched[k4 - 2]["dev"])
                elif pxr[k4 % 2]:
                    sy.wait_ge(ctx.dv.h, pxr[k4 % 2])
                sy.dma_start(out=xr[:, (k4 % 2) * D:(k4 % 2 + 1) * D],
                             in_=x_d[tt * 128:(tt + 1) * 128, :]
                             ).then_inc((ctx.la0, ctx.la1)[k4 % 2].h, 16)
        ctx.sec("sync", p_ldr)

        def p_te(t, g=g, tilesched=tilesched, prl=dict(prev["r1ev"]),
                 ppx=dict(pxq_free)):
            def omm(k4):
                tt = g * 4 + k4
                s = tilesched[k4]
                if k4 >= 1:
                    t.wait_ge(ctx.dv.h, tilesched[k4 - 1]["r1ev"])
                elif prl[0]:
                    t.wait_ge(ctx.dv.h, prl[0])
                for os in range(2):
                    for dc in range(NDC):
                        mm = t.matmul(
                            out=pO[:, os * 512:(os + 1) * 512],
                            lhsT=attnoutT[:, dc * T + tt * 128:dc * T + (tt + 1) * 128],
                            rhs=WT_o[:, dc * D + os * 512:dc * D + (os + 1) * 512],
                            start=(dc == 0), stop=(dc == NDC - 1))
                got = ctx.pe.inc(mm)
                assert got == s["omm"], (got, s["omm"], k4)

            def tp(k4):
                s = tilesched[k4]
                par = s["par"]
                t.wait_ge(ctx.dv.h, s["q"])
                if k4 >= 2:
                    t.wait_ge(ctx.dv.h, tilesched[k4 - 2]["xqev"])
                elif ppx[par]:
                    t.wait_ge(ctx.dv.h, ppx[par])
                    ppx[par] = 0
                for dc in range(NDC):
                    mm = t.transpose(
                        out=pXQ[:, par * 1024 + dc * 128:par * 1024 + (dc + 1) * 128],
                        in_=xqb[:, (k4 % 2) * D + dc * 128:(k4 % 2) * D + (dc + 1) * 128],
                        identity=identb[:])
                got = ctx.pe.inc(mm)
                assert got == s["tp"], (got, s["tp"], k4)

            omm(0); omm(1); tp(0); omm(2); tp(1); omm(3); tp(2); tp(3)
        ctx.sec("tensor", p_te)

        def p_ve(v, g=g, tilesched=tilesched, px1=dict(prev["x1st"])):
            for k4 in range(4):
                tt = g * 4 + k4
                s = tilesched[k4]
                v.wait_ge((ctx.la0, ctx.la1)[k4 % 2].h, s["lda"])
                v.tensor_add(out=tmp[:], in0=xr[:, (k4 % 2) * D:(k4 % 2 + 1) * D],
                             in1=bObc[:])
                if k4 >= 2:
                    v.wait_ge((ctx.sx0, ctx.sx1)[k4 % 2].h,
                              tilesched[k4 - 2]["x1st"])
                elif px1[k4 % 2]:
                    v.wait_ge((ctx.sx0, ctx.sx1)[k4 % 2].h, px1[k4 % 2])
                v.wait_ge(ctx.pe.h, s["omm"])
                got = ctx.dv.inc(v.tensor_add(
                    out=r1[:, (k4 % 2) * D:(k4 % 2 + 1) * D],
                    in0=pO[:, 0:1024], in1=tmp[:]))
                assert got == s["r1ev"]
                rs = r1[:, (k4 % 2) * D:(k4 % 2 + 1) * D]
                v.reduce_sum(out=lnt[:, 0:1], in_=rs, axis=AX.X)
                v.tensor_scalar_mul(out=lnt[:, 0:1], in0=lnt[:, 0:1], scalar1=1.0 / D)
                got = ctx.dv.inc(v.tensor_scalar_sub(out=rs, in0=rs,
                                                     scalar1=lnt[:, 0:1]))
                assert got == s["dev"]
                v.scalar_tensor_tensor(out=tmp[:], in0=rs, scalar=1.0, in1=rs,
                                       op0=ALU.mult, op1=ALU.mult,
                                       accum_out=lnt[:, 1:2])
                v.tensor_scalar_mul(out=vars_[:, tt:tt + 1], in0=lnt[:, 1:2],
                                    scalar1=1.0 / D)
                v.reduce_max(out=lnt[:, 2:3], in_=rs, axis=AX.X,
                             apply_absolute_value=True)
                v.tensor_scalar_max(out=lnt[:, 2:3], in0=lnt[:, 2:3], scalar1=1e-30)
                v.tensor_scalar_mul(out=mdq[:, tt:tt + 1], in0=lnt[:, 2:3],
                                    scalar1=1.0 / 127.0)
                v.reciprocal(out=lnt[:, 3:4], in_=lnt[:, 2:3])
                v.tensor_scalar_mul(out=lnt[:, 3:4], in0=lnt[:, 3:4], scalar1=127.0)
                v.tensor_scalar(out=tmp[:], in0=rs, scalar1=lnt[:, 3:4],
                                scalar2=MAGIC, op0=ALU.mult, op1=ALU.add)
                v.tensor_single_scalar(out=tmp[:], in_=tmp[:], scalar=MAGIC,
                                       op=ALU.subtract)
                v.tensor_copy(out=tmp.bitcast(I32)[:], in_=tmp.bitcast(I32)[:])
                got = ctx.dv.inc(v.tensor_copy(
                    out=xqb[:, (k4 % 2) * D:(k4 % 2 + 1) * D], in_=tmp[:]))
                assert got == s["q"]
                v.wait_ge(ctx.pe.h, s["tp"])
                xqr = xqTg.ap().rearrange("p (dc t) -> p dc t", dc=NDC)
                pxr_ = pXQ.ap().rearrange("p (b dc t) -> p b dc t", b=2, dc=NDC)
                got = ctx.dv.inc(v.tensor_copy(
                    out=xqr[:, :, k4 * 128:(k4 + 1) * 128], in_=pxr_[:, s["par"]]))
                assert got == s["xqev"]
        ctx.sec("vector", p_ve)

        def p_st(sy, g=g, tilesched=tilesched):
            for k4 in range(4):
                tt = g * 4 + k4
                s = tilesched[k4]
                sy.wait_ge(ctx.dv.h, s["q"])
                sy.dma_start(
                    out=x1_d[tt * 128:(tt + 1) * 128, :],
                    in_=r1[:, (k4 % 2) * D:(k4 % 2 + 1) * D]
                ).then_inc((ctx.sx0, ctx.sx1)[k4 % 2].h, 16)
        ctx.sec("sync", p_st)

        for k4 in range(4):
            pxq_free[tilesched[k4]["par"]] = tilesched[k4]["xqev"]
        prev["r1ev"][0] = tilesched[3]["r1ev"]
        prev["xrfree"][0] = tilesched[2]["dev"]
        prev["xrfree"][1] = tilesched[3]["dev"]
        prev["x1st"][0] = tilesched[2]["x1st"]
        prev["x1st"][1] = tilesched[3]["x1st"]

        msq = ctx.marks()
        sq_sc = {k4: msq["sc"] + k4 + 1 for k4 in range(4)}

        def p_sq(s, g=g, tilesched=tilesched):
            for k4 in range(4):
                tt = g * 4 + k4
                s.wait_ge(ctx.dv.h, tilesched[k4]["q"])
                a = s.activation(out=lnt[:, 4 + k4:5 + k4], in_=vars_[:, tt:tt + 1],
                                 func=AF.Sqrt, bias=epsln[:])
                inst = a
                inst.then_inc(ctx.sc.h, 1)
                ctx._note_sync(inst, ctx.sc.h, sq_sc[k4])
            ctx.sc.n = sq_sc[3]
        ctx.sec("scalar", p_sq)

        def p_dq(v, g=g, tilesched=tilesched):
            for k4 in range(4):
                tt = g * 4 + k4
                v.wait_ge(ctx.sc.h, sq_sc[k4])
                v.reciprocal(out=rstd_a[:, tt:tt + 1], in_=lnt[:, 4 + k4:5 + k4])
                v.tensor_mul(out=lnt[:, 0:1], in0=mdq[:, tt:tt + 1],
                             in1=rstd_a[:, tt:tt + 1])
                v.tensor_mul(out=dqg[:, tt:tt + 1], in0=lnt[:, 0:1], in1=wsc128[:, 0:1])
                ctx.dv.inc(v.tensor_mul(out=dqu[:, tt:tt + 1], in0=lnt[:, 0:1],
                                        in1=wsc128[:, 1:2]))
        ctx.sec("vector", p_dq)
        dq_done = ctx.dv.n

        mg = ctx.marks()
        gu_mm, gu_sc, gu_ev = {}, {}, {}
        pe_n, sc_n, dv_n = mg["pe"], mg["sc"], mg["dv"]
        for is_ in range(8):
            for k4 in range(4):
                pe_n += 2
                gu_mm[(is_, k4)] = pe_n
                sc_n += 1
                gu_sc[(is_, k4)] = sc_n
                dv_n += 2
                gu_ev[(is_, k4)] = dv_n
        ctx.sc.n, ctx.dv.n = sc_n, dv_n
        hst_mark = {}
        st_n = mg["st"]
        for i in range(32):
            st_n += 16
            hst_mark[i] = st_n
        ctx.st.n = st_n
        h_all_stored = st_n
        lw_base = mg["lw"]
        ctx.lw.n += 16 * 16

        def gu_sy(sy, g=g, mg=mg):
            for is_ in range(8):
                if is_ >= 2:
                    sy.wait_ge(ctx.pe.h, gu_mm[(is_ - 2, 3)])
                b = is_ % 2
                if g == 0 and is_ == 0:
                    sy.wait_ge(ctx.sg.h, tern_st)
                sy.dma_start(
                    out=wgs[:, b * 4096:(b + 1) * 4096].rearrange(
                        "p (ic o) -> p ic o", ic=NDC),
                    in_=tg_d[:, is_ * 512:(is_ + 1) * 512].rearrange(
                        "(ic p) o -> p ic o", p=128)).then_inc(ctx.lw.h, 16)
                sy.dma_start(
                    out=wus[:, b * 4096:(b + 1) * 4096].rearrange(
                        "p (ic o) -> p ic o", ic=NDC),
                    in_=tu_d[:, is_ * 512:(is_ + 1) * 512].rearrange(
                        "(ic p) o -> p ic o", p=128)).then_inc(ctx.lw.h, 16)
                if is_ >= 1:
                    for k4 in range(4):
                        i = (is_ - 1) * 4 + k4
                        sy.wait_ge(ctx.dv.h, gu_ev[(is_ - 1, k4)])
                        sy.dma_start(
                            out=h_d[(g * 4 + k4) * 128:(g * 4 + k4 + 1) * 128,
                                    (is_ - 1) * 512:is_ * 512],
                            in_=hst[:, (i % 2) * 512:(i % 2 + 1) * 512]
                            ).then_inc(ctx.st.h, 16)
            for k4 in range(4):
                i = 7 * 4 + k4
                sy.wait_ge(ctx.dv.h, gu_ev[(7, k4)])
                sy.dma_start(
                    out=h_d[(g * 4 + k4) * 128:(g * 4 + k4 + 1) * 128, 7 * 512:8 * 512],
                    in_=hst[:, (i % 2) * 512:(i % 2 + 1) * 512]).then_inc(ctx.st.h, 16)
        ctx.sec("sync", gu_sy)

        def gu_te(t, g=g, mg=mg, lw_base=lw_base):
            for is_ in range(8):
                b = is_ % 2
                t.wait_ge(ctx.lw.h, lw_base + 32 * min(is_ + 2, 8))
                for k4 in range(4):
                    i = is_ * 4 + k4
                    if i >= 2:
                        pr2 = (is_, k4 - 2) if k4 >= 2 else (is_ - 1, k4 + 2)
                        t.wait_ge(ctx.dv.h, gu_ev[pr2])
                    else:
                        t.wait_ge(ctx.dv.h, mg["dv"])
                    sG = (i % 2) * 512
                    sU = (2 + i % 2) * 512
                    for ic in range(NDC):
                        t.matmul(out=pGUD[:, sG:sG + 512],
                                 lhsT=xqTg[:, ic * 512 + k4 * 128:ic * 512 + (k4 + 1) * 128],
                                 rhs=wgs[:, b * 4096 + ic * 512:b * 4096 + (ic + 1) * 512],
                                 start=(ic == 0), stop=(ic == NDC - 1))
                        mm = t.matmul(out=pGUD[:, sU:sU + 512],
                                      lhsT=xqTg[:, ic * 512 + k4 * 128:ic * 512 + (k4 + 1) * 128],
                                      rhs=wus[:, b * 4096 + ic * 512:b * 4096 + (ic + 1) * 512],
                                      start=(ic == 0), stop=(ic == NDC - 1))
                    got = ctx.pe.inc(mm, 2)
                    assert got == gu_mm[(is_, k4)]
        ctx.sec("tensor", gu_te)

        def gu_scf(s, g=g):
            s.wait_ge(ctx.dv.h, dq_done)
            for is_ in range(8):
                for k4 in range(4):
                    tt = g * 4 + k4
                    i = is_ * 4 + k4
                    s.wait_ge(ctx.pe.h, gu_mm[(is_, k4)])
                    if (is_, k4) != (0, 0):
                        prevk = (is_, k4 - 1) if k4 > 0 else (is_ - 1, 3)
                        s.wait_ge(ctx.dv.h, gu_ev[prevk])
                    a = s.activation(out=gss[:], in_=pGUD[:, (i % 2) * 512:(i % 2 + 1) * 512],
                                     func=AF.Silu, scale=dqg[:, tt:tt + 1])
                    inst = a
                    inst.then_inc(ctx.sc.h, 1)
                    ctx._note_sync(inst, ctx.sc.h, gu_sc[(is_, k4)])
        ctx.sec("scalar", gu_scf)

        def gu_ve(v, g=g):
            for is_ in range(8):
                for k4 in range(4):
                    tt = g * 4 + k4
                    i = is_ * 4 + k4
                    v.wait_ge(ctx.pe.h, gu_mm[(is_, k4)])
                    v.tensor_scalar_mul(out=uss[:], in0=pGUD[:, (2 + i % 2) * 512:(3 + i % 2) * 512],
                                        scalar1=dqu[:, tt:tt + 1])
                    v.wait_ge(ctx.sc.h, gu_sc[(is_, k4)])
                    if i >= 2:
                        v.wait_ge(ctx.st.h, hst_mark[i - 1])
                    ev = v.tensor_mul(out=hst[:, (i % 2) * 512:(i % 2 + 1) * 512],
                                      in0=gss[:], in1=uss[:])
                    ev.then_inc(ctx.dv.h, 2)
                    ctx._note_sync(ev, ctx.dv.h, gu_ev[(is_, k4)])
        ctx.sec("vector", gu_ve)

        for pr_ in range(2):
            tts = [g * 4 + pr_ * 2, g * 4 + pr_ * 2 + 1]
            hq_dv = {}
            for tk, tt in enumerate(tts):
                mh = ctx.marks()
                hld = mh["ld"] + 16
                ctx.ld.n += 16

                def h_ld(sy, tt=tt, mh=mh, pp=prev["tp_pe"]):
                    sy.wait_ge(ctx.dv.h, mh["dv"])
                    if pp:
                        sy.wait_ge(ctx.pe.h, pp)
                    sy.wait_ge(ctx.st.h, h_all_stored)
                    sy.dma_start(out=hrl[:], in_=h_d[tt * 128:(tt + 1) * 128, :]
                                 ).then_inc(ctx.ld.h, 16)
                ctx.sec("sync", h_ld)

                def h_q(v, tt=tt, tk=tk, hld=hld):
                    v.wait_ge(ctx.ld.h, hld)
                    v.reduce_max(out=lnt[:, 0:1], in_=hrl[:], axis=AX.X,
                                 apply_absolute_value=True)
                    v.tensor_scalar_max(out=lnt[:, 0:1], in0=lnt[:, 0:1], scalar1=1e-5)
                    v.tensor_scalar_mul(out=lnt[:, 1:2], in0=lnt[:, 0:1],
                                        scalar1=1.0 / 127.0)
                    v.tensor_mul(out=dqd2[:, tt:tt + 1], in0=lnt[:, 1:2],
                                 in1=wsc128[:, 2:3])
                    v.reciprocal(out=lnt[:, 2:3], in_=lnt[:, 0:1])
                    v.tensor_scalar_mul(out=lnt[:, 2:3], in0=lnt[:, 2:3], scalar1=127.0)
                    v.tensor_scalar(out=hrl[:], in0=hrl[:], scalar1=lnt[:, 2:3],
                                    scalar2=MAGIC, op0=ALU.mult, op1=ALU.add)
                    v.tensor_single_scalar(out=hrl[:], in_=hrl[:], scalar=MAGIC,
                                           op=ALU.subtract)
                    v.tensor_copy(out=hrl.bitcast(I32)[:], in_=hrl.bitcast(I32)[:])
                    hq_dv[tk] = ctx.dv.inc(v.tensor_copy(
                        out=hqb[:, tk * INNER:(tk + 1) * INNER], in_=hrl[:]))
                ctx.sec("vector", h_q)

            mp = ctx.marks()
            dn_lw = {icg: mp["lw"] + 16 * (icg + 1) for icg in range(4)}
            ctx.lw.n += 64
            dn_tp, dn_mm, dn_hq = {}, {}, {}
            pe_n = mp["pe"]
            dv_n = mp["dv"]
            for icg in range(4):
                for tk in range(2):
                    pe_n += 1
                    dn_tp[(icg, tk)] = pe_n
                    dv_n += 1
                    dn_hq[(icg, tk)] = dv_n
                    pe_n += 1
                    dn_mm[(icg, tk)] = pe_n
            ctx.dv.n = dv_n

            def dn_sy(sy, mp=mp, wds_free=prev["mm_pe"]):
                for icg in range(4):
                    if icg >= 1:
                        sy.wait_ge(ctx.pe.h, dn_mm[(icg - 1, 1)])
                    else:
                        sy.wait_ge(ctx.st.h, h_all_stored)
                        if wds_free:
                            sy.wait_ge(ctx.pe.h, wds_free)
                    sy.dma_start(
                        out=wds.ap().rearrange("p (ic o) -> p ic o", ic=8),
                        in_=td_d[icg * 1024:(icg + 1) * 1024, :].rearrange(
                            "(ic p) o -> p ic o", p=128)).then_inc(ctx.lw.h, 16)
            ctx.sec("sync", dn_sy)

            def dn_te(t, mp=mp, ppx=dict(pxq_free)):
                for icg in range(4):
                    t.wait_ge(ctx.lw.h, dn_lw[icg])
                    for tk in range(2):
                        if (icg, tk) == (0, 0):
                            t.wait_ge(ctx.dv.h, hq_dv[1])
                        j = icg * 2 + tk
                        parj = j % 2
                        if j >= 2:
                            pj = j - 2
                            t.wait_ge(ctx.dv.h, dn_hq[(pj // 2, pj % 2)])
                        elif ppx[parj]:
                            t.wait_ge(ctx.dv.h, ppx[parj])
                            ppx[parj] = 0
                        for i8 in range(8):
                            ic = icg * 8 + i8
                            mm = t.transpose(
                                out=pXQ[:, parj * 1024 + i8 * 128:parj * 1024 + (i8 + 1) * 128],
                                in_=hqb[:, tk * INNER + ic * 128:tk * INNER + (ic + 1) * 128],
                                identity=identb[:])
                        got = ctx.pe.inc(mm)
                        assert got == dn_tp[(icg, tk)]
                        t.wait_ge(ctx.dv.h, dn_hq[(icg, tk)])
                        for i8 in range(8):
                            for os in range(2):
                                mm = t.matmul(
                                    out=pGUD[:, (tk * 2 + os) * 512:(tk * 2 + os + 1) * 512],
                                    lhsT=hqT8[:, tk * 1024 + i8 * 128:tk * 1024 + (i8 + 1) * 128],
                                    rhs=wds[:, i8 * D + os * 512:i8 * D + (os + 1) * 512],
                                    start=(icg == 0 and i8 == 0),
                                    stop=(icg == 3 and i8 == 7))
                        got = ctx.pe.inc(mm)
                        assert got == dn_mm[(icg, tk)]
            ctx.sec("tensor", dn_te)

            def dn_ve(v, mp=mp):
                for icg in range(4):
                    for tk in range(2):
                        v.wait_ge(ctx.pe.h, dn_tp[(icg, tk)])
                        parj = (icg * 2 + tk) % 2
                        ev = v.tensor_copy(out=hqT8[:, tk * 1024:(tk + 1) * 1024],
                                           in_=pXQ[:, parj * 1024:(parj + 1) * 1024])
                        ev.then_inc(ctx.dv.h, 1)
                        ctx._note_sync(ev, ctx.dv.h, dn_hq[(icg, tk)])
            ctx.sec("vector", dn_ve)
            prev["tp_pe"] = dn_tp[(3, 1)]
            prev["mm_pe"] = dn_mm[(3, 1)]
            pxq_free[0] = dn_hq[(3, 0)]
            pxq_free[1] = dn_hq[(3, 1)]

            me = ctx.marks()
            x1_ld = {tk: me[("la0", "la1")[tk]] + 16 for tk in range(2)}
            ctx.la0.n += 16
            ctx.la1.n += 16
            ev_dv = {tk: me["dv"] + 3 * tk + 3 for tk in range(2)}
            ctx.dv.n += 6
            sq2 = {tk: me["sc"] + tk + 1 for tk in range(2)}
            ctx.sc.n += 2
            y_dv = {tk: ctx.dv.n + tk + 1 for tk in range(2)}
            ctx.dv.n += 2
            y_st = {tk: ctx.sy.n + 16 * (tk + 1) for tk in range(2)}
            ctx.sy.n += 32

            def e_ld(sy, tts=tts, me=me):
                sy.wait_ge(ctx.sx0.h, me["sx0"])
                sy.wait_ge(ctx.sx1.h, me["sx1"])
                for tk, tt in enumerate(tts):
                    sy.wait_ge(ctx.dv.h, me["dv"])
                    sy.dma_start(out=x1r[:, tk * D:(tk + 1) * D],
                                 in_=x1_d[tt * 128:(tt + 1) * 128, :]
                                 ).then_inc((ctx.la0, ctx.la1)[tk].h, 16)
            ctx.sec("sync", e_ld)

            def e_ev(v, tts=tts, me=me, pr_=pr_):
                for tk, tt in enumerate(tts):
                    v.wait_ge((ctx.la0, ctx.la1)[tk].h, x1_ld[tk])
                    v.wait_ge(ctx.pe.h, dn_mm[(3, 1)])
                    v.tensor_scalar_mul(out=x1r[:, tk * D:(tk + 1) * D],
                                        in0=x1r[:, tk * D:(tk + 1) * D],
                                        scalar1=rstd_a[:, tt:tt + 1])
                    rs2 = r1[:, tk * D:(tk + 1) * D]
                    for os in range(2):
                        v.scalar_tensor_tensor(
                            out=r1[:, tk * D + os * 512:tk * D + (os + 1) * 512],
                            in0=pGUD[:, (tk * 2 + os) * 512:(tk * 2 + os + 1) * 512],
                            scalar=dqd2[:, tt:tt + 1],
                            in1=x1r[:, tk * D + os * 512:tk * D + (os + 1) * 512],
                            op0=ALU.mult, op1=ALU.add)
                    v.reduce_sum(out=lnt[:, 0:1], in_=rs2, axis=AX.X)
                    v.tensor_scalar_mul(out=lnt[:, 0:1], in0=lnt[:, 0:1], scalar1=1.0 / D)
                    i1 = v.tensor_scalar_sub(out=rs2, in0=rs2, scalar1=lnt[:, 0:1])
                    i1.then_inc(ctx.dv.h, 1)
                    ctx._note_sync(i1, ctx.dv.h, me["dv"] + 3 * tk + 1)
                    v.scalar_tensor_tensor(out=tmp[:], in0=rs2, scalar=1.0, in1=rs2,
                                           op0=ALU.mult, op1=ALU.mult,
                                           accum_out=lnt[:, 1:2])
                    i2 = v.tensor_scalar_mul(out=lnt[:, 4 + tk:5 + tk],
                                             in0=lnt[:, 1:2], scalar1=1.0 / D)
                    i2.then_inc(ctx.dv.h, 2)
                    ctx._note_sync(i2, ctx.dv.h, ev_dv[tk])
            ctx.sec("vector", e_ev)

            def e_sq(s, me=me):
                for tk in range(2):
                    s.wait_ge(ctx.dv.h, ev_dv[tk])
                    a = s.activation(out=lnt[:, 2 + tk:3 + tk], in_=lnt[:, 4 + tk:5 + tk],
                                     func=AF.Sqrt, bias=epsln[:])
                    a.then_inc(ctx.sc.h, 1)
                    ctx._note_sync(a, ctx.sc.h, sq2[tk])
            ctx.sec("scalar", e_sq)

            def e_out(v, tts=tts, pys=dict(prev["y_st"])):
                for tk, tt in enumerate(tts):
                    v.wait_ge(ctx.sc.h, sq2[tk])
                    if pys[tk]:
                        v.wait_ge(ctx.sy.h, pys[tk])
                    v.reciprocal(out=lnt[:, 0:1], in_=lnt[:, 2 + tk:3 + tk])
                    ev = v.tensor_scalar_mul(out=yt[:, tk * D:(tk + 1) * D],
                                             in0=r1[:, tk * D:(tk + 1) * D],
                                             scalar1=lnt[:, 0:1])
                    ev.then_inc(ctx.dv.h, 1)
                    ctx._note_sync(ev, ctx.dv.h, y_dv[tk])
            ctx.sec("vector", e_out)

            def e_st(sy, tts=tts):
                for tk, tt in enumerate(tts):
                    sy.wait_ge(ctx.dv.h, y_dv[tk])
                    sy.dma_start(out=y_d[tt * 128:(tt + 1) * 128, :],
                                 in_=yt[:, tk * D:(tk + 1) * D]).then_inc(ctx.sy.h, 16)
            ctx.sec("sync", e_st)
            prev["y_st"][0] = y_st[1]
            prev["y_st"][1] = y_st[1]
        prev["dev"] = tilesched[3]["dev"]

    if "x1" in taps:
        def fx(sy):
            m = ctx.marks()
            ctx.wait_marks(sy, m, ("pe", "dv", "sc", "gp", "ld", "lw",
                                   "la0", "la1", "sx0", "sx1", "sy"))
            sy.wait_ge(ctx.st.h, m["st"])
            ctx.st.inc(sy.dma_start(out=tap_d["x1"], in_=x1_d), 16)
        ctx.sec("sync", fx)
    if "hsc" in taps:
        def fh(sy):
            m = ctx.marks()
            ctx.wait_marks(sy, m, ("pe", "dv", "sc", "gp", "ld", "lw",
                                   "la0", "la1", "sx0", "sx1", "sy"))
            sy.wait_ge(ctx.st.h, m["st"])
            ctx.st.inc(sy.dma_start(out=tap_d["hsc"], in_=h_d), 16)
        ctx.sec("sync", fh)


def make_inputs(L0_attn_x, L0_attn_qW, L0_attn_qb, L0_attn_kW, L0_attn_kb,
                L0_attn_vW, L0_attn_vb, L0_attn_oW, L0_attn_ob, L0_attn_J,
                L0_ffn_w_gate, L0_ffn_w_up, L0_ffn_w_down):
    import ml_dtypes
    common = {
        "qW": np.ascontiguousarray(np.asarray(L0_attn_qW, dtype=np.float32)),
        "kW": np.ascontiguousarray(np.asarray(L0_attn_kW, dtype=np.float32)),
        "vW": np.ascontiguousarray(np.asarray(L0_attn_vW, dtype=np.float32)),
        "oW": np.ascontiguousarray(np.asarray(L0_attn_oW, dtype=np.float32)),
        "qb": np.ascontiguousarray(np.asarray(L0_attn_qb, dtype=np.float32)),
        "kb": np.ascontiguousarray(np.asarray(L0_attn_kb, dtype=np.float32)),
        "vb": np.ascontiguousarray(np.asarray(L0_attn_vb, dtype=np.float32)),
        "ob": np.ascontiguousarray(np.asarray(L0_attn_ob, dtype=np.float32)),
        "J": np.ascontiguousarray(np.asarray(L0_attn_J, dtype=np.float32)),
        "wgT": np.ascontiguousarray(np.asarray(L0_ffn_w_gate, dtype=np.float32).T),
        "wuT": np.ascontiguousarray(np.asarray(L0_ffn_w_up, dtype=np.float32).T),
        "wdT": np.ascontiguousarray(np.asarray(L0_ffn_w_down, dtype=np.float32).T),
        "idb": np.eye(128, dtype=ml_dtypes.bfloat16),
        "idf": np.eye(128, dtype=np.float32),
    }
    xs = np.asarray(L0_attn_x, dtype=np.float32)
    return [dict(common, x=np.ascontiguousarray(xs[i])) for i in range(xs.shape[0])]


def kernel(L0_attn_x, L0_attn_qW, L0_attn_qb, L0_attn_kW, L0_attn_kb,
           L0_attn_vW, L0_attn_vb, L0_attn_oW, L0_attn_ob, L0_attn_J,
           L0_ffn_w_gate, L0_ffn_w_up, L0_ffn_w_down):
    nc = build()
    in_maps = make_inputs(L0_attn_x, L0_attn_qW, L0_attn_qb, L0_attn_kW,
                          L0_attn_kb, L0_attn_vW, L0_attn_vb, L0_attn_oW,
                          L0_attn_ob, L0_attn_J, L0_ffn_w_gate, L0_ffn_w_up,
                          L0_ffn_w_down)
    res = run_bass_kernel_spmd(nc, in_maps, core_ids=list(range(len(in_maps))))
    return np.stack([r["y"] for r in res.results], axis=0)


# version 18 (final)
# speedup vs baseline: 1.1899x; 1.0595x over previous
import numpy as np
from contextlib import ExitStack
import concourse.bass as bass
import concourse.mybir as mybir
from concourse.bass_utils import run_bass_kernel_spmd

F32 = mybir.dt.float32
BF16 = mybir.dt.bfloat16
I32 = mybir.dt.int32
AF = mybir.ActivationFunctionType
ALU = mybir.AluOpType
AX = mybir.AxisListType

T = 2048
D = 1024
H = 16
INNER = 4096
NTT = T // 128
NDC = D // 128
NIC = INNER // 128
MAGIC = 12582912.0
EPS_LN = 1e-5
EPS_QN = 1e-6
NELEM_W = D * INNER

SIGN_TABLE = {
    0: [(0, 1.0), (1, -1.0), (2, -1.0), (3, -1.0)],
    1: [(1, 1.0), (0, 1.0), (3, -1.0), (2, 1.0)],
    2: [(2, 1.0), (3, 1.0), (0, 1.0), (1, -1.0)],
    3: [(3, 1.0), (2, -1.0), (1, 1.0), (0, 1.0)],
}


class Ctr:
    def __init__(self, stack, nc, name, ctx=None):
        self.h = stack.enter_context(nc.semaphore(name))
        self.n = 0
        self.ctx = ctx

    def inc(self, inst, k=1):
        inst.then_inc(self.h, k)
        self.n += k
        if self.ctx is not None:
            self.ctx._note_inc(inst, self)
        return self.n


_SERIAL_OPS = frozenset((
    "tensor_copy", "tensor_tensor", "tensor_add", "tensor_sub", "tensor_mul",
    "tensor_max", "tensor_scalar", "tensor_single_scalar", "tensor_scalar_add",
    "tensor_scalar_sub", "tensor_scalar_mul", "tensor_scalar_max",
    "tensor_scalar_min", "tensor_relu", "tensor_reduce", "reduce_max",
    "reduce_sum", "scalar_tensor_tensor", "tensor_tensor_scan", "reciprocal",
    "memset", "memzero", "activation", "sign", "square", "sqrt", "mul",
    "copy", "add", "select", "copy_predicated", "transpose", "iota",
    "partition_broadcast", "partition_all_reduce", "engine_nop", "pool",
    "pool_avg", "pool_max", "bn_stats", "bn_aggr", "max", "stream_shuffle",
))


class SerialEng:
    def __init__(self, ctr):
        self._eng = None
        self._ctr = ctr
        self._pend = None
        self._sync = None

    def _rebind(self, eng):
        self._eng = eng
        return self

    def _user_inc(self, inst, ctr):
        if self._pend is inst:
            self._pend = None
            self._sync = (ctr.h, ctr.n)

    def _fence(self):
        if self._pend is not None:
            v = self._ctr.inc(self._pend)
            self._sync = (self._ctr.h, v)
            self._pend = None
        if self._sync is not None:
            self._eng.wait_ge(*self._sync)
            self._sync = None

    def __getattr__(self, name):
        f = getattr(self._eng, name)
        if name not in _SERIAL_OPS:
            return f

        def g(*a, **k):
            self._fence()
            inst = f(*a, **k)
            self._pend = inst
            return inst
        return g


class Ctx:
    def __init__(self, nc, block, stack):
        self.nc = nc
        self.block = block
        self.stack = stack
        for c in ("pe", "dv", "sc", "gp", "ld", "lw", "st",
                  "hdr", "fb", "qw", "lx0", "lx1",
                  "rs", "rl0", "rl1", "la0", "la1",
                  "sx0", "sx1", "sy",
                  "wa", "wv", "wp", "lg", "sg", "sg2", "w0", "w1", "wr",
                  "dvs", "scs", "gps"):
            setattr(self, c, Ctr(stack, nc, "s_" + c, ctx=self))
        self._proxies = {"vector": SerialEng(self.dvs),
                         "scalar": SerialEng(self.scs),
                         "gpsimd": SerialEng(self.gps)}

    def _note_inc(self, inst, ctr):
        for p in self._proxies.values():
            p._user_inc(inst, ctr)

    def _note_sync(self, inst, h, v):
        for p in self._proxies.values():
            if p._pend is inst:
                p._pend = None
                p._sync = (h, v)

    def sec(self, eng, fn):
        if eng in self._proxies:
            getattr(self.block, eng)(lambda e: fn(self._proxies[eng]._rebind(e)))
        else:
            getattr(self.block, eng)(fn)

    def marks(self):
        return {c: getattr(self, c).n
                for c in ("pe", "dv", "sc", "gp", "ld", "lw", "st", "hdr", "fb",
                          "qw", "lx0", "lx1", "rs", "rl0", "rl1", "la0", "la1",
                          "sx0", "sx1", "sy", "wa", "wv", "wp", "lg", "sg", "sg2",
                          "w0", "w1", "wr")}

    def wait_marks(self, eng, m, which=("pe", "dv", "sc", "gp")):
        for w in which:
            if m[w] > 0:
                eng.wait_ge(getattr(self, w).h, m[w])

    def barrier(self, engines=("sync", "vector", "scalar", "tensor", "gpsimd"),
                on=("pe", "dv", "sc", "gp", "ld", "lw", "hdr", "fb", "qw",
                    "lx0", "lx1", "rs", "wa", "wv", "wp")):
        m = self.marks()
        for e in engines:
            self.sec(e, lambda eng, m=m: self.wait_marks(eng, m, on))


def sb(ctx, name, shape, dt, stack=None):
    return (stack or ctx.stack).enter_context(ctx.nc.sbuf_tensor(name, shape, dt))


def psum(ctx, name, shape, dt=F32, stack=None):
    return (stack or ctx.stack).enter_context(ctx.nc.psum_tensor(name, shape, dt))


def _tap(ctx, dst, src_ap):
    def f(sy):
        m = ctx.marks()
        ctx.wait_marks(sy, m, ("pe", "dv", "sc", "gp", "ld", "lw", "hdr",
                               "fb", "qw", "lx0", "lx1", "rs", "rl0", "rl1",
                               "la0", "la1", "sx0", "sx1", "sy",
                               "wa", "wv", "wp"))
        ctx.st.inc(sy.dma_start(out=dst, in_=src_ap), 16)
    ctx.sec("sync", f)


def build(n_heads=H, taps=(), phase_stop=None):
    nc = bass.Bass("TRN2", target_bir_lowering=False, detect_race_conditions=False)
    taps = set(taps)
    E = {}

    for nm, shape in (("x", [T, D]), ("qW", [4, 256, 256]), ("kW", [4, 256, 256]),
                      ("vW", [4, 256, 256]), ("oW", [4, 256, 256]),
                      ("qb", [4, 256]), ("kb", [4, 256]), ("vb", [4, 256]),
                      ("ob", [4, 256]), ("J", [D]), ("wgT", [D, INNER]),
                      ("wuT", [D, INNER]), ("wdT", [INNER, D])):
        E[nm + "_d"] = nc.dram_tensor(nm, shape, F32, kind="ExternalInput").ap()
    E["idb_d"] = nc.dram_tensor("idb", [128, 128], BF16, kind="ExternalInput").ap()
    E["idf_d"] = nc.dram_tensor("idf", [128, 128], F32, kind="ExternalInput").ap()
    E["y_d"] = nc.dram_tensor("y", [T, D], F32, kind="ExternalOutput").ap()

    E["fbs_d"] = nc.dram_tensor("fbs", [4, D], F32, kind="Internal").ap()
    E["tg_d"] = nc.dram_tensor("tern_g", [D, INNER], BF16, kind="Internal").ap()
    E["tu_d"] = nc.dram_tensor("tern_u", [D, INNER], BF16, kind="Internal").ap()
    E["td_d"] = nc.dram_tensor("tern_d", [INNER, D], BF16, kind="Internal").ap()
    E["h_d"] = nc.dram_tensor("h_sc", [T, INNER], F32, kind="Internal").ap()
    E["rscr_d"] = nc.dram_tensor("rscr", [64, 512], F32, kind="Internal").ap()
    E["x1_d"] = nc.dram_tensor("x1_sc", [T, D], F32, kind="Internal").ap()
    E["wsum_d"] = nc.dram_tensor("wsum", [3, 128], F32, kind="Internal").ap()
    E["sc_d"] = nc.dram_tensor("wsc_sc", [3, 2], F32, kind="Internal").ap()

    tap_shapes = {
        "xT": ([128, NDC * T], BF16), "qT": ([128, NDC * T], BF16),
        "kT": ([128, NDC * T], BF16), "Vaug": ([128, NTT * 1040], BF16),
        "attnoutT": ([128, NDC * T], BF16), "x1": ([T, D], F32),
        "PT0": ([128, 8 * T], BF16), "cb": ([128, 1], F32),
        "fbT": ([128, 32], F32), "WTo": ([128, NDC * D], BF16),
        "terng": ([D, INNER], BF16), "ternd": ([INNER, D], BF16),
        "wsc": ([128, 3], F32), "hsc": ([T, INNER], F32),
        "wacc": ([128, 2], F32), "wrow": ([1, 128], F32), "ws11": ([1, 2], F32),
    }
    E["tap_d"] = {name: nc.dram_tensor("tap_" + name, *tap_shapes[name],
                                       kind="ExternalOutput").ap() for name in taps}
    E["taps"] = taps

    with ExitStack() as st0:
        block = st0.enter_context(nc.Block())
        ctx = Ctx(nc, block, st0)

        for nm, shape, dt in (
                ("ones128", [128, 128], F32), ("identb", [128, 128], BF16),
                ("identf", [128, 128], F32), ("cb128", [128, 1], F32),
                ("epsqn", [128, 1], F32), ("epsln", [128, 1], F32),
                ("fbT", [128, 32], F32), ("fbrows", [1, 2 * D], F32),
                ("c11", [1, 1], F32), ("wsc128", [128, 3], F32),
                ("swq128", [128, 3], F32),
                ("WT_o", [128, NDC * D], BF16),
                ("xT", [128, NDC * T], BF16)):
            E[nm] = sb(ctx, nm, shape, dt)
        E["attnoutT"] = E["xT"]

        stA = st0.enter_context(ExitStack())
        E["qT"] = sb(ctx, "qT", [128, NDC * T], BF16, stA)
        E["kT"] = sb(ctx, "kT", [128, NDC * T], BF16, stA)
        E["Vaug"] = sb(ctx, "Vaug", [128, NTT * 1040], BF16, stA)

        stWQ = st0.enter_context(ExitStack())
        CW = 1024
        E["wq_bufs"] = dict(
            CW=CW,
            wbuf=sb(ctx, "wq_wbuf", [128, 2 * CW], F32, stWQ),
            tbuf=sb(ctx, "wq_tbuf", [128, 2 * CW], BF16, stWQ),
            accs=sb(ctx, "wq_acc", [128, 2], F32, stWQ),
            rowx=sb(ctx, "wq_row", [1, 128], F32, stWQ),
            s11=sb(ctx, "wq_s11", [1, 2], F32, stWQ),
        )
        build_p0(ctx, E, st0)
        if phase_stop == "P0":
            finish(ctx); return nc
        stPW = st0.enter_context(ExitStack())
        build_proj(ctx, E, stPW)
        if phase_stop == "PROJ":
            finish(ctx); return nc
        wq_thunks = build_wq(ctx, E, stWQ, stPW)
        if phase_stop == "WQ":
            for e, f in wq_thunks:
                ctx.sec(e, f)
            finish(ctx); return nc
        m = ctx.marks()
        for e in ("vector", "scalar", "tensor", "gpsimd"):
            ctx.sec(e, lambda eng, m=m: ctx.wait_marks(
                eng, m, ("pe", "dv", "sc", "gp")))
        stPW.close()

        stT = st0.enter_context(ExitStack())
        build_attn(ctx, E, stT, n_heads, wq_thunks)
        ctx.barrier(on=("pe", "dv", "sc", "gp", "rs", "rl0", "rl1",
                        "wv", "wa", "wp", "lw", "wr"))
        if phase_stop == "ATTN":
            finish(ctx); return nc
        stT.close()
        stWQ.close()
        stA.close()

        stF = st0.enter_context(ExitStack())
        build_ffn(ctx, E, stF)
        finish(ctx)
    return nc


def finish(ctx):
    def f(sy):
        m = ctx.marks()
        ctx.wait_marks(sy, m, ("pe", "dv", "sc", "gp", "ld", "lw", "hdr", "fb",
                               "qw", "lx0", "lx1", "rs", "rl0", "rl1",
                               "la0", "la1", "sx0", "sx1", "sy",
                               "wa", "wv", "wp", "lg", "sg", "sg2", "w0", "w1", "wr"))
        sy.wait_ge(ctx.st.h, ctx.st.n)
    ctx.sec("sync", f)


def build_p0(ctx, E, stX):
    taps, tap_d = E["taps"], E["tap_d"]
    ones128, identb, identf = E["ones128"], E["identb"], E["identf"]
    cb128, fbT, fbrows, c11 = E["cb128"], E["fbT"], E["fbrows"], E["c11"]
    Vaug, xT = E["Vaug"], E["xT"]

    with ExitStack() as s0:
        xld = sb(ctx, "xld", [128, 2 * D], F32, s0)
        xbf = sb(ctx, "xbf", [128, 2 * D], BF16, s0)
        fb32 = sb(ctx, "fb32", [32, 128], F32, s0)
        brow = sb(ctx, "brow", [1, D], F32, s0)
        jrow = sb(ctx, "jrow", [1, D], F32, s0)
        braw = sb(ctx, "braw", [1, 4 * D], F32, s0)
        pxt = psum(ctx, "pxt", [128, 2 * D], BF16, s0)
        pfb = psum(ctx, "pfb", [128, 32], F32, s0)
        pbc = psum(ctx, "pbc", [128, 1], F32, s0)

        cast_mark, evacT_mark = {}, {}
        n = ctx.dv.n + 2
        for tt in range(NTT):
            n += 1
            cast_mark[tt] = n
            if tt >= 1:
                n += 1
                evacT_mark[tt - 1] = n
        n += 1
        evacT_mark[NTT - 1] = n
        cb_mark, fbT_mark = n + 1, n + 2
        pe0 = ctx.pe.n
        xpose_mark = {tt: pe0 + tt + 1 for tt in range(NTT)}
        cbc_mark, pfb_mark = pe0 + NTT + 1, pe0 + NTT + 2
        HDRN = 7 * 16

        def xwait(eng, tt):
            lx = ctx.lx0 if tt % 2 == 0 else ctx.lx1
            eng.wait_ge(lx.h, 16 * (tt // 2 + 1))

        def sy1(sy):
            ctx.hdr.inc(sy.dma_start(out=jrow[:], in_=E["J_d"].rearrange("(p n) -> p n", p=1)), 16)
            for P, nm in enumerate(("qb", "kb", "vb", "ob")):
                ctx.hdr.inc(sy.dma_start(out=braw[:, P * D:(P + 1) * D],
                                         in_=E[nm + "_d"].rearrange("(o a) b -> o (a b)", o=1)), 16)
            ctx.hdr.inc(sy.dma_start(out=identb[:], in_=E["idb_d"]), 16)
            ctx.hdr.inc(sy.dma_start(out=identf[:], in_=E["idf_d"]), 16)
            for tt in range(NTT):
                if tt >= 2:
                    sy.wait_ge(ctx.dv.h, cast_mark[tt - 2])
                lx = ctx.lx0 if tt % 2 == 0 else ctx.lx1
                lx.inc(sy.dma_start(out=xld[:, (tt % 2) * D:(tt % 2 + 1) * D],
                                    in_=E["x_d"][tt * 128:(tt + 1) * 128, :]), 16)
        ctx.sec("sync", sy1)

        def sc1(s):
            s.wait_ge(ctx.hdr.h, HDRN)
            ctx.sc.inc(s.activation(out=jrow[:], in_=jrow[:], func=AF.Sin))
        ctx.sec("scalar", sc1)
        sc_sin = ctx.sc.n

        def ve1(v):
            v.memset(ones128[:], 1.0)
            v.memset(E["epsqn"][:], EPS_QN)
            v.memset(E["epsln"][:], EPS_LN)
            v.memset(Vaug[:], 1.0)
            v.wait_ge(ctx.sc.h, sc_sin)
            v.reduce_sum(out=c11[:], in_=jrow[:], axis=AX.X)
            got = ctx.dv.inc(v.tensor_scalar(out=c11[:], in0=c11[:], scalar1=0.1 / D,
                                             scalar2=-8.0, op0=ALU.mult, op1=ALU.add))
            assert got == 1
            v.wait_ge(ctx.hdr.h, HDRN)
            for P in range(4):
                base = P * D
                b = lambda w: braw[:, base + w * 256: base + (w + 1) * 256]
                t01, t01m = brow[:, 0:256], brow[:, 256:512]
                t23, t23m = brow[:, 512:768], brow[:, 768:1024]
                v.tensor_add(out=t01, in0=b(0), in1=b(1))
                v.tensor_sub(out=t01m, in0=b(0), in1=b(1))
                v.tensor_add(out=t23, in0=b(2), in1=b(3))
                v.tensor_sub(out=t23m, in0=b(2), in1=b(3))
                v.tensor_sub(out=b(0), in0=t01m, in1=t23)
                v.tensor_add(out=b(1), in0=t01, in1=t23m)
                v.tensor_add(out=b(2), in0=t01m, in1=t23)
                last = v.tensor_sub(out=b(3), in0=t01, in1=t23m)
            ctx.dv.inc(last)
            v.tensor_copy(out=fbrows[:, 0:D], in_=braw[:, 2 * D:3 * D])
            v.tensor_copy(out=fbrows[:, D:2 * D], in_=braw[:, 3 * D:4 * D])
            xTr = xT.ap().rearrange("p (dc t) -> p dc t", dc=NDC)
            pxtr = pxt.ap().rearrange("p (b dc t) -> p b dc t", b=2, dc=NDC)

            def evac(tt):
                v.wait_ge(ctx.pe.h, xpose_mark[tt])
                got = ctx.dv.inc(v.tensor_copy(out=xTr[:, :, tt * 128:(tt + 1) * 128],
                                               in_=pxtr[:, tt % 2, :, :]))
                assert got == evacT_mark[tt], (got, evacT_mark[tt])
            for tt in range(NTT):
                xwait(v, tt)
                got = ctx.dv.inc(v.tensor_copy(out=xbf[:, (tt % 2) * D:(tt % 2 + 1) * D],
                                               in_=xld[:, (tt % 2) * D:(tt % 2 + 1) * D]))
                assert got == cast_mark[tt], (got, cast_mark[tt])
                if tt >= 1:
                    evac(tt - 1)
            evac(NTT - 1)
            v.wait_ge(ctx.pe.h, cbc_mark)
            got = ctx.dv.inc(v.tensor_copy(out=cb128[:], in_=pbc[:]))
            assert got == cb_mark
            v.wait_ge(ctx.pe.h, pfb_mark)
            got = ctx.dv.inc(v.tensor_copy(out=fbT[:], in_=pfb[:]))
            assert got == fbT_mark
        ctx.sec("vector", ve1)

        def te1(t):
            for tt in range(NTT):
                t.wait_ge(ctx.dv.h, cast_mark[tt])
                if tt >= 2:
                    t.wait_ge(ctx.dv.h, evacT_mark[tt - 2])
                for dc in range(NDC):
                    mm = t.transpose(
                        out=pxt[:, (tt % 2) * D + dc * 128:(tt % 2) * D + (dc + 1) * 128],
                        in_=xbf[:, (tt % 2) * D + dc * 128:(tt % 2) * D + (dc + 1) * 128],
                        identity=identb[:])
                got = ctx.pe.inc(mm)
                assert got == xpose_mark[tt]
            t.wait_ge(ctx.dv.h, 1)
            got = ctx.pe.inc(t.matmul(out=pbc[:], lhsT=ones128[0:1, :], rhs=c11[:],
                                      start=True, stop=True))
            assert got == cbc_mark
            t.wait_ge(ctx.fb.h, 16)
            got = ctx.pe.inc(t.transpose(out=pfb[:], in_=fb32[:], identity=identf[0:32, 0:32]))
            assert got == pfb_mark
        ctx.sec("tensor", te1)

        def sy2(sy):
            sy.wait_ge(ctx.dv.h, 2)
            base = ctx.st.n
            for P in range(4):
                ctx.st.inc(sy.dma_start(out=E["fbs_d"][P:P + 1, :],
                                        in_=braw[:, P * D:(P + 1) * D]), 16)
            sy.wait_ge(ctx.st.h, base + 64)
            ctx.fb.inc(sy.dma_start(out=fb32[:],
                                    in_=E["fbs_d"].rearrange("a (c p) -> (a c) p", p=128)), 16)
        ctx.sec("sync", sy2)

        if "xT" in taps:
            _tap(ctx, tap_d["xT"], xT[:])
        if "cb" in taps:
            _tap(ctx, tap_d["cb"], cb128[:])
        if "fbT" in taps:
            _tap(ctx, tap_d["fbT"], fbT[:])
        ctx.barrier()


def build_wq(ctx, E, s0, sP):
    wsc128, swq128 = E["wsc128"], E["swq128"]

    B = E["wq_bufs"]
    CW = B["CW"]
    wbuf, tbuf = B["wbuf"], B["tbuf"]
    accs, rowx, s11 = B["accs"], B["rowx"], B["s11"]
    wsum_d, sc_d = E["wsum_d"], E["sc_d"]

    mats = []
    for wk, tk_ in (("wgT_d", "tg_d"), ("wuT_d", "tu_d"), ("wdT_d", "td_d")):
        w_d, t_d = E[wk], E[tk_]
        rows, cols = w_d.shape
        chunks = [(r, c) for r in range(rows // 128) for c in range(cols // CW)]
        mats.append((w_d, t_d, chunks))

    thunks = []
    sec = lambda e, f: thunks.append((e, f))

    sc_done = {}
    prev_p1 = {0: 0, 1: 0}
    prev_acc_store = 0
    for mi, (w_d, t_d, chunks) in enumerate(mats):
        nch = len(chunks)
        p1_mark = {ch: ctx.wv.n + ch + 1 for ch in range(nch)}
        ctx.wv.n += nch
        ld_mark = {}
        for ch in range(nch):
            lp = (ctx.w0, ctx.w1)[ch % 2]
            ld_mark[ch] = lp.n + 16
            lp.n += 16

        for ch, (r, c) in enumerate(chunks):
            def f_l(sy, ch=ch, r=r, c=c, w_d=w_d, p1_mark=p1_mark,
                    pp=dict(prev_p1)):
                if ch >= 2:
                    sy.wait_ge(ctx.wv.h, p1_mark[ch - 2])
                elif pp[ch % 2]:
                    sy.wait_ge(ctx.wv.h, pp[ch % 2])
                sy.dma_start(out=wbuf[:, (ch % 2) * CW:(ch % 2 + 1) * CW],
                             in_=w_d[r * 128:(r + 1) * 128, c * CW:(c + 1) * CW]
                             ).then_inc((ctx.w0, ctx.w1)[ch % 2].h, 16)
            sec("gpsimd", f_l)

            def f_s(v, ch=ch, ld_mark=ld_mark, p1_mark=p1_mark,
                    pas=prev_acc_store):
                v.wait_ge((ctx.w0, ctx.w1)[ch % 2].h, ld_mark[ch])
                if ch == 0 and pas:
                    v.wait_ge(ctx.wr.h, pas)
                src = wbuf[:, (ch % 2) * CW:(ch % 2 + 1) * CW]
                v.scalar_tensor_tensor(out=src, in0=src, scalar=-1.0,
                                       in1=src, op0=ALU.mult, op1=ALU.max,
                                       accum_out=accs[:, 1:2])
                if ch == 0:
                    inst = v.tensor_copy(out=accs[:, 0:1], in_=accs[:, 1:2])
                else:
                    inst = v.tensor_add(out=accs[:, 0:1], in0=accs[:, 0:1],
                                        in1=accs[:, 1:2])
                inst.then_inc(ctx.wv.h, 1)
                ctx._note_sync(inst, ctx.wv.h, p1_mark[ch])
            sec("vector", f_s)
        prev_p1 = {0: p1_mark[nch - 2], 1: p1_mark[nch - 1]}

        acc_done = p1_mark[nch - 1]
        wr0 = ctx.wr.n
        m_strow, m_ldrow = wr0 + 16, wr0 + 32
        s11_mark = ctx.wv.n + 1
        ctx.wv.n += 1
        m_sts11, m_bc = wr0 + 48, wr0 + 80
        ctx.wr.n = m_bc
        prev_acc_store = m_strow

        def f_rt1(sy, mi=mi, acc_done=acc_done, m_strow=m_strow, wr0=wr0):
            sy.wait_ge(ctx.wv.h, acc_done)
            if wr0:
                sy.wait_ge(ctx.wr.h, wr0)
            sy.dma_start(
                out=wsum_d[mi:mi + 1, :].rearrange("a b -> b a"),
                in_=accs[:, 0:1]).then_inc(ctx.wr.h, 16)
            sy.wait_ge(ctx.wr.h, m_strow)
            sy.dma_start(out=rowx[:], in_=wsum_d[mi:mi + 1, :]
                         ).then_inc(ctx.wr.h, 16)
        sec("gpsimd", f_rt1)

        def f_sum(v, m_ldrow=m_ldrow, s11_mark=s11_mark):
            v.wait_ge(ctx.wr.h, m_ldrow)
            v.reduce_sum(out=s11[:, 0:1], in_=rowx[:], axis=AX.X)
            v.tensor_scalar(out=s11[:, 0:1], in0=s11[:, 0:1],
                            scalar1=1.0 / NELEM_W, scalar2=1e-5,
                            op0=ALU.mult, op1=ALU.max)
            inst = v.reciprocal(out=s11[:, 1:2], in_=s11[:, 0:1])
            inst.then_inc(ctx.wv.h, 1)
            ctx._note_sync(inst, ctx.wv.h, s11_mark)
        sec("vector", f_sum)

        def f_rt2(sy, mi=mi, s11_mark=s11_mark, m_sts11=m_sts11):
            sy.wait_ge(ctx.wv.h, s11_mark)
            sy.dma_start(out=sc_d[mi:mi + 1, :], in_=s11[:]
                         ).then_inc(ctx.wr.h, 16)
            sy.wait_ge(ctx.wr.h, m_sts11)
            _, bap0 = bass.broadcast_tensor_aps(wsc128[:, mi:mi + 1],
                                                sc_d[mi:mi + 1, 0:1])
            sy.dma_start(out=wsc128[:, mi:mi + 1], in_=bap0
                         ).then_inc(ctx.wr.h, 16)
            _, bap1 = bass.broadcast_tensor_aps(swq128[:, mi:mi + 1],
                                                sc_d[mi:mi + 1, 1:2])
            sy.dma_start(out=swq128[:, mi:mi + 1], in_=bap1
                         ).then_inc(ctx.wr.h, 16)
        sec("gpsimd", f_rt2)
        sc_done[mi] = m_bc

    p1_final = dict(prev_p1)
    E["wq_scales_wr"] = ctx.wr.n

    prev_q = 0
    for mi, (w_d, t_d, chunks) in enumerate(mats):
        nch = len(chunks)
        st_base, st_base2 = ctx.sg.n, ctx.sg2.n
        q_mark = {ch: ctx.wv.n + ch + 1 for ch in range(nch)}
        ctx.wv.n += nch
        s_mark2 = {}
        for j in range(nch):
            sp = (ctx.sg, ctx.sg2)[j % 2]
            s_mark2[j] = sp.n + 16
            sp.n += 16
        ld2_mark = {}
        for ch in range(nch):
            lp = (ctx.w0, ctx.w1)[ch % 2]
            ld2_mark[ch] = lp.n + 16
            lp.n += 16

        for ch, (r, c) in enumerate(chunks):
            def f_ls(sy, ch=ch, r=r, c=c, chunks=chunks, q_mark=q_mark,
                     nch=nch, t_d=t_d, w_d=w_d, prev_q=prev_q):
                if ch >= 2:
                    sy.wait_ge(ctx.wv.h, q_mark[ch - 2])
                elif prev_q:
                    sy.wait_ge(ctx.wv.h, prev_q)
                else:
                    sy.wait_ge(ctx.wv.h, p1_final[ch % 2])
                sy.dma_start(out=wbuf[:, (ch % 2) * CW:(ch % 2 + 1) * CW],
                             in_=w_d[r * 128:(r + 1) * 128, c * CW:(c + 1) * CW]
                             ).then_inc((ctx.w0, ctx.w1)[ch % 2].h, 16)
                if ch >= 1:
                    r0, c0 = chunks[ch - 1]
                    sy.wait_ge(ctx.wv.h, q_mark[ch - 1])
                    sy.dma_start(out=t_d[r0 * 128:(r0 + 1) * 128, c0 * CW:(c0 + 1) * CW],
                                 in_=tbuf[:, ((ch - 1) % 2) * CW:((ch - 1) % 2 + 1) * CW]
                                 ).then_inc((ctx.sg, ctx.sg2)[(ch - 1) % 2].h, 16)
                if ch == nch - 1:
                    sy.wait_ge(ctx.wv.h, q_mark[ch])
                    sy.dma_start(out=t_d[r * 128:(r + 1) * 128, c * CW:(c + 1) * CW],
                                 in_=tbuf[:, (ch % 2) * CW:(ch % 2 + 1) * CW]
                                 ).then_inc((ctx.sg, ctx.sg2)[ch % 2].h, 16)
            sec("gpsimd", f_ls)

            def f_q(v, ch=ch, mi=mi, ld2_mark=ld2_mark, sc_done=sc_done,
                    nch=nch, st_base=st_base, st_base2=st_base2,
                    q_mark=q_mark, s_mark2=s_mark2):
                v.wait_ge((ctx.w0, ctx.w1)[ch % 2].h, ld2_mark[ch])
                if ch == 0:
                    v.wait_ge(ctx.wr.h, sc_done[mi])
                if ch >= 2:
                    v.wait_ge((ctx.sg, ctx.sg2)[ch % 2].h, s_mark2[ch - 2])
                else:
                    v.wait_ge(ctx.sg.h, st_base)
                    v.wait_ge(ctx.sg2.h, st_base2)
                src = wbuf[:, (ch % 2) * CW:(ch % 2 + 1) * CW]
                v.tensor_scalar(out=src, in0=src, scalar1=swq128[:, mi:mi + 1],
                                scalar2=MAGIC, op0=ALU.mult, op1=ALU.add)
                v.tensor_scalar(out=src, in0=src, scalar1=MAGIC, scalar2=1.0,
                                op0=ALU.subtract, op1=ALU.min)
                v.tensor_single_scalar(out=src, in_=src, scalar=-1.0, op=ALU.max)
                v.tensor_copy(out=src.bitcast(I32), in_=src.bitcast(I32))
                inst = v.tensor_copy(out=tbuf[:, (ch % 2) * CW:(ch % 2 + 1) * CW],
                                     in_=src)
                inst.then_inc(ctx.wv.h, 1)
                ctx._note_sync(inst, ctx.wv.h, q_mark[ch])
            sec("vector", f_q)
        prev_q = q_mark[nch - 1]

    E["tern_st_mark"] = (ctx.sg.n, ctx.sg2.n)
    return thunks


def build_proj(ctx, E, s0):
    taps, tap_d = E["taps"], E["tap_d"]
    xT, qT, kT, Vaug, WT_o = E["xT"], E["qT"], E["kT"], E["Vaug"], E["WT_o"]
    fbT, fbrows, ones128, epsqn = E["fbT"], E["fbrows"], E["ones128"], E["epsqn"]

    if True:
        qWs = sb(ctx, "qWs", [128, 4 * 256], F32, s0)
        WTx = sb(ctx, "WTx", [128, NDC * D], BF16, s0)
        sqs = sb(ctx, "sqs", [128, 4 * 512], F32, s0)
        nb = sb(ctx, "nb", [128, 512], F32, s0)
        invb = sb(ctx, "invb", [128, 512], F32, s0)
        vbbc = sqs.ap()[:, 0:D]
        pq = psum(ctx, "pq", [128, 4 * 512], F32, s0)
        pv = psum(ctx, "pv", [128, 2 * 512], F32, s0)
        pbb = pq.ap()[:, 0:512]

        for nm in ("oW", "qW", "kW", "vW"):
            pidx = {"qW": 0, "kW": 1, "vW": 2, "oW": 3}[nm]
            WT_dst = WT_o if nm == "oW" else WTx
            mld = ctx.marks()

            for wh in range(2):
                mld2 = ctx.marks()

                def ld_w(sy, w_d=E[nm + "_d"], wh=wh, mld=mld, mld2=mld2):
                    if wh == 0:
                        ctx.wait_marks(sy, mld, ("pe", "dv"))
                    else:
                        sy.wait_ge(ctx.dv.h, mld2["dv"])
                    ctx.qw.inc(sy.dma_start(
                        out=qWs.ap().rearrange("p (w dh o) -> p w dh o", w=2, dh=2),
                        in_=w_d.rearrange("w (dh p) o -> p w dh o",
                                          p=128)[:, wh * 2:(wh + 1) * 2]), 16)
                ctx.sec("sync", ld_w)
                mw = ctx.marks()

                def asm(v, WT_dst=WT_dst, wh=wh, mw=mw):
                    v.wait_ge(ctx.qw.h, mw["qw"])
                    if wh == 0:
                        v.wait_ge(ctx.pe.h, mw["pe"])
                    for dc in range(NDC):
                        c, dh = dc // 2, dc % 2
                        for p in range(4):
                            w, sign = SIGN_TABLE[p][c]
                            if w // 2 != wh:
                                continue
                            last = v.tensor_scalar_mul(
                                out=WT_dst[:, dc * D + p * 256:dc * D + (p + 1) * 256],
                                in0=qWs[:, ((w % 2) * 2 + dh) * 256:((w % 2) * 2 + dh + 1) * 256],
                                scalar1=sign)
                    ctx.dv.inc(last)
                ctx.sec("vector", asm)
            masm = ctx.marks()

            if nm == "oW":
                continue

            if nm in ("qW", "kW"):
                dstT = qT if nm == "qW" else kT
                sched = {}
                pe_n, sc_n, dv_n = ctx.pe.n, ctx.sc.n, ctx.dv.n
                for i, (g, ts_) in enumerate([(g, s_) for g in range(2) for s_ in range(4)]):
                    pe_n += 1
                    sc_n += 1
                    dv_n += 1
                    sc_n += 1
                    dv_n += 4
                    sched[(g, ts_)] = dict(mm=pe_n, sq=sc_n - 1, ssum=dv_n - 4,
                                           sqrt=sc_n, out=dv_n)

                def qk_te(t, masm=masm, sched=sched):
                    t.wait_ge(ctx.dv.h, masm["dv"])
                    for g in range(2):
                        for ts_ in range(4):
                            if (g, ts_) != (0, 0):
                                prev = (g, ts_ - 1) if ts_ > 0 else (g - 1, 3)
                                t.wait_ge(ctx.dv.h, sched[prev]["out"])
                            for jj in range(4):
                                j = g + 2 * jj
                                for dc in range(NDC):
                                    mm = t.matmul(
                                        out=pq[:, jj * 512:(jj + 1) * 512],
                                        lhsT=WTx[:, dc * D + j * 128:dc * D + (j + 1) * 128],
                                        rhs=xT[:, dc * T + ts_ * 512:dc * T + (ts_ + 1) * 512],
                                        start=(dc == 0), stop=(dc == NDC - 1))
                            got = ctx.pe.inc(mm)
                            assert got == sched[(g, ts_)]["mm"]
                ctx.sec("tensor", qk_te)

                def qk_sc(s, pidx=pidx, sched=sched):
                    for g in range(2):
                        for ts_ in range(4):
                            sd = sched[(g, ts_)]
                            s.wait_ge(ctx.pe.h, sd["mm"])
                            for jj in range(4):
                                j = g + 2 * jj
                                a = s.activation(out=sqs[:, jj * 512:(jj + 1) * 512],
                                                 in_=pq[:, jj * 512:(jj + 1) * 512],
                                                 func=AF.Square,
                                                 bias=fbT[:, pidx * 8 + j:pidx * 8 + j + 1])
                            got = ctx.sc.inc(a)
                            assert got == sd["sq"]
                            s.wait_ge(ctx.dv.h, sd["ssum"])
                            got = ctx.sc.inc(s.activation(out=nb[:], in_=sqs[:, 0:512],
                                                          func=AF.Sqrt, bias=epsqn[:]))
                            assert got == sd["sqrt"]
                ctx.sec("scalar", qk_sc)

                def qk_ve(v, pidx=pidx, dstT=dstT, sched=sched):
                    for g in range(2):
                        for ts_ in range(4):
                            sd = sched[(g, ts_)]
                            v.wait_ge(ctx.sc.h, sd["sq"])
                            v.tensor_add(out=sqs[:, 0:512], in0=sqs[:, 0:512],
                                         in1=sqs[:, 512:1024])
                            v.tensor_add(out=sqs[:, 0:512], in0=sqs[:, 0:512],
                                         in1=sqs[:, 1024:1536])
                            got = ctx.dv.inc(v.tensor_add(out=sqs[:, 0:512],
                                                          in0=sqs[:, 0:512],
                                                          in1=sqs[:, 1536:2048]))
                            assert got == sd["ssum"]
                            v.wait_ge(ctx.sc.h, sd["sqrt"])
                            v.reciprocal(out=invb[:], in_=nb[:])
                            for jj in range(4):
                                j = g + 2 * jj
                                got = ctx.dv.inc(v.scalar_tensor_tensor(
                                    out=dstT[:, j * T + ts_ * 512:j * T + (ts_ + 1) * 512],
                                    in0=pq[:, jj * 512:(jj + 1) * 512],
                                    scalar=fbT[:, pidx * 8 + j:pidx * 8 + j + 1],
                                    in1=invb[:], op0=ALU.add, op1=ALU.mult))
                            assert got == sd["out"]
                ctx.sec("vector", qk_ve)
            else:
                mv0 = ctx.marks()
                bb_pe = {0: mv0["pe"] + 1, 1: mv0["pe"] + 2}
                bb_dv = {0: mv0["dv"] + 1, 1: mv0["dv"] + 2}

                def vb_te(t, mv0=mv0, bb_dv=bb_dv):
                    t.wait_ge(ctx.dv.h, mv0["dv"])
                    for os in range(2):
                        if os == 1:
                            t.wait_ge(ctx.dv.h, bb_dv[0])
                        ctx.pe.inc(t.matmul(
                            out=pbb[:], lhsT=ones128[0:1, :],
                            rhs=fbrows[:, os * 512:(os + 1) * 512],
                            start=True, stop=True))
                ctx.sec("tensor", vb_te)

                def vb_ve(v, bb_pe=bb_pe):
                    for os in range(2):
                        v.wait_ge(ctx.pe.h, bb_pe[os])
                        ctx.dv.inc(v.tensor_copy(out=vbbc[:, os * 512:(os + 1) * 512],
                                                 in_=pbb[:]))
                ctx.sec("vector", vb_ve)
                mvb = ctx.marks()

                Vr = Vaug.ap().rearrange("p (tt h c) -> p tt h c", tt=NTT, h=H)
                vb_r = vbbc.rearrange("p (os h c) -> p os h c", os=2, c=64)
                pv_r = pv.ap().rearrange("p (b h c) -> p b h c", b=2, c=64)
                vmm = {i: mvb["pe"] + i + 1 for i in range(32)}
                vev = {i: mvb["dv"] + i + 1 for i in range(32)}

                def v_te(t, masm=masm, mvb=mvb, vmm=vmm, vev=vev):
                    t.wait_ge(ctx.dv.h, masm["dv"])
                    for tt in range(NTT):
                        for os in range(2):
                            i = tt * 2 + os
                            if i >= 2:
                                t.wait_ge(ctx.dv.h, vev[i - 2])
                            for dc in range(NDC):
                                mm = t.matmul(
                                    out=pv[:, (i % 2) * 512:(i % 2 + 1) * 512],
                                    lhsT=xT[:, dc * T + tt * 128:dc * T + (tt + 1) * 128],
                                    rhs=WTx[:, dc * D + os * 512:dc * D + (os + 1) * 512],
                                    start=(dc == 0), stop=(dc == NDC - 1))
                            got = ctx.pe.inc(mm)
                            assert got == vmm[i]
                ctx.sec("tensor", v_te)

                def v_ve(v, vmm=vmm, vev=vev):
                    for tt in range(NTT):
                        for os in range(2):
                            i = tt * 2 + os
                            v.wait_ge(ctx.pe.h, vmm[i])
                            got = ctx.dv.inc(v.tensor_tensor(
                                out=Vr[:, tt, os * 8:(os + 1) * 8, 0:64],
                                in0=pv_r[:, i % 2, :, :],
                                in1=vb_r[:, os, :, :], op=ALU.add))
                            assert got == vev[i]
                ctx.sec("vector", v_ve)

        if "qT" in taps:
            _tap(ctx, tap_d["qT"], qT[:])
        if "kT" in taps:
            _tap(ctx, tap_d["kT"], kT[:])
        if "Vaug" in taps:
            _tap(ctx, tap_d["Vaug"], Vaug[:])
        if "WTo" in taps:
            _tap(ctx, tap_d["WTo"], WT_o[:])


def build_attn(ctx, E, stT, n_heads, wq_thunks=()):
    taps, tap_d = E["taps"], E["tap_d"]
    qT, kT, Vaug, attnoutT = E["qT"], E["kT"], E["Vaug"], E["attnoutT"]
    cb128, rscr_d = E["cb128"], E["rscr_d"]

    NSLOT = 8
    PT = sb(ctx, "PT", [128, NSLOT * T], BF16, stT)
    rrow = sb(ctx, "rrow", [128, 2 * 512], F32, stT)
    rbc = sb(ctx, "rbc", [128, 2 * 512], F32, stT)
    pS0 = psum(ctx, "pS0", [128, 1024], F32, stT)
    pS1 = psum(ctx, "pS1", [128, 1024], F32, stT)
    pS = (pS0, pS1)
    pPV = psum(ctx, "pPV", [128, 4 * 512], F32, stT)

    m0 = ctx.marks()
    Vr = Vaug.ap().rearrange("p (tt hc) -> p tt hc", tt=NTT)
    NH2 = n_heads * 2

    smm, expm, pvm = {}, {}, {}
    rrm, evm = {}, {}
    rs_store, rs_read = {}, {}
    pe_n, sc_n, dv_n, rs_n = ctx.pe.n, ctx.sc.n, ctx.dv.n, ctx.rs.n
    rl_n = {0: ctx.rl0.n, 1: ctx.rl1.n}
    for gh in range(NH2 + 1):
        for kc8 in range(8):
            if gh >= 1:
                pe_n += 1
                pvm[(gh - 1, kc8)] = pe_n
            if gh < NH2:
                for qh in range(2):
                    pe_n += 1
                    smm[(gh, kc8, qh)] = pe_n
                    sc_n += 1
                    expm[(gh, kc8, qh)] = sc_n
        if gh < NH2:
            expm[gh] = sc_n
        if gh >= 1:
            pvm[gh - 1] = pvm[(gh - 1, 7)]
        if gh >= 2 and gh % 2 == 0:
            h = (gh - 2) // 2
            for qb in range(4):
                dv_n += 1
                rrm[(h, qb)] = dv_n
                rs_n += 16
                rs_store[(h, qb)] = rs_n
                rl_n[qb % 2] += 16
                rs_read[(h, qb)] = rl_n[qb % 2]
            for qb in range(4):
                dv_n += 1
                evm[(h, qb)] = dv_n
            evm[h] = dv_n
    def emit_half_pe(gh):
        def f(t, gh=gh):
            if gh < NH2:
                h, ha = gh // 2, gh % 2
                p0 = 64 * (h % 2)
                bq = (h // 2) * T
            if gh >= 1:
                hp, hap = (gh - 1) // 2, (gh - 1) % 2
                oddp = hp % 2
            for kc8 in range(8):
                if gh >= 1:
                    kcp = hap * 8 + kc8
                    t.wait_ge(ctx.sc.h, expm[(gh - 1, kc8, 1)])
                    if kcp == 0 and hp > 0:
                        t.wait_ge(ctx.dv.h, evm[hp - 1])
                    for qb in range(4):
                        rhsPT = PT[:, kc8 * T + qb * 512:kc8 * T + (qb + 1) * 512]
                        if not oddp:
                            mm = t.matmul(
                                out=pPV[0:65, qb * 512:(qb + 1) * 512],
                                lhsT=Vr[:, kcp, hp * 65:(hp + 1) * 65],
                                rhs=rhsPT, start=(kcp == 0), stop=(kcp == 15))
                        else:
                            t.matmul(
                                out=pPV[64:128, qb * 512:(qb + 1) * 512],
                                lhsT=Vr[:, kcp, hp * 65:hp * 65 + 64],
                                rhs=rhsPT, start=(kcp == 0), stop=(kcp == 15),
                                tile_position=(0, 64))
                            mm = t.matmul(
                                out=pPV[0:1, qb * 512:(qb + 1) * 512],
                                lhsT=Vr[:, kcp, hp * 65 + 64:hp * 65 + 65],
                                rhs=rhsPT, start=(kcp == 0), stop=(kcp == 15),
                                tile_position=(0, 0))
                    got = ctx.pe.inc(mm)
                    assert got == pvm[(gh - 1, kc8)]
                if gh < NH2:
                    kc = ha * 8 + kc8
                    for qh in range(2):
                        gp_ = gh * 16 + kc8 * 2 + qh
                        if gp_ >= 2:
                            pg = gp_ - 2
                            t.wait_ge(ctx.sc.h, expm[(pg // 16, (pg % 16) // 2, pg % 2)])
                        else:
                            ctx.wait_marks(t, m0, ("dv", "sc"))
                        buf = pS[gp_ % 2]
                        for qs in (2 * qh, 2 * qh + 1):
                            mm = t.matmul(
                                out=buf[:, (qs % 2) * 512:(qs % 2 + 1) * 512],
                                lhsT=kT[p0:p0 + 64, bq + kc * 128:bq + (kc + 1) * 128],
                                rhs=qT[p0:p0 + 64, bq + qs * 512:bq + (qs + 1) * 512],
                                start=True, stop=True)
                        got = ctx.pe.inc(mm)
                        assert got == smm[(gh, kc8, qh)]
        ctx.sec("tensor", f)

    def emit_all_exps():
        def f(s):
            for gh in range(NH2):
                for kc8 in range(8):
                    gkc = gh * 8 + kc8
                    slot = gkc % NSLOT
                    for qh in range(2):
                        gp_ = gh * 16 + kc8 * 2 + qh
                        s.wait_ge(ctx.pe.h, smm[(gh, kc8, qh)])
                        if gh >= 1 and qh == 0:
                            s.wait_ge(ctx.pe.h, pvm[(gh - 1, kc8)])
                        a = s.activation(
                            out=PT[:, slot * T + qh * 1024:slot * T + (qh + 1) * 1024],
                            in_=pS[gp_ % 2][:], func=AF.Exp, scale=0.125,
                            bias=cb128[:])
                        got = ctx.sc.inc(a)
                        assert got == expm[(gh, kc8, qh)]
        ctx.sec("scalar", f)

    rr_free = [0]
    rb_free = {0: 0, 1: 0}

    def emit_tail(h):
        odd = h % 2
        p0 = 64 * odd
        srow = 0 if odd else 64
        dcq = h // 2

        def fv1(v, h=h, srow=srow):
            v.wait_ge(ctx.pe.h, pvm[2 * h + 1])
            for qb in range(4):
                if rr_free[0]:
                    v.wait_ge(ctx.rs.h, rr_free[0])
                r = v.reciprocal(out=rrow[srow:srow + 1, (qb % 2) * 512:(qb % 2 + 1) * 512],
                                 in_=pPV[srow:srow + 1, qb * 512:(qb + 1) * 512])
                r.then_inc(ctx.dv.h, 1)
                ctx._note_sync(r, ctx.dv.h, rrm[(h, qb)])
                rr_free[0] = rs_store[(h, qb)]
            ctx.dv.n = max(ctx.dv.n, rrm[(h, 3)])
        ctx.sec("vector", fv1)

        def fs1(sy, h=h, srow=srow, p0=p0):
            for qb in range(4):
                rl = (ctx.rl0, ctx.rl1)[qb % 2]
                sy.wait_ge(ctx.dv.h, rrm[(h, qb)])
                sy.dma_start(out=rscr_d[h * 4 + qb:h * 4 + qb + 1, :],
                             in_=rrow[srow:srow + 1, (qb % 2) * 512:(qb % 2 + 1) * 512]
                             ).then_inc(ctx.rs.h, 16)
                sy.wait_ge(ctx.rs.h, rs_store[(h, qb)])
                if rb_free[qb % 2]:
                    sy.wait_ge(ctx.dv.h, rb_free[qb % 2])
                bsrc = rscr_d[h * 4 + qb:h * 4 + qb + 1, :]
                _, bap = bass.broadcast_tensor_aps(rbc[p0:p0 + 64, (qb % 2) * 512:(qb % 2 + 1) * 512], bsrc)
                sy.dma_start(out=rbc[p0:p0 + 64, (qb % 2) * 512:(qb % 2 + 1) * 512],
                             in_=bap).then_inc(rl.h, 16)
                rb_free[qb % 2] = evm[(h, qb)]
        ctx.sec("sync", fs1)
        ctx.rs.n = rs_store[(h, 3)]
        ctx.rl0.n = rs_read[(h, 2)]
        ctx.rl1.n = rs_read[(h, 3)]

        def fv2(v, h=h, p0=p0, dcq=dcq):
            for qb in range(4):
                rl = (ctx.rl0, ctx.rl1)[qb % 2]
                v.wait_ge(rl.h, rs_read[(h, qb)])
                ev = v.tensor_tensor(
                    out=attnoutT[p0:p0 + 64, dcq * T + qb * 512:dcq * T + (qb + 1) * 512],
                    in0=pPV[p0:p0 + 64, qb * 512:(qb + 1) * 512],
                    in1=rbc[p0:p0 + 64, (qb % 2) * 512:(qb % 2 + 1) * 512],
                    op=ALU.mult)
                ev.then_inc(ctx.dv.h, 1)
                ctx._note_sync(ev, ctx.dv.h, evm[(h, qb)])
            ctx.dv.n = max(ctx.dv.n, evm[(h, 3)])
        ctx.sec("vector", fv2)

    wq_i = [0]
    wq_thunks = list(wq_thunks)

    def emit_wq_batch(frac_done):
        want = int(len(wq_thunks) * frac_done)
        while wq_i[0] < want:
            e, f = wq_thunks[wq_i[0]]
            ctx.sec(e, f)
            wq_i[0] += 1

    emit_all_exps()
    for gh in range(NH2 + 1):
        emit_half_pe(gh)
        if gh >= 2 and gh % 2 == 0:
            emit_tail((gh - 2) // 2)
        emit_wq_batch((gh + 1) / (NH2 + 1))
    emit_wq_batch(1.0)
    assert ctx.pe.n == pe_n and ctx.sc.n == sc_n, (ctx.pe.n, pe_n, ctx.sc.n, sc_n)
    assert ctx.dv.n == dv_n and ctx.rs.n == rs_n, (ctx.dv.n, dv_n, ctx.rs.n, rs_n)

    if "PT0" in taps:
        _tap(ctx, tap_d["PT0"], PT[:, 0:8 * T])
    if "attnoutT" in taps:
        _tap(ctx, tap_d["attnoutT"], attnoutT[:])


def build_ffn(ctx, E, stF):
    taps, tap_d = E["taps"], E["tap_d"]
    attnoutT, WT_o = E["attnoutT"], E["WT_o"]
    fbrows, ones128, epsln = E["fbrows"], E["ones128"], E["epsln"]
    identb = E["identb"]
    wsc128 = E["wsc128"]
    tg_d, tu_d, td_d, h_d, x1_d, y_d = (E["tg_d"], E["tu_d"], E["td_d"],
                                        E["h_d"], E["x1_d"], E["y_d"])
    x_d = E["x_d"]
    tern_st = E["tern_st_mark"]

    yt = sb(ctx, "yt", [128, 2 * D], F32, stF)
    x1r = sb(ctx, "x1r", [128, 2 * D], F32, stF)
    bObc = sb(ctx, "bObc", [128, D], F32, stF)
    xr = sb(ctx, "xr", [128, 2 * D], F32, stF)
    r1 = sb(ctx, "r1", [128, 2 * D], F32, stF)
    tmp = sb(ctx, "tmp", [128, D], F32, stF)
    lnt = sb(ctx, "lnt", [128, 8], F32, stF)
    vars_ = sb(ctx, "vars", [128, NTT], F32, stF)
    rstd_a = sb(ctx, "rstd_a", [128, NTT], F32, stF)
    mdq = sb(ctx, "mdq", [128, NTT], F32, stF)
    dqg = sb(ctx, "dqg", [128, NTT], F32, stF)
    dqu = sb(ctx, "dqu", [128, NTT], F32, stF)
    dqd2 = sb(ctx, "dqd2", [128, NTT], F32, stF)
    xqb = sb(ctx, "xqb", [128, 2 * D], BF16, stF)
    xqTg = sb(ctx, "xqTg", [128, NDC * 512], BF16, stF)
    gss = sb(ctx, "gss", [128, 512], F32, stF)
    uss = sb(ctx, "uss", [128, 512], F32, stF)
    hst = sb(ctx, "hst", [128, 2 * 512], F32, stF)
    wgs = sb(ctx, "wgs", [128, 2 * NDC * 512], BF16, stF)
    wus = sb(ctx, "wus", [128, 2 * NDC * 512], BF16, stF)
    wds = sb(ctx, "wds", [128, 8 * D], BF16, stF)
    hrl = sb(ctx, "hrl", [128, INNER], F32, stF)
    hqb = sb(ctx, "hqb", [128, 2 * INNER], BF16, stF)
    hqT8 = sb(ctx, "hqT8", [128, 2 * 1024], BF16, stF)
    pO = psum(ctx, "pO", [128, 2 * 512], F32, stF)
    pGUD = psum(ctx, "pGUD", [128, 4 * 512], F32, stF)
    pXQ = psum(ctx, "pXQ", [128, 2 * 1024], BF16, stF)

    mo = ctx.marks()

    def ob_te(t, mo=mo):
        ctx.wait_marks(t, mo, ("dv",))
        for os in range(2):
            if os == 1:
                t.wait_ge(ctx.dv.h, mo["dv"] + 1)
            ctx.pe.inc(t.matmul(out=pO[:, 0:512], lhsT=ones128[0:1, :],
                                rhs=fbrows[:, D + os * 512:D + (os + 1) * 512],
                                start=True, stop=True))
    ctx.sec("tensor", ob_te)

    def ob_ve(v, mo=mo):
        for os in range(2):
            v.wait_ge(ctx.pe.h, mo["pe"] + os + 1)
            ctx.dv.inc(v.tensor_copy(out=bObc[:, os * 512:(os + 1) * 512],
                                     in_=pO[:, 0:512]))
    ctx.sec("vector", ob_ve)
    ctx.barrier(on=("pe", "dv"))

    xq_par = [0]
    pxq_free = {0: 0, 1: 0}
    prev = dict(dev=0, r1ev={0: 0, 1: 0}, xrfree={0: 0, 1: 0},
                tp_pe=0, mm_pe=0, x1st={0: 0, 1: 0}, y_st={0: 0, 1: 0})

    for g in range(4):
        tilesched = []
        for k4 in range(4):
            tt = g * 4 + k4
            s = {}
            la = (ctx.la0, ctx.la1)[k4 % 2]
            s["lda"] = la.n + 16
            la.n += 16
            tilesched.append(s)
        pe_s, dv_s, st_s = ctx.pe.n, ctx.dv.n, ctx.st.n

        for k4, off in enumerate((1, 2, 4, 6)):
            tilesched[k4]["omm"] = pe_s + off
        for k4, off in enumerate((3, 5, 7, 8)):
            tilesched[k4]["tp"] = pe_s + off
        for k4 in range(4):
            s = tilesched[k4]
            s["r1ev"] = ctx.dv.n + 1
            s["dev"] = ctx.dv.n + 2
            s["q"] = ctx.dv.n + 3
            s["xqev"] = ctx.dv.n + 4
            ctx.dv.n += 4
            s["par"] = xq_par[0] % 2
            xq_par[0] += 1
            sx = (ctx.sx0, ctx.sx1)[k4 % 2]
            s["x1st"] = sx.n + 16
            sx.n += 16
        ctx.dv.n = dv_s

        def p_ldr(sy, g=g, tilesched=tilesched, pxr=dict(prev["xrfree"]),
                  pdev=prev.get("lastdev", 0)):
            for k4 in range(4):
                tt = g * 4 + k4
                if k4 >= 2:
                    sy.wait_ge(ctx.dv.h, tilesched[k4 - 2]["dev"])
                elif pxr[k4 % 2]:
                    sy.wait_ge(ctx.dv.h, pxr[k4 % 2])
                sy.dma_start(out=xr[:, (k4 % 2) * D:(k4 % 2 + 1) * D],
                             in_=x_d[tt * 128:(tt + 1) * 128, :]
                             ).then_inc((ctx.la0, ctx.la1)[k4 % 2].h, 16)
        ctx.sec("sync", p_ldr)

        def p_te(t, g=g, tilesched=tilesched, prl=dict(prev["r1ev"]),
                 ppx=dict(pxq_free)):
            def omm(k4):
                tt = g * 4 + k4
                s = tilesched[k4]
                if k4 >= 1:
                    t.wait_ge(ctx.dv.h, tilesched[k4 - 1]["r1ev"])
                elif prl[0]:
                    t.wait_ge(ctx.dv.h, prl[0])
                for os in range(2):
                    for dc in range(NDC):
                        mm = t.matmul(
                            out=pO[:, os * 512:(os + 1) * 512],
                            lhsT=attnoutT[:, dc * T + tt * 128:dc * T + (tt + 1) * 128],
                            rhs=WT_o[:, dc * D + os * 512:dc * D + (os + 1) * 512],
                            start=(dc == 0), stop=(dc == NDC - 1))
                got = ctx.pe.inc(mm)
                assert got == s["omm"], (got, s["omm"], k4)

            def tp(k4):
                s = tilesched[k4]
                par = s["par"]
                t.wait_ge(ctx.dv.h, s["q"])
                if k4 >= 2:
                    t.wait_ge(ctx.dv.h, tilesched[k4 - 2]["xqev"])
                elif ppx[par]:
                    t.wait_ge(ctx.dv.h, ppx[par])
                    ppx[par] = 0
                for dc in range(NDC):
                    mm = t.transpose(
                        out=pXQ[:, par * 1024 + dc * 128:par * 1024 + (dc + 1) * 128],
                        in_=xqb[:, (k4 % 2) * D + dc * 128:(k4 % 2) * D + (dc + 1) * 128],
                        identity=identb[:])
                got = ctx.pe.inc(mm)
                assert got == s["tp"], (got, s["tp"], k4)

            omm(0); omm(1); tp(0); omm(2); tp(1); omm(3); tp(2); tp(3)
        ctx.sec("tensor", p_te)

        def p_ve(v, g=g, tilesched=tilesched, px1=dict(prev["x1st"])):
            for k4 in range(4):
                tt = g * 4 + k4
                s = tilesched[k4]
                v.wait_ge((ctx.la0, ctx.la1)[k4 % 2].h, s["lda"])
                v.tensor_add(out=tmp[:], in0=xr[:, (k4 % 2) * D:(k4 % 2 + 1) * D],
                             in1=bObc[:])
                if k4 >= 2:
                    v.wait_ge((ctx.sx0, ctx.sx1)[k4 % 2].h,
                              tilesched[k4 - 2]["x1st"])
                elif px1[k4 % 2]:
                    v.wait_ge((ctx.sx0, ctx.sx1)[k4 % 2].h, px1[k4 % 2])
                v.wait_ge(ctx.pe.h, s["omm"])
                got = ctx.dv.inc(v.tensor_add(
                    out=r1[:, (k4 % 2) * D:(k4 % 2 + 1) * D],
                    in0=pO[:, 0:1024], in1=tmp[:]))
                assert got == s["r1ev"]
                rs = r1[:, (k4 % 2) * D:(k4 % 2 + 1) * D]
                v.reduce_sum(out=lnt[:, 0:1], in_=rs, axis=AX.X)
                v.tensor_scalar_mul(out=lnt[:, 0:1], in0=lnt[:, 0:1], scalar1=1.0 / D)
                got = ctx.dv.inc(v.tensor_scalar_sub(out=rs, in0=rs,
                                                     scalar1=lnt[:, 0:1]))
                assert got == s["dev"]
                v.scalar_tensor_tensor(out=tmp[:], in0=rs, scalar=1.0, in1=rs,
                                       op0=ALU.mult, op1=ALU.mult,
                                       accum_out=lnt[:, 1:2])
                v.tensor_scalar_mul(out=vars_[:, tt:tt + 1], in0=lnt[:, 1:2],
                                    scalar1=1.0 / D)
                v.reduce_max(out=lnt[:, 2:3], in_=rs, axis=AX.X,
                             apply_absolute_value=True)
                v.tensor_scalar_max(out=lnt[:, 2:3], in0=lnt[:, 2:3], scalar1=1e-30)
                v.tensor_scalar_mul(out=mdq[:, tt:tt + 1], in0=lnt[:, 2:3],
                                    scalar1=1.0 / 127.0)
                v.reciprocal(out=lnt[:, 3:4], in_=lnt[:, 2:3])
                v.tensor_scalar_mul(out=lnt[:, 3:4], in0=lnt[:, 3:4], scalar1=127.0)
                v.tensor_scalar(out=tmp[:], in0=rs, scalar1=lnt[:, 3:4],
                                scalar2=MAGIC, op0=ALU.mult, op1=ALU.add)
                v.tensor_single_scalar(out=tmp[:], in_=tmp[:], scalar=MAGIC,
                                       op=ALU.subtract)
                v.tensor_copy(out=tmp.bitcast(I32)[:], in_=tmp.bitcast(I32)[:])
                got = ctx.dv.inc(v.tensor_copy(
                    out=xqb[:, (k4 % 2) * D:(k4 % 2 + 1) * D], in_=tmp[:]))
                assert got == s["q"]
                v.wait_ge(ctx.pe.h, s["tp"])
                xqr = xqTg.ap().rearrange("p (dc t) -> p dc t", dc=NDC)
                pxr_ = pXQ.ap().rearrange("p (b dc t) -> p b dc t", b=2, dc=NDC)
                got = ctx.dv.inc(v.tensor_copy(
                    out=xqr[:, :, k4 * 128:(k4 + 1) * 128], in_=pxr_[:, s["par"]]))
                assert got == s["xqev"]
        ctx.sec("vector", p_ve)

        def p_st(sy, g=g, tilesched=tilesched):
            for k4 in range(4):
                tt = g * 4 + k4
                s = tilesched[k4]
                sy.wait_ge(ctx.dv.h, s["q"])
                sy.dma_start(
                    out=x1_d[tt * 128:(tt + 1) * 128, :],
                    in_=r1[:, (k4 % 2) * D:(k4 % 2 + 1) * D]
                ).then_inc((ctx.sx0, ctx.sx1)[k4 % 2].h, 16)
        ctx.sec("sync", p_st)

        for k4 in range(4):
            pxq_free[tilesched[k4]["par"]] = tilesched[k4]["xqev"]
        prev["r1ev"][0] = tilesched[3]["r1ev"]
        prev["xrfree"][0] = tilesched[2]["dev"]
        prev["xrfree"][1] = tilesched[3]["dev"]
        prev["x1st"][0] = tilesched[2]["x1st"]
        prev["x1st"][1] = tilesched[3]["x1st"]

        msq = ctx.marks()
        sq_sc = {k4: msq["sc"] + k4 + 1 for k4 in range(4)}

        def p_sq(s, g=g, tilesched=tilesched):
            for k4 in range(4):
                tt = g * 4 + k4
                s.wait_ge(ctx.dv.h, tilesched[k4]["q"])
                a = s.activation(out=lnt[:, 4 + k4:5 + k4], in_=vars_[:, tt:tt + 1],
                                 func=AF.Sqrt, bias=epsln[:])
                inst = a
                inst.then_inc(ctx.sc.h, 1)
                ctx._note_sync(inst, ctx.sc.h, sq_sc[k4])
            ctx.sc.n = sq_sc[3]
        ctx.sec("scalar", p_sq)

        def p_dq(v, g=g, tilesched=tilesched):
            for k4 in range(4):
                tt = g * 4 + k4
                v.wait_ge(ctx.sc.h, sq_sc[k4])
                v.reciprocal(out=rstd_a[:, tt:tt + 1], in_=lnt[:, 4 + k4:5 + k4])
                v.tensor_mul(out=lnt[:, 0:1], in0=mdq[:, tt:tt + 1],
                             in1=rstd_a[:, tt:tt + 1])
                v.tensor_mul(out=dqg[:, tt:tt + 1], in0=lnt[:, 0:1], in1=wsc128[:, 0:1])
                ctx.dv.inc(v.tensor_mul(out=dqu[:, tt:tt + 1], in0=lnt[:, 0:1],
                                        in1=wsc128[:, 1:2]))
        ctx.sec("vector", p_dq)
        dq_done = ctx.dv.n

        mg = ctx.marks()
        gu_mm, gu_sc, gu_ev = {}, {}, {}
        pe_n, sc_n, dv_n = mg["pe"], mg["sc"], mg["dv"]
        for is_ in range(8):
            for k4 in range(4):
                pe_n += 2
                gu_mm[(is_, k4)] = pe_n
                sc_n += 1
                gu_sc[(is_, k4)] = sc_n
                dv_n += 2
                gu_ev[(is_, k4)] = dv_n
        ctx.sc.n, ctx.dv.n = sc_n, dv_n
        hst_mark = {}
        st_n = mg["st"]
        for i in range(32):
            st_n += 16
            hst_mark[i] = st_n
        ctx.st.n = st_n
        h_all_stored = st_n
        lw_base = mg["lw"]
        ctx.lw.n += 16 * 16

        def gu_sy(sy, g=g, mg=mg):
            for is_ in range(8):
                if is_ >= 2:
                    sy.wait_ge(ctx.pe.h, gu_mm[(is_ - 2, 3)])
                b = is_ % 2
                if g == 0 and is_ == 0:
                    sy.wait_ge(ctx.sg.h, tern_st[0])
                    sy.wait_ge(ctx.sg2.h, tern_st[1])
                sy.dma_start(
                    out=wgs[:, b * 4096:(b + 1) * 4096].rearrange(
                        "p (ic o) -> p ic o", ic=NDC),
                    in_=tg_d[:, is_ * 512:(is_ + 1) * 512].rearrange(
                        "(ic p) o -> p ic o", p=128)).then_inc(ctx.lw.h, 16)
                sy.dma_start(
                    out=wus[:, b * 4096:(b + 1) * 4096].rearrange(
                        "p (ic o) -> p ic o", ic=NDC),
                    in_=tu_d[:, is_ * 512:(is_ + 1) * 512].rearrange(
                        "(ic p) o -> p ic o", p=128)).then_inc(ctx.lw.h, 16)
                if is_ >= 1:
                    for k4 in range(4):
                        i = (is_ - 1) * 4 + k4
                        sy.wait_ge(ctx.dv.h, gu_ev[(is_ - 1, k4)])
                        sy.dma_start(
                            out=h_d[(g * 4 + k4) * 128:(g * 4 + k4 + 1) * 128,
                                    (is_ - 1) * 512:is_ * 512],
                            in_=hst[:, (i % 2) * 512:(i % 2 + 1) * 512]
                            ).then_inc(ctx.st.h, 16)
            for k4 in range(4):
                i = 7 * 4 + k4
                sy.wait_ge(ctx.dv.h, gu_ev[(7, k4)])
                sy.dma_start(
                    out=h_d[(g * 4 + k4) * 128:(g * 4 + k4 + 1) * 128, 7 * 512:8 * 512],
                    in_=hst[:, (i % 2) * 512:(i % 2 + 1) * 512]).then_inc(ctx.st.h, 16)
        ctx.sec("sync", gu_sy)

        def gu_te(t, g=g, mg=mg, lw_base=lw_base):
            for is_ in range(8):
                b = is_ % 2
                t.wait_ge(ctx.lw.h, lw_base + 32 * min(is_ + 2, 8))
                for k4 in range(4):
                    i = is_ * 4 + k4
                    if i >= 2:
                        pr2 = (is_, k4 - 2) if k4 >= 2 else (is_ - 1, k4 + 2)
                        t.wait_ge(ctx.dv.h, gu_ev[pr2])
                    else:
                        t.wait_ge(ctx.dv.h, mg["dv"])
                    sG = (i % 2) * 512
                    sU = (2 + i % 2) * 512
                    for ic in range(NDC):
                        t.matmul(out=pGUD[:, sG:sG + 512],
                                 lhsT=xqTg[:, ic * 512 + k4 * 128:ic * 512 + (k4 + 1) * 128],
                                 rhs=wgs[:, b * 4096 + ic * 512:b * 4096 + (ic + 1) * 512],
                                 start=(ic == 0), stop=(ic == NDC - 1))
                        mm = t.matmul(out=pGUD[:, sU:sU + 512],
                                      lhsT=xqTg[:, ic * 512 + k4 * 128:ic * 512 + (k4 + 1) * 128],
                                      rhs=wus[:, b * 4096 + ic * 512:b * 4096 + (ic + 1) * 512],
                                      start=(ic == 0), stop=(ic == NDC - 1))
                    got = ctx.pe.inc(mm, 2)
                    assert got == gu_mm[(is_, k4)]
        ctx.sec("tensor", gu_te)

        def gu_scf(s, g=g):
            s.wait_ge(ctx.dv.h, dq_done)
            for is_ in range(8):
                for k4 in range(4):
                    tt = g * 4 + k4
                    i = is_ * 4 + k4
                    s.wait_ge(ctx.pe.h, gu_mm[(is_, k4)])
                    if (is_, k4) != (0, 0):
                        prevk = (is_, k4 - 1) if k4 > 0 else (is_ - 1, 3)
                        s.wait_ge(ctx.dv.h, gu_ev[prevk])
                    a = s.activation(out=gss[:], in_=pGUD[:, (i % 2) * 512:(i % 2 + 1) * 512],
                                     func=AF.Silu, scale=dqg[:, tt:tt + 1])
                    inst = a
                    inst.then_inc(ctx.sc.h, 1)
                    ctx._note_sync(inst, ctx.sc.h, gu_sc[(is_, k4)])
        ctx.sec("scalar", gu_scf)

        def gu_ve(v, g=g):
            for is_ in range(8):
                for k4 in range(4):
                    tt = g * 4 + k4
                    i = is_ * 4 + k4
                    v.wait_ge(ctx.pe.h, gu_mm[(is_, k4)])
                    v.tensor_scalar_mul(out=uss[:], in0=pGUD[:, (2 + i % 2) * 512:(3 + i % 2) * 512],
                                        scalar1=dqu[:, tt:tt + 1])
                    v.wait_ge(ctx.sc.h, gu_sc[(is_, k4)])
                    if i >= 2:
                        v.wait_ge(ctx.st.h, hst_mark[i - 1])
                    ev = v.tensor_mul(out=hst[:, (i % 2) * 512:(i % 2 + 1) * 512],
                                      in0=gss[:], in1=uss[:])
                    ev.then_inc(ctx.dv.h, 2)
                    ctx._note_sync(ev, ctx.dv.h, gu_ev[(is_, k4)])
        ctx.sec("vector", gu_ve)

        for pr_ in range(2):
            tts = [g * 4 + pr_ * 2, g * 4 + pr_ * 2 + 1]
            hq_dv = {}
            for tk, tt in enumerate(tts):
                mh = ctx.marks()
                hld = mh["ld"] + 16
                ctx.ld.n += 16

                def h_ld(sy, tt=tt, mh=mh, pp=prev["tp_pe"]):
                    sy.wait_ge(ctx.dv.h, mh["dv"])
                    if pp:
                        sy.wait_ge(ctx.pe.h, pp)
                    sy.wait_ge(ctx.st.h, h_all_stored)
                    sy.dma_start(out=hrl[:], in_=h_d[tt * 128:(tt + 1) * 128, :]
                                 ).then_inc(ctx.ld.h, 16)
                ctx.sec("sync", h_ld)

                def h_q(v, tt=tt, tk=tk, hld=hld):
                    v.wait_ge(ctx.ld.h, hld)
                    v.reduce_max(out=lnt[:, 0:1], in_=hrl[:], axis=AX.X,
                                 apply_absolute_value=True)
                    v.tensor_scalar_max(out=lnt[:, 0:1], in0=lnt[:, 0:1], scalar1=1e-5)
                    v.tensor_scalar_mul(out=lnt[:, 1:2], in0=lnt[:, 0:1],
                                        scalar1=1.0 / 127.0)
                    v.tensor_mul(out=dqd2[:, tt:tt + 1], in0=lnt[:, 1:2],
                                 in1=wsc128[:, 2:3])
                    v.reciprocal(out=lnt[:, 2:3], in_=lnt[:, 0:1])
                    v.tensor_scalar_mul(out=lnt[:, 2:3], in0=lnt[:, 2:3], scalar1=127.0)
                    v.tensor_scalar(out=hrl[:], in0=hrl[:], scalar1=lnt[:, 2:3],
                                    scalar2=MAGIC, op0=ALU.mult, op1=ALU.add)
                    v.tensor_single_scalar(out=hrl[:], in_=hrl[:], scalar=MAGIC,
                                           op=ALU.subtract)
                    v.tensor_copy(out=hrl.bitcast(I32)[:], in_=hrl.bitcast(I32)[:])
                    hq_dv[tk] = ctx.dv.inc(v.tensor_copy(
                        out=hqb[:, tk * INNER:(tk + 1) * INNER], in_=hrl[:]))
                ctx.sec("vector", h_q)

            mp = ctx.marks()
            dn_lw = {icg: mp["lw"] + 16 * (icg + 1) for icg in range(4)}
            ctx.lw.n += 64
            dn_tp, dn_mm, dn_hq = {}, {}, {}
            pe_n = mp["pe"]
            dv_n = mp["dv"]
            for icg in range(4):
                for tk in range(2):
                    pe_n += 1
                    dn_tp[(icg, tk)] = pe_n
                    dv_n += 1
                    dn_hq[(icg, tk)] = dv_n
                    pe_n += 1
                    dn_mm[(icg, tk)] = pe_n
            ctx.dv.n = dv_n

            def dn_sy(sy, mp=mp, wds_free=prev["mm_pe"]):
                for icg in range(4):
                    if icg >= 1:
                        sy.wait_ge(ctx.pe.h, dn_mm[(icg - 1, 1)])
                    else:
                        sy.wait_ge(ctx.st.h, h_all_stored)
                        if wds_free:
                            sy.wait_ge(ctx.pe.h, wds_free)
                    sy.dma_start(
                        out=wds.ap().rearrange("p (ic o) -> p ic o", ic=8),
                        in_=td_d[icg * 1024:(icg + 1) * 1024, :].rearrange(
                            "(ic p) o -> p ic o", p=128)).then_inc(ctx.lw.h, 16)
            ctx.sec("sync", dn_sy)

            def dn_te(t, mp=mp, ppx=dict(pxq_free)):
                for icg in range(4):
                    t.wait_ge(ctx.lw.h, dn_lw[icg])
                    for tk in range(2):
                        if (icg, tk) == (0, 0):
                            t.wait_ge(ctx.dv.h, hq_dv[1])
                        j = icg * 2 + tk
                        parj = j % 2
                        if j >= 2:
                            pj = j - 2
                            t.wait_ge(ctx.dv.h, dn_hq[(pj // 2, pj % 2)])
                        elif ppx[parj]:
                            t.wait_ge(ctx.dv.h, ppx[parj])
                            ppx[parj] = 0
                        for i8 in range(8):
                            ic = icg * 8 + i8
                            mm = t.transpose(
                                out=pXQ[:, parj * 1024 + i8 * 128:parj * 1024 + (i8 + 1) * 128],
                                in_=hqb[:, tk * INNER + ic * 128:tk * INNER + (ic + 1) * 128],
                                identity=identb[:])
                        got = ctx.pe.inc(mm)
                        assert got == dn_tp[(icg, tk)]
                        t.wait_ge(ctx.dv.h, dn_hq[(icg, tk)])
                        for i8 in range(8):
                            for os in range(2):
                                mm = t.matmul(
                                    out=pGUD[:, (tk * 2 + os) * 512:(tk * 2 + os + 1) * 512],
                                    lhsT=hqT8[:, tk * 1024 + i8 * 128:tk * 1024 + (i8 + 1) * 128],
                                    rhs=wds[:, i8 * D + os * 512:i8 * D + (os + 1) * 512],
                                    start=(icg == 0 and i8 == 0),
                                    stop=(icg == 3 and i8 == 7))
                        got = ctx.pe.inc(mm)
                        assert got == dn_mm[(icg, tk)]
            ctx.sec("tensor", dn_te)

            def dn_ve(v, mp=mp):
                for icg in range(4):
                    for tk in range(2):
                        v.wait_ge(ctx.pe.h, dn_tp[(icg, tk)])
                        parj = (icg * 2 + tk) % 2
                        ev = v.tensor_copy(out=hqT8[:, tk * 1024:(tk + 1) * 1024],
                                           in_=pXQ[:, parj * 1024:(parj + 1) * 1024])
                        ev.then_inc(ctx.dv.h, 1)
                        ctx._note_sync(ev, ctx.dv.h, dn_hq[(icg, tk)])
            ctx.sec("vector", dn_ve)
            prev["tp_pe"] = dn_tp[(3, 1)]
            prev["mm_pe"] = dn_mm[(3, 1)]
            pxq_free[0] = dn_hq[(3, 0)]
            pxq_free[1] = dn_hq[(3, 1)]

            me = ctx.marks()
            x1_ld = {tk: me[("la0", "la1")[tk]] + 16 for tk in range(2)}
            ctx.la0.n += 16
            ctx.la1.n += 16
            ev_dv = {tk: me["dv"] + 3 * tk + 3 for tk in range(2)}
            ctx.dv.n += 6
            sq2 = {tk: me["sc"] + tk + 1 for tk in range(2)}
            ctx.sc.n += 2
            y_dv = {tk: ctx.dv.n + tk + 1 for tk in range(2)}
            ctx.dv.n += 2
            y_st = {tk: ctx.sy.n + 16 * (tk + 1) for tk in range(2)}
            ctx.sy.n += 32

            def e_ld(sy, tts=tts, me=me):
                sy.wait_ge(ctx.sx0.h, me["sx0"])
                sy.wait_ge(ctx.sx1.h, me["sx1"])
                for tk, tt in enumerate(tts):
                    sy.wait_ge(ctx.dv.h, me["dv"])
                    sy.dma_start(out=x1r[:, tk * D:(tk + 1) * D],
                                 in_=x1_d[tt * 128:(tt + 1) * 128, :]
                                 ).then_inc((ctx.la0, ctx.la1)[tk].h, 16)
            ctx.sec("sync", e_ld)

            def e_ev(v, tts=tts, me=me, pr_=pr_):
                for tk, tt in enumerate(tts):
                    v.wait_ge((ctx.la0, ctx.la1)[tk].h, x1_ld[tk])
                    v.wait_ge(ctx.pe.h, dn_mm[(3, 1)])
                    v.tensor_scalar_mul(out=x1r[:, tk * D:(tk + 1) * D],
                                        in0=x1r[:, tk * D:(tk + 1) * D],
                                        scalar1=rstd_a[:, tt:tt + 1])
                    rs2 = r1[:, tk * D:(tk + 1) * D]
                    for os in range(2):
                        v.scalar_tensor_tensor(
                            out=r1[:, tk * D + os * 512:tk * D + (os + 1) * 512],
                            in0=pGUD[:, (tk * 2 + os) * 512:(tk * 2 + os + 1) * 512],
                            scalar=dqd2[:, tt:tt + 1],
                            in1=x1r[:, tk * D + os * 512:tk * D + (os + 1) * 512],
                            op0=ALU.mult, op1=ALU.add)
                    v.reduce_sum(out=lnt[:, 0:1], in_=rs2, axis=AX.X)
                    v.tensor_scalar_mul(out=lnt[:, 0:1], in0=lnt[:, 0:1], scalar1=1.0 / D)
                    i1 = v.tensor_scalar_sub(out=rs2, in0=rs2, scalar1=lnt[:, 0:1])
                    i1.then_inc(ctx.dv.h, 1)
                    ctx._note_sync(i1, ctx.dv.h, me["dv"] + 3 * tk + 1)
                    v.scalar_tensor_tensor(out=tmp[:], in0=rs2, scalar=1.0, in1=rs2,
                                           op0=ALU.mult, op1=ALU.mult,
                                           accum_out=lnt[:, 1:2])
                    i2 = v.tensor_scalar_mul(out=lnt[:, 4 + tk:5 + tk],
                                             in0=lnt[:, 1:2], scalar1=1.0 / D)
                    i2.then_inc(ctx.dv.h, 2)
                    ctx._note_sync(i2, ctx.dv.h, ev_dv[tk])
            ctx.sec("vector", e_ev)

            def e_sq(s, me=me):
                for tk in range(2):
                    s.wait_ge(ctx.dv.h, ev_dv[tk])
                    a = s.activation(out=lnt[:, 2 + tk:3 + tk], in_=lnt[:, 4 + tk:5 + tk],
                                     func=AF.Sqrt, bias=epsln[:])
                    a.then_inc(ctx.sc.h, 1)
                    ctx._note_sync(a, ctx.sc.h, sq2[tk])
            ctx.sec("scalar", e_sq)

            def e_out(v, tts=tts, pys=dict(prev["y_st"])):
                for tk, tt in enumerate(tts):
                    v.wait_ge(ctx.sc.h, sq2[tk])
                    if pys[tk]:
                        v.wait_ge(ctx.sy.h, pys[tk])
                    v.reciprocal(out=lnt[:, 0:1], in_=lnt[:, 2 + tk:3 + tk])
                    ev = v.tensor_scalar_mul(out=yt[:, tk * D:(tk + 1) * D],
                                             in0=r1[:, tk * D:(tk + 1) * D],
                                             scalar1=lnt[:, 0:1])
                    ev.then_inc(ctx.dv.h, 1)
                    ctx._note_sync(ev, ctx.dv.h, y_dv[tk])
            ctx.sec("vector", e_out)

            def e_st(sy, tts=tts):
                for tk, tt in enumerate(tts):
                    sy.wait_ge(ctx.dv.h, y_dv[tk])
                    sy.dma_start(out=y_d[tt * 128:(tt + 1) * 128, :],
                                 in_=yt[:, tk * D:(tk + 1) * D]).then_inc(ctx.sy.h, 16)
            ctx.sec("sync", e_st)
            prev["y_st"][0] = y_st[1]
            prev["y_st"][1] = y_st[1]
        prev["dev"] = tilesched[3]["dev"]

    if "x1" in taps:
        def fx(sy):
            m = ctx.marks()
            ctx.wait_marks(sy, m, ("pe", "dv", "sc", "gp", "ld", "lw",
                                   "la0", "la1", "sx0", "sx1", "sy"))
            sy.wait_ge(ctx.st.h, m["st"])
            ctx.st.inc(sy.dma_start(out=tap_d["x1"], in_=x1_d), 16)
        ctx.sec("sync", fx)
    if "hsc" in taps:
        def fh(sy):
            m = ctx.marks()
            ctx.wait_marks(sy, m, ("pe", "dv", "sc", "gp", "ld", "lw",
                                   "la0", "la1", "sx0", "sx1", "sy"))
            sy.wait_ge(ctx.st.h, m["st"])
            ctx.st.inc(sy.dma_start(out=tap_d["hsc"], in_=h_d), 16)
        ctx.sec("sync", fh)


def make_inputs(L0_attn_x, L0_attn_qW, L0_attn_qb, L0_attn_kW, L0_attn_kb,
                L0_attn_vW, L0_attn_vb, L0_attn_oW, L0_attn_ob, L0_attn_J,
                L0_ffn_w_gate, L0_ffn_w_up, L0_ffn_w_down):
    import ml_dtypes
    common = {
        "qW": np.ascontiguousarray(np.asarray(L0_attn_qW, dtype=np.float32)),
        "kW": np.ascontiguousarray(np.asarray(L0_attn_kW, dtype=np.float32)),
        "vW": np.ascontiguousarray(np.asarray(L0_attn_vW, dtype=np.float32)),
        "oW": np.ascontiguousarray(np.asarray(L0_attn_oW, dtype=np.float32)),
        "qb": np.ascontiguousarray(np.asarray(L0_attn_qb, dtype=np.float32)),
        "kb": np.ascontiguousarray(np.asarray(L0_attn_kb, dtype=np.float32)),
        "vb": np.ascontiguousarray(np.asarray(L0_attn_vb, dtype=np.float32)),
        "ob": np.ascontiguousarray(np.asarray(L0_attn_ob, dtype=np.float32)),
        "J": np.ascontiguousarray(np.asarray(L0_attn_J, dtype=np.float32)),
        "wgT": np.ascontiguousarray(np.asarray(L0_ffn_w_gate, dtype=np.float32).T),
        "wuT": np.ascontiguousarray(np.asarray(L0_ffn_w_up, dtype=np.float32).T),
        "wdT": np.ascontiguousarray(np.asarray(L0_ffn_w_down, dtype=np.float32).T),
        "idb": np.eye(128, dtype=ml_dtypes.bfloat16),
        "idf": np.eye(128, dtype=np.float32),
    }
    xs = np.asarray(L0_attn_x, dtype=np.float32)
    return [dict(common, x=np.ascontiguousarray(xs[i])) for i in range(xs.shape[0])]


def kernel(L0_attn_x, L0_attn_qW, L0_attn_qb, L0_attn_kW, L0_attn_kb,
           L0_attn_vW, L0_attn_vb, L0_attn_oW, L0_attn_ob, L0_attn_J,
           L0_ffn_w_gate, L0_ffn_w_up, L0_ffn_w_down):
    nc = build()
    in_maps = make_inputs(L0_attn_x, L0_attn_qW, L0_attn_qb, L0_attn_kW,
                          L0_attn_kb, L0_attn_vW, L0_attn_vb, L0_attn_oW,
                          L0_attn_ob, L0_attn_J, L0_ffn_w_gate, L0_ffn_w_up,
                          L0_ffn_w_down)
    res = run_bass_kernel_spmd(nc, in_maps, core_ids=list(range(len(in_maps))))
    return np.stack([r["y"] for r in res.results], axis=0)
